# version 11
# baseline (speedup 1.0000x reference)
import math
import numpy as np
import ml_dtypes
from contextlib import ExitStack

import concourse.bass as bass
import concourse.mybir as mybir
from concourse.bass_utils import run_bass_kernel_spmd
from concourse.bass_core_rust import IndirectOffsetOnAxis

F32 = mybir.dt.float32
BF16 = mybir.dt.bfloat16
AF = mybir.ActivationFunctionType
ALU = mybir.AluOpType
AX = mybir.AxisListType.X
BFNP = ml_dtypes.bfloat16

B, N, DM, H, HD, T = 2, 4096, 2048, 16, 128, 128
NCH = N // T
CPC = 8
TOK = CPC * T
DT = DM // 128
HID = 5461
E, RLO = 8, 16
EPS32 = float(np.finfo(np.float32).eps)
INV_SQRT_HD = 1.0 / math.sqrt(HD)
NCORES = 8


class SemC:
    def __init__(self, h):
        self.h = h
        self.n = 0

    def inc(self, k=1):
        self.n += k
        return self.n


class SEng:

    PASS = ("wait_ge",)

    def __init__(self, e, sem):
        self._e = e
        self._s = sem

    def __getattr__(self, name):
        tgt = getattr(self._e, name)
        if name in SEng.PASS or not callable(tgt):
            return tgt

        def call(*a, **k):
            if self._s.n:
                self._e.wait_ge(self._s.h, self._s.n)
            ins = tgt(*a, **k)
            ins.then_inc(self._s.h, 1)
            self._s.inc(1)
            return ins
        return call


class Stream:

    def __init__(self, P, name, nslots=2):
        self.sems = [SemC(P.ctx.enter_context(P.nc.semaphore(f"{name}{i}")))
                     for i in range(nslots)]
        self.k = nslots

    def issue(self, g, out, in_, i=0):
        s = self.sems[i % self.k]
        g.dma_start(out=out, in_=in_).then_inc(s.h, 16)
        return (s, s.inc(16))


class LB:
    def __init__(self, name):
        self.nc = bass.Bass("TRN2", target_bir_lowering=False)
        self.name = name
        self.ctx = ExitStack()
        self.sems = {}
        self.block = None

    def start(self):
        for s in ("pe", "ac", "ve", "gp", "cc"):
            self.sems[s] = SemC(self.ctx.enter_context(self.nc.semaphore(s)))
        self.block = self.ctx.enter_context(self.nc.Block())
        self.PSF = self.ctx.enter_context(self.nc.psum_tensor("PSF", [128, 4096], F32))
        self.streams = []
        self.tag = "x"
        self.default_sc = None
        for nm in ("mvec", "mld", "stq", "stx", "wl", "wlg", "sqd", "sol", "sxn",
                   "sgw", "srp", "sqo", "svo", "lq", "lk", "lv", "sy",
                   "syl", "sxl", "sxs", "sml", "srr"):
            setattr(self, nm, self.stream(nm))
        self.mld4 = self.stream("mld4", 8)
        return self

    def phase_barrier(self, include_cc=True):
        engs = ("pe", "ac", "ve", "gp", "cc") if include_cc else ("pe", "ac", "ve", "gp")
        vals = [(self.sems[s].h, self.sems[s].n)
                for s in engs if self.sems[s].n]
        for st in self.streams:
            for s in st.sems:
                if s.n:
                    vals.append((s.h, s.n))

        for eng in ("tensor", "scalar", "sync", "gpsimd", "vector"):
            def fgate(e, vals=vals):
                for h, v in vals:
                    e.wait_ge(h, v)
            getattr(self.block, eng)(fgate)

    def stream(self, name, k=2):
        st = Stream(self, name, k)
        self.streams.append(st)
        return st

    def wait_all_streams(self, e):
        for st in self.streams:
            for s in st.sems:
                if s.n:
                    e.wait_ge(s.h, s.n)

    def bank(self, i, w=1, rows=128):
        return self.PSF[0:rows, i * 512:(i + w) * 512]

    def bankc(self, i, cols, rows=128):
        return self.PSF[0:rows, i * 512:i * 512 + cols]

    def sem(self, name):
        return self.sems[name]

    def sbuf(self, name, shape, dtype, sc=None):
        return (sc or self.default_sc or self.ctx).enter_context(
            self.nc.sbuf_tensor(f"{self.tag}_{name}", shape, dtype))

    def dram(self, name, shape, dtype):
        return self.nc.dram_tensor(f"{self.tag}_{name}", shape, dtype, kind="Internal")

    def din(self, name, shape, dtype):
        return self.nc.dram_tensor(name, shape, dtype, kind="ExternalInput")

    def dout(self, name, shape, dtype):
        return self.nc.dram_tensor(name, shape, dtype, kind="ExternalOutput")

    def vbody(self, fn):
        ve = self.sem("ve")

        def wrap(v):
            fn(SEng(v, ve))
        self.block.vector(wrap)
        return ve.n

    def sbody(self, fn):
        ac = self.sem("ac")

        def wrap(s):
            fn(SEng(s, ac))
        self.block.scalar(wrap)
        return ac.n

    def gbody(self, fn):
        gp = self.sem("gp")

        def wrap(g):
            fn(g, SEng(g, gp))
        self.block.gpsimd(wrap)
        return gp.n

    def finish(self):
        self.ctx.close()
        return self.nc


def tile_list(rows, cols, tr=128, tc=1024):
    out = []
    r = 0
    while r < rows:
        pr = min(tr, rows - r)
        c = 0
        while c < cols:
            pc = min(tc, cols - c)
            out.append((r, pr, c, pc))
            c += pc
        r += pr
    return out


def emit_consts(P):
    C = {}
    C["ones16"] = P.sbuf("c_ones16", [128, 1], BF16)
    C["ones1r32"] = P.sbuf("c_ones1r32", [1, 128], F32)
    C["ones1r16"] = P.sbuf("c_ones1r16", [1, 128], BF16)
    C["onescol32"] = P.sbuf("c_onescol32", [128, 1], F32)
    C["iota_pf"] = P.sbuf("c_iotapf", [128, 128], F32)
    C["trilT"] = P.sbuf("c_trilT", [128, 128], F32)
    C["ident16"] = P.sbuf("c_ident16", [128, 128], BF16)
    gp, ve = P.sem("gp"), P.sem("ve")

    def gfn(g, G):
        G.memset(C["ones16"][:], 1.0)
        G.memset(C["ones1r32"][:], 1.0)
        G.memset(C["ones1r16"][:], 1.0)
        G.memset(C["onescol32"][:], 1.0)
        G.iota(C["iota_pf"][:], pattern=[[-1, 128]], base=0, channel_multiplier=1,
               allow_small_or_imprecise_dtypes=True)
    v_gp = P.gbody(gfn)

    def vfn(V):
        V.wait_ge(gp.h, v_gp)
        V.tensor_single_scalar(C["trilT"][:], C["iota_pf"][:], 0.0, ALU.is_le)
        V.tensor_single_scalar(C["ident16"][:], C["iota_pf"][:], 0.0, ALU.is_equal)
    v_ve = P.vbody(vfn)

    for eng in ("tensor", "scalar", "sync", "gpsimd", "vector"):
        def fgate(e, v_gp=v_gp, v_ve=v_ve):
            e.wait_ge(gp.h, v_gp)
            e.wait_ge(ve.h, v_ve)
        getattr(P.block, eng)(fgate)
    return C


_POOL_N = [0]


def alloc_pools(P, sc):
    k = _POOL_N[0]
    _POOL_N[0] += 1
    P.wb32 = [P.sbuf(f"wb32_{k}_{i}", [128, 1024], F32, sc) for i in range(2)]
    P.ba16 = [P.sbuf(f"ba16_{k}_{i}", [128, 1024], BF16, sc) for i in range(2)]
    P.bb16 = [P.sbuf(f"bb16_{k}_{i}", [128, 1024], BF16, sc) for i in range(2)]



def emit_mean_pass(P, C, sc, wsrc, rows, cols, clip_lo, tag, mq_store=None,
                   entry=()):
    pe, ve = P.sem("pe"), P.sem("ve")
    acc = P.sbuf(f"{tag}_acc", [128, 1], F32, sc)
    red = P.sbuf(f"{tag}_red", [128, 1], F32, sc)
    m_sb = P.sbuf(f"{tag}_m", [1, 1], F32, sc)
    th_pos = P.sbuf(f"{tag}_thp", [128, 1], F32, sc)
    th_neg = P.sbuf(f"{tag}_thn", [128, 1], F32, sc)
    tiles = tile_list(rows, cols)
    nelem = rows * cols

    def vfn0(V):
        for es, evv in entry:
            V.wait_ge(es.h, evv)
        V.memset(acc[:], 0.0)
    P.vbody(vfn0)

    ld_v, ve_done = [], []
    for i, (r, pr, c, pc) in enumerate(tiles):
        bi = i % 2

        @P.block.gpsimd
        def _(g):
            if i >= 2:
                g.wait_ge(ve.h, ve_done[i - 2])
            else:
                for es, evv in entry:
                    g.wait_ge(es.h, evv)
            ld_v.append(P.mld.issue(g, P.wb32[bi][0:pr, 0:pc],
                                    wsrc[r:r + pr, c:c + pc], i))

        def vfn(V, i=i, bi=bi, pr=pr, pc=pc):
            V.wait_ge(ld_v[i][0].h, ld_v[i][1])
            V.tensor_reduce(red[0:pr, 0:1], P.wb32[bi][0:pr, 0:pc], AX, ALU.add,
                            apply_absolute_value=True)
            V.tensor_add(acc[0:pr, 0:1], acc[0:pr, 0:1], red[0:pr, 0:1])
        ve_done.append(P.vbody(vfn))

    @P.block.tensor
    def _(t):
        t.wait_ge(ve.h, ve_done[-1])
        t.matmul(P.bankc(7, 1, rows=1), acc[:, 0:1], C["onescol32"][:, 0:1],
                 start=True, stop=True).then_inc(pe.h, 1)
    v_pe = pe.inc(1)

    def vfn2(V):
        V.wait_ge(pe.h, v_pe)
        if clip_lo is None:
            V.tensor_single_scalar(m_sb[:], P.bankc(7, 1, rows=1), 1.0 / nelem, ALU.mult)
        else:
            V.tensor_scalar(m_sb[:], P.bankc(7, 1, rows=1), 1.0 / nelem, clip_lo,
                            ALU.mult, ALU.max)
    v_m = P.vbody(vfn2)

    @P.block.tensor
    def _(t):
        t.wait_ge(ve.h, v_m)
        t.matmul(P.bankc(7, 1), C["ones1r32"][0:1, :], m_sb[0:1, 0:1],
                 start=True, stop=True).then_inc(pe.h, 1)
    v_pe2 = pe.inc(1)

    def vfn3(V):
        V.wait_ge(pe.h, v_pe2)
        V.tensor_single_scalar(th_pos[:], P.bankc(7, 1), 0.5, ALU.mult)
        V.tensor_single_scalar(th_neg[:], th_pos[:], -1.0, ALU.mult)
    done = P.vbody(vfn3)

    if mq_store is not None:
        @P.block.gpsimd
        def _(g):
            g.wait_ge(ve.h, done)
            P.mvec.issue(g, mq_store, m_sb[:])
    return m_sb, th_pos, th_neg, done


def emit_quantize_pass(P, wsrc, wdst, rows, cols, th_pos, th_neg, th_ready, tag):
    ac, ve = P.sem("ac"), P.sem("ve")
    tiles = tile_list(rows, cols)
    ld_v, ac_done, ve_done, st_v = [], [], [], []
    pre_stq = [(s, s.n) for s in P.stq.sems]
    for i, (r, pr, c, pc) in enumerate(tiles):
        bi = i % 2

        @P.block.gpsimd
        def _(g):
            if i >= 2:
                g.wait_ge(ac.h, ac_done[i - 2])
            else:
                g.wait_ge(ve.h, th_ready)
            ld_v.append(P.mld.issue(g, P.wb32[bi][0:pr, 0:pc],
                                    wsrc[r:r + pr, c:c + pc], i))

        def sfn(S_, i=i, bi=bi, pr=pr, pc=pc):
            S_.wait_ge(ld_v[i][0].h, ld_v[i][1])
            if i == 0:
                S_.wait_ge(ve.h, th_ready)
            if i < 2:
                for s_, vv in pre_stq:
                    if vv:
                        S_.wait_ge(s_.h, vv)
            if i >= 2:
                S_.wait_ge(st_v[i - 2][0].h, st_v[i - 2][1])
            S_.activation(P.ba16[bi][0:pr, 0:pc], P.wb32[bi][0:pr, 0:pc], AF.Sign,
                          bias=th_neg[0:pr, 0:1])
            S_.activation(P.bb16[bi][0:pr, 0:pc], P.wb32[bi][0:pr, 0:pc], AF.Sign,
                          bias=th_pos[0:pr, 0:1])
        ac_done.append(P.sbody(sfn))

        def vfn(V, i=i, bi=bi, pr=pr, pc=pc):
            V.wait_ge(ac.h, ac_done[i])
            if i >= 2:
                V.wait_ge(st_v[i - 2][0].h, st_v[i - 2][1])
            V.tensor_add(P.bb16[bi][0:pr, 0:pc], P.bb16[bi][0:pr, 0:pc],
                         P.ba16[bi][0:pr, 0:pc])
        ve_done.append(P.vbody(vfn))

        @P.block.gpsimd
        def _(g):
            g.wait_ge(ve.h, ve_done[i])
            st_v.append(P.stq.issue(g, wdst[r:r + pr, c:c + pc],
                                    P.bb16[bi][0:pr, 0:pc], i))
    return st_v, ve_done[-1]


def emit_ln_h(P, C, sc, xsrc, h16, nrm_sb, tag, xn_dram=None, entry=(),
              eps_ln=1e-5, eps_h=EPS32, with_h=True, xn_tiles=None):
    pe, ac, ve, gp = (P.sem(s) for s in ("pe", "ac", "ve", "gp"))
    ps_sum = P.PSF[0:1, 0:1024]
    ps_sq = P.PSF[0:1, 1024:2048]
    A = P.sbuf(f"{tag}_A", [1, 1024], F32, sc)
    Bv = P.sbuf(f"{tag}_B", [1, 1024], F32, sc)
    Cv = P.sbuf(f"{tag}_C", [1, 1024], F32, sc)
    Dv = P.sbuf(f"{tag}_D", [1, 1024], F32, sc)
    ep1 = P.sbuf(f"{tag}_e1", [1, 1], F32, sc)
    ep2 = P.sbuf(f"{tag}_e2", [1, 1], F32, sc)
    bm = P.sbuf(f"{tag}_bm", [128, 1024], F32, sc)
    br = P.sbuf(f"{tag}_br", [128, 1024], F32, sc)
    brh = P.sbuf(f"{tag}_brh", [128, 1024], F32, sc) if with_h else None

    def gfn(g, G):
        G.memset(ep1[:], eps_ln)
        G.memset(ep2[:], eps_h)
    v_eps = P.gbody(gfn)

    ld_v, ac_done, ve_done, pe_done = [], [], [], []
    for i in range(DT):
        bi = i % 2

        @P.block.gpsimd
        def _(g):
            if i >= 2:
                g.wait_ge(ac.h, ac_done[i - 2])
            else:
                for es, evv in entry:
                    g.wait_ge(es.h, evv)
            ld_v.append(P.mld.issue(g, P.wb32[bi][:],
                                    xsrc[i * 128:(i + 1) * 128, :], i))

        def vfn(V, i=i, bi=bi):
            V.wait_ge(ld_v[i][0].h, ld_v[i][1])
            if i >= 2:
                V.wait_ge(pe.h, pe_done[i - 2])
            V.tensor_copy(P.ba16[bi][:], P.wb32[bi][:])
        ve_done.append(P.vbody(vfn))

        def sfn(S_, i=i, bi=bi):
            S_.wait_ge(ve.h, ve_done[i])
            S_.activation(P.bb16[bi][:], P.wb32[bi][:], AF.Square)
        ac_done.append(P.sbody(sfn))

        @P.block.tensor
        def _(t):
            t.wait_ge(ac.h, ac_done[i])
            for off in (0, 512):
                t.matmul(ps_sum[0:1, off:off + 512], C["ones16"][:, 0:1],
                         P.ba16[bi][:, off:off + 512], start=(i == 0), stop=(i == DT - 1))
                mm = t.matmul(ps_sq[0:1, off:off + 512], C["ones16"][:, 0:1],
                              P.bb16[bi][:, off:off + 512], start=(i == 0),
                              stop=(i == DT - 1))
            mm.then_inc(pe.h, 1)
        pe_done.append(pe.inc(1))

    def vfn1(V):
        V.wait_ge(pe.h, pe_done[-1])
        V.tensor_single_scalar(A[:], ps_sum, 1.0 / DM, ALU.mult)
        V.tensor_single_scalar(Bv[:], ps_sq, 1.0 / DM, ALU.mult)
    v1 = P.vbody(vfn1)

    def sfn1(S_):
        S_.wait_ge(ve.h, v1)
        S_.activation(Dv[:], A[:], AF.Square)
    v2 = P.sbody(sfn1)

    def vfn2(V):
        V.wait_ge(ac.h, v2)
        V.tensor_sub(Bv[:], Bv[:], Dv[:])
    v3 = P.vbody(vfn2)

    def sfn2(S_):
        S_.wait_ge(ve.h, v3)
        S_.wait_ge(gp.h, v_eps)
        S_.activation(Cv[:], Bv[:], AF.Ln, bias=ep1[0:1, 0:1])
        if with_h:
            S_.activation(Dv[:], Cv[:], AF.Exp, scale=-1.0)
        S_.activation(Cv[:], Cv[:], AF.Exp, scale=-0.5)
    v4 = P.sbody(sfn2)

    def vfn3(V):
        V.wait_ge(ac.h, v4)
        if with_h:
            V.tensor_mul(Bv[:], Bv[:], Dv[:])
        V.tensor_mul(A[:], A[:], Cv[:])
    v5 = P.vbody(vfn3)

    if with_h:
        def sfn3(S_):
            S_.wait_ge(ve.h, v5)
            S_.activation(Bv[:], Bv[:], AF.Ln, bias=ep2[0:1, 0:1])
            S_.activation(Bv[:], Bv[:], AF.Exp, scale=-0.5)
        v6 = P.sbody(sfn3)
    else:
        v6 = None

    @P.block.tensor
    def _(t):
        t.wait_ge(ve.h, v5)
        if v6 is not None:
            t.wait_ge(ac.h, v6)
        for off in (0, 512):
            t.matmul(P.PSF[:, 2048 + off:2048 + off + 512], C["ones1r32"][0:1, :],
                     A[0:1, off:off + 512], start=True, stop=True)
            mm = t.matmul(P.PSF[:, 3072 + off:3072 + off + 512], C["ones1r32"][0:1, :],
                          Cv[0:1, off:off + 512], start=True, stop=True)
            if with_h:
                mm = t.matmul(P.PSF[:, 1024 + off:1024 + off + 512],
                              C["ones1r32"][0:1, :],
                              Bv[0:1, off:off + 512], start=True, stop=True)
        mm.then_inc(pe.h, 1)
    v7 = pe.inc(1)

    def vfn4(V):
        V.wait_ge(pe.h, v7)
        V.tensor_copy(bm[:], P.PSF[:, 2048:3072])
        V.tensor_copy(br[:], P.PSF[:, 3072:4096])
        if with_h:
            V.tensor_copy(brh[:], P.PSF[:, 1024:2048])
    v8 = P.vbody(vfn4)

    ld2, xn_ready, h_ready, xn_st = [], [], [], []
    for i in range(DT):
        bi = i % 2

        @P.block.gpsimd
        def _(g):
            if i < 2:
                g.wait_ge(ve.h, v8)
            else:
                g.wait_ge(ve.h, (h_ready if with_h else xn_ready)[i - 2])
                if xn_st:
                    g.wait_ge(xn_st[i - 2][0].h, xn_st[i - 2][1])
            ld2.append(P.mld.issue(g, P.wb32[bi][:],
                                   xsrc[i * 128:(i + 1) * 128, :], i))

        dst_t = xn_tiles[i] if xn_tiles is not None else P.ba16[bi]

        def vfn5(V, i=i, bi=bi, dst_t=dst_t):
            V.wait_ge(ld2[i][0].h, ld2[i][1])
            V.tensor_mul(P.wb32[bi][:], P.wb32[bi][:], br[:])
            V.tensor_sub(dst_t[:], P.wb32[bi][:], bm[:])
        xn_ready.append(P.vbody(vfn5))

        if xn_dram is not None:
            @P.block.gpsimd
            def _(g):
                g.wait_ge(ve.h, xn_ready[i])
                xn_st.append(P.stx.issue(g, xn_dram[i * 128:(i + 1) * 128, :],
                                         dst_t[:], i))

        if with_h:
            def vfn6(V, i=i, dst_t=dst_t):
                V.scalar_tensor_tensor(h16[i][:], dst_t[:], nrm_sb[:, i:i + 1],
                                       brh[:], ALU.mult, ALU.mult)
            h_ready.append(P.vbody(vfn6))
    return xn_ready, h_ready, xn_st


WPREP = [
    ("qkv", 256, 6144, 6144 * 2048, 1e-5),
    ("prj", 256, 2048, 2048 * 2048, 1e-5),
    ("w1", 256, HID, 2048 * HID, None),
    ("w3", 256, HID, 2048 * HID, None),
    ("w2", 683, 2048, HID * 2048, None),
]


def alloc_wprep(P, sc):
    w = {}
    w["acc"] = P.sbuf("w_acc", [128, 1], F32, sc)
    w["red"] = P.sbuf("w_red", [128, 1], F32, sc)
    w["prow"] = P.sbuf("w_prow", [1, 8], F32, sc)
    w["mrow_sb"] = P.sbuf("w_mrow", [1, 8], F32, sc)
    w["throw"] = P.sbuf("w_throw", [1, 8], F32, sc)
    w["th_bc"] = P.sbuf("w_thbc", [128, 16], F32, sc)
    w["wb"] = [P.sbuf(f"w_wb{i}", [128, 1024], F32, sc) for i in range(2)]
    w["q16"] = [P.sbuf(f"w_q16{i}", [128, 1024], BF16, sc) for i in range(2)]
    w["gtt"] = P.sbuf("w_gtt", [128, 1024], BF16, sc)
    w["ltt"] = P.sbuf("w_ltt", [128, 1024], BF16, sc)
    w["prev_st"] = []
    w["pr_d"] = P.dram("w_pr_d", [1, 8], F32).ap()
    w["sums_d"] = P.dram("w_sums_d", [1, 8], F32).ap()
    w["prev_q_ve"] = 0
    return w


def emit_wprep_means(P, C, io, w):
    pe, ve, cc = P.sem("pe"), P.sem("ve"), P.sem("cc")
    GROUPS8 = [list(range(8))]
    acc, red, prow = w["acc"], w["red"], w["prow"]
    mrow_sb, throw, th_bc = w["mrow_sb"], w["throw"], w["th_bc"]
    wb = w["wb"]
    nb = len(wb)
    pr_d, sums_d = w["pr_d"], w["sums_d"]

    part_ve = []

    def vfn_z(V):
        V.memset(prow[:], 0.0)
        V.memset(mrow_sb[:], 0.0)
        V.memset(throw[:], 0.0)
    P.vbody(vfn_z)
    gi = [0]
    ve_done = {}
    for wi, (nm, R, Cc, nelem, clip) in enumerate(WPREP):
        wsl = io[nm + "_sl"]
        tiles = tile_list(R, Cc)

        def vfn0(V, wi=wi):
            V.memset(acc[:], 0.0)
        P.vbody(vfn0)
        for ti, (r, pr_, c, pc) in enumerate(tiles):
            i = gi[0]
            bi = i % nb

            @P.block.gpsimd
            def _(g):
                if i >= nb:
                    g.wait_ge(ve.h, ve_done[i - nb])
                w.setdefault("ld_v", {})[i] = P.mld4.issue(
                    g, wb[bi][0:pr_, 0:pc], wsl[r:r + pr_, c:c + pc], i % nb)

            def vfn(V, i=i, bi=bi, pr_=pr_, pc=pc):
                lv = w["ld_v"][i]
                V.wait_ge(lv[0].h, lv[1])
                V.tensor_reduce(red[0:pr_, 0:1], wb[bi][0:pr_, 0:pc], AX, ALU.add,
                                apply_absolute_value=True)
                V.tensor_add(acc[0:pr_, 0:1], acc[0:pr_, 0:1], red[0:pr_, 0:1])
            ve_done[i] = P.vbody(vfn)
            gi[0] += 1

        @P.block.tensor
        def _(t):
            t.wait_ge(ve.h, ve_done[gi[0] - 1])
            t.matmul(P.PSF[0:1, 7 * 512 + wi:7 * 512 + wi + 1], acc[:, 0:1],
                     C["onescol32"][:, 0:1], start=True, stop=True).then_inc(pe.h, 1)
        v_pe = pe.inc(1)

        def vfn2(V, wi=wi, v_pe=v_pe):
            V.wait_ge(pe.h, v_pe)
            V.tensor_copy(prow[0:1, wi:wi + 1], P.PSF[0:1, 7 * 512 + wi:7 * 512 + wi + 1])
        part_ve.append(P.vbody(vfn2))

    @P.block.gpsimd
    def _(g):
        g.wait_ge(ve.h, part_ve[-1])
        P.sgw.issue(g, pr_d, prow[:])
        g.wait_ge(P.sgw.sems[0].h, P.sgw.sems[0].n)
        g.collective_compute("AllReduce", ALU.add, replica_groups=GROUPS8,
                             ins=[pr_d], outs=[sums_d]).then_inc(cc.h, 1)
    v_cc_ar = cc.inc(1)

    @P.block.gpsimd
    def _(g):
        g.wait_ge(cc.h, v_cc_ar)
        P.sgw.issue(g, mrow_sb[:], sums_d)
    v_sums = (P.sgw.sems[0], P.sgw.sems[0].n)

    def vfn_m(V):
        V.wait_ge(v_sums[0].h, v_sums[1])
        for wi, (nm, R, Cc, nelem, clip) in enumerate(WPREP):
            sl = slice(wi, wi + 1)
            if clip is None:
                V.tensor_single_scalar(mrow_sb[0:1, sl], mrow_sb[0:1, sl],
                                       1.0 / nelem, ALU.mult)
            else:
                V.tensor_scalar(mrow_sb[0:1, sl], mrow_sb[0:1, sl], 1.0 / nelem,
                                clip, ALU.mult, ALU.max)
        V.tensor_single_scalar(throw[:], mrow_sb[:], 0.5, ALU.mult)
    v_m = P.vbody(vfn_m)

    @P.block.gpsimd
    def _(g):
        g.wait_ge(ve.h, v_m)
        P.sgw.issue(g, io["mrow"], mrow_sb[:])

    @P.block.tensor
    def _(t):
        t.wait_ge(ve.h, v_m)
        t.matmul(P.PSF[:, 7 * 512:7 * 512 + 8], C["ones1r32"][0:1, :],
                 throw[0:1, 0:8], start=True, stop=True).then_inc(pe.h, 1)
    v_thb = pe.inc(1)

    def vfn_tb(V):
        V.wait_ge(pe.h, v_thb)
        V.tensor_copy(th_bc[:, 0:8], P.PSF[:, 7 * 512:7 * 512 + 8])
        V.tensor_single_scalar(th_bc[:, 8:16], th_bc[:, 0:8], -1.0, ALU.mult)
    w["v_tb"] = P.vbody(vfn_tb)


def emit_wprep_quant(P, C, io, w, names):
    ve, cc = P.sem("ve"), P.sem("cc")
    GROUPS8 = [list(range(8))]
    wb, q16, gtt, ltt, th_bc = w["wb"], w["q16"], w["gtt"], w["ltt"], w["th_bc"]
    cc_gates = {}
    widx = {nm: i for i, (nm, _, _, _, _) in enumerate(WPREP)}
    ag_jobs = []
    for nm in names:
        wi = widx[nm]
        R, Cc = WPREP[wi][1], WPREP[wi][2]
        wsl = io[nm + "_sl"]
        pub = io[nm + "_pub"]
        tiles = tile_list(R, Cc)
        ld_v, ve_done, st_v = [], [], []
        prev_st = w["prev_st"]
        for i, (r, pr_, c, pc) in enumerate(tiles):
            bi = i % 2

            @P.block.gpsimd
            def _(g):
                if i >= 2:
                    g.wait_ge(ve.h, ve_done[i - 2])
                else:
                    g.wait_ge(ve.h, w["prev_q_ve"] if w["prev_q_ve"] else w["v_tb"])
                ld_v.append(P.sqd.issue(g, wb[bi][0:pr_, 0:pc],
                                        wsl[r:r + pr_, c:c + pc], i))

            def vfn(V, i=i, bi=bi, pr_=pr_, pc=pc, wi=wi, prev_st=prev_st):
                V.wait_ge(ld_v[i][0].h, ld_v[i][1])
                if i == 0:
                    V.wait_ge(ve.h, w["v_tb"])
                if i < 2:
                    for s_, v_ in prev_st[-2:]:
                        V.wait_ge(s_.h, v_)
                if i >= 2:
                    V.wait_ge(st_v[i - 2][0].h, st_v[i - 2][1])
                V.tensor_single_scalar(gtt[0:pr_, 0:pc], wb[bi][0:pr_, 0:pc],
                                       th_bc[0:pr_, wi:wi + 1], ALU.is_gt)
                V.tensor_single_scalar(ltt[0:pr_, 0:pc], wb[bi][0:pr_, 0:pc],
                                       th_bc[0:pr_, 8 + wi:9 + wi], ALU.is_lt)
                V.tensor_sub(q16[bi][0:pr_, 0:pc], gtt[0:pr_, 0:pc],
                             ltt[0:pr_, 0:pc])
            ve_done.append(P.vbody(vfn))

            @P.block.gpsimd
            def _(g):
                g.wait_ge(ve.h, ve_done[i])
                st_v.append(P.sol.issue(g, pub[r:r + pr_, c:c + pc],
                                        q16[bi][0:pr_, 0:pc], i))

        w["prev_q_ve"] = ve_done[-1]
        w["prev_st"] = st_v[-2:]
        ag_jobs.append((nm, pub, [(s.h, s.n) for s in P.sol.sems if s.n]))

    for nm, pub, gates in ag_jobs:
        @P.block.gpsimd
        def _(g, pub=pub, gates=gates):
            for h_, v_ in gates:
                g.wait_ge(h_, v_)
            g.collective_compute("AllGather", ALU.bypass, replica_groups=GROUPS8,
                                 ins=[pub], outs=[io[nm + "_ag"]]).then_inc(cc.h, 1)
        cc_gates[nm] = cc.inc(1)
    return cc_gates


WCAST = [("watt", 256, 6144), ("wp1", 256, 2048), ("wf", 256, 2048)]


def emit_wprep_cast(P, C, io, w, names, stage):
    cc = P.sem("cc")
    GROUPS8 = [list(range(8))]
    cc_gates = {}
    if stage == 0:
        w["cast_st"] = {}

        @P.block.gpsimd
        def _(g):
            for nm in names:
                w["cast_st"][nm] = P.srr.issue(g, io[nm + "_pub"], io[nm + "_sl"], 1)
        return cc_gates

    for nm in names:
        st = w["cast_st"][nm]

        @P.block.gpsimd
        def _(g, st=st, nm=nm):
            g.wait_ge(st[0].h, st[1])
            g.collective_compute("AllGather", ALU.bypass, replica_groups=GROUPS8,
                                 ins=[io[nm + "_pub"]],
                                 outs=[io[nm + "_ag"]]).then_inc(cc.h, 1)
        cc_gates[nm] = cc.inc(1)
    return cc_gates


def _finish_early(P):
    @P.block.gpsimd
    def _(g):
        P.wait_all_streams(g)

    @P.block.sync
    def _(sy):
        P.wait_all_streams(sy)
    return P.finish()

def emit_l0a(P, C, io, upto=99, walloc_cb=None, wprep_cb=None):
    P.tag = "a"
    sc_main = ExitStack()
    P.default_sc = sc_main
    xT = io["xT"]
    qkvnrm = io["qkvnrm"]
    s_col = io["s_col"]
    s_row = io["s_row"]
    xn_out = io["xn0"]
    oloc_out = io["oloc"]
    qdec_out = io["qdec"]
    kvagg_out = io["kvagg"]

    wq_q = io["qkv_ag"]
    cc_qkv = io.get("cc_qkv", 0)
    rowscr = P.dram("rowscr", [1, 4096], F32).ap()
    rowscr2 = P.dram("rowscr2", [1, 4096], F32).ap()

    pe, ac, ve, gp, cc = (P.sem(s) for s in ("pe", "ac", "ve", "gp", "cc"))

    s_sb = P.sbuf("s_sb", [H, 1], F32)
    srow_sb = P.sbuf("srow_sb", [1, H], F32)
    nrm_sb = P.sbuf("nrm_sb", [128, DT], F32)
    neg_s = P.sbuf("neg_s", [H, 1], F32)
    sT_bias = P.sbuf("sT_bias", [H, 1], F32)
    lam_t16 = P.sbuf("lam_t16", [H, 128], F32)
    lam_rev16 = P.sbuf("lam_rev16", [H, 128], F32)
    lamT16 = P.sbuf("lamT16", [H, 1], F32)
    powj16 = P.sbuf("powj16", [H, 8], F32)
    iota_t = P.sbuf("iota_t", [H, 128], F32)
    iota_j = P.sbuf("iota_j", [H, 8], F32)
    lam_btall = P.sbuf("lam_btall", [128, H * 128], BF16)
    lamrev_btall = P.sbuf("lamrev_btall", [128, H * 128], BF16)
    lam_bt = [lam_btall[:, h * 128:(h + 1) * 128] for h in range(H)]
    lamrev_bt = [lamrev_btall[:, h * 128:(h + 1) * 128] for h in range(H)]
    powj_bc = P.sbuf("powj_bc", [128, 128], F32)
    lamT_bc = P.sbuf("lamT_bc", [128, 16], F32)
    lamT_ball = P.sbuf("lamT_ball", [128, H * 128], F32)
    negs_bc = P.sbuf("negs_bc", [128, 16], F32)
    diffpos = P.sbuf("diffpos", [128, 128], F32)
    maskTall = P.sbuf("maskTall", [128, H * 128], BF16)
    maskT = [maskTall[:, h * 128:(h + 1) * 128] for h in range(H)]
    h16 = [P.sbuf(f"h16_{i}", [128, 1024], BF16) for i in range(DT)]
    V16 = [P.sbuf(f"V16_{c}", [128, 2048], BF16) for c in range(CPC)]
    S = P.sbuf("S", [128, 2048], F32)
    S16 = P.sbuf("S16", [128, 2048], BF16)

    @P.block.gpsimd
    def _(g):
        P.mvec.issue(g, s_sb[:], s_col)
        P.mvec.issue(g, srow_sb[:], s_row)
        for i in range(DT):
            P.mvec.issue(g, nrm_sb[:, i:i + 1], qkvnrm[i * 128:(i + 1) * 128, 0:1])
    v_vecs = (P.mvec.sems[0], P.mvec.sems[0].n)

    def gfn_iota(g, G):
        G.iota(iota_t[:], pattern=[[1, 128]], base=0, channel_multiplier=0,
               allow_small_or_imprecise_dtypes=True)
        G.iota(iota_j[:], pattern=[[128, 8]], base=0, channel_multiplier=0,
               allow_small_or_imprecise_dtypes=True)
    v_iot = P.gbody(gfn_iota)

    def vfn_negs(V):
        V.wait_ge(v_vecs[0].h, v_vecs[1])
        V.tensor_single_scalar(neg_s[:], s_sb[:], -1.0, ALU.mult)
        V.tensor_single_scalar(sT_bias[:], s_sb[:], -float(T), ALU.mult)
    v_negs = P.vbody(vfn_negs)

    def sfn_tabs(S_):
        S_.wait_ge(gp.h, v_iot)
        S_.wait_ge(ve.h, v_negs)
        S_.activation(lam_t16[:], iota_t[:], AF.Exp, scale=neg_s[:, 0:1])
        S_.activation(lam_rev16[:], iota_t[:], AF.Exp, scale=s_sb[:, 0:1],
                      bias=sT_bias[:, 0:1])
        S_.activation(lamT16[:], s_sb[:], AF.Exp, scale=-float(T))
        S_.activation(powj16[:], iota_j[:], AF.Exp, scale=neg_s[:, 0:1])
    v_tabs = P.sbody(sfn_tabs)

    @P.block.gpsimd
    def _(g):
        g.wait_ge(ac.h, v_tabs)
        for hh in range(H):
            P.mvec.issue(g, rowscr[0:1, hh * 128:(hh + 1) * 128],
                         lam_t16[hh:hh + 1, :])
            P.mvec.issue(g, rowscr2[0:1, hh * 128:(hh + 1) * 128],
                         lam_rev16[hh:hh + 1, :])
            P.mvec.issue(g, rowscr[0:1, 2048 + hh * 8:2048 + (hh + 1) * 8],
                         powj16[hh:hh + 1, :])
            P.mvec.issue(g, rowscr2[0:1, 2048 + hh:2049 + hh],
                         lamT16[hh:hh + 1, 0:1])
    v_spread = (P.mvec.sems[0], P.mvec.sems[0].n)

    sc_rows = ExitStack()
    lamrow = P.sbuf("lamrow", [1, 2048], BF16, sc_rows)
    lamrevrow = P.sbuf("lamrevrow", [1, 2048], BF16, sc_rows)
    powrow = P.sbuf("powrow", [1, 128], F32, sc_rows)
    lamTrow = P.sbuf("lamTrow", [1, 16], F32, sc_rows)

    @P.block.gpsimd
    def _(g):
        g.wait_ge(v_spread[0].h, v_spread[1])
        P.mvec.issue(g, lamrow[:], rowscr[0:1, 0:2048])
        P.mvec.issue(g, lamrevrow[:], rowscr2[0:1, 0:2048])
        P.mvec.issue(g, powrow[:], rowscr[0:1, 2048:2176])
        P.mvec.issue(g, lamTrow[:], rowscr2[0:1, 2048:2064])
    v_rowsld = (P.mvec.sems[0], P.mvec.sems[0].n)

    last_copy = [0]
    for grp in range(4):
        for which, (src, dstl) in enumerate(((lamrow, lam_bt), (lamrevrow, lamrev_bt))):
            @P.block.tensor
            def _(t):
                if grp == 0 and which == 0:
                    t.wait_ge(v_rowsld[0].h, v_rowsld[1])
                else:
                    t.wait_ge(ve.h, last_copy[0])
                t.matmul(P.bank(6), C["ones1r16"][0:1, :],
                         src[0:1, grp * 512:(grp + 1) * 512],
                         start=True, stop=True).then_inc(pe.h, 1)
            v1 = pe.inc(1)

            def vfn(V, grp=grp, dstl=dstl, v1=v1):
                V.wait_ge(pe.h, v1)
                for k in range(4):
                    hh = grp * 4 + k
                    V.tensor_copy(dstl[hh][:],
                                  P.PSF[:, 6 * 512 + k * 128:6 * 512 + (k + 1) * 128])
            last_copy[0] = P.vbody(vfn)

    @P.block.tensor
    def _(t):
        t.wait_ge(ve.h, last_copy[0])
        t.matmul(P.bankc(6, 128), C["ones1r32"][0:1, :], powrow[0:1, 0:128],
                 start=True, stop=True)
        t.matmul(P.PSF[:, 6 * 512 + 128:6 * 512 + 144], C["ones1r32"][0:1, :],
                 lamTrow[0:1, 0:16], start=True, stop=True)
        t.matmul(P.PSF[:, 6 * 512 + 144:6 * 512 + 160], C["ones1r32"][0:1, :],
                 srow_sb[0:1, 0:16], start=True, stop=True).then_inc(pe.h, 1)
    v_bt = pe.inc(1)

    def vfn_bc(V):
        V.wait_ge(pe.h, v_bt)
        V.tensor_copy(powj_bc[:], P.bankc(6, 128))
        V.tensor_copy(lamT_bc[:], P.PSF[:, 6 * 512 + 128:6 * 512 + 144])
        V.tensor_single_scalar(negs_bc[:], P.PSF[:, 6 * 512 + 144:6 * 512 + 160],
                               -1.0, ALU.mult)
        V.memset(lamT_ball[:], 1.0)
        for hh in range(H):
            sl = slice(hh * 128, (hh + 1) * 128)
            V.tensor_single_scalar(lamT_ball[:, sl], lamT_ball[:, sl],
                                   lamT_bc[:, hh:hh + 1], ALU.mult)
    v_bc_done = P.vbody(vfn_bc)
    sc_rows.close()

    def vfn_dp(V):
        V.tensor_single_scalar(diffpos[:], C["iota_pf"][:], -1.0, ALU.mult)
        V.tensor_scalar_max(diffpos[:], diffpos[:], 0.0)
    v_dp = P.vbody(vfn_dp)

    mask_ve = []
    for hh in range(H):
        def sfn_m(S_, hh=hh):
            if hh == 0:
                S_.wait_ge(ve.h, v_bc_done)
                S_.wait_ge(ve.h, v_dp)
            S_.activation(maskT[hh][:], diffpos[:], AF.Exp,
                          scale=negs_bc[:, hh:hh + 1])
        v_m = P.sbody(sfn_m)

        def vfn_m(V, hh=hh, v_m=v_m):
            V.wait_ge(ac.h, v_m)
            V.tensor_mul(maskT[hh][:], maskT[hh][:], C["trilT"][:])
        mask_ve.append(P.vbody(vfn_m))
    v_masks = mask_ve[-1]

    if walloc_cb is not None:
        walloc_cb()

    sc_pool = ExitStack()
    alloc_pools(P, sc_pool)

    sc_ln = ExitStack()
    xn_ready, h_ready, xn_st = emit_ln_h(
        P, C, sc_ln, xT, h16, nrm_sb, "ln0", xn_dram=xn_out,
        entry=[(ve, v_bc_done)])
    v_h = h_ready[-1]
    sc_ln.close()
    sc_pool.close()
    stx_fin = [(s.h, s.n) for s in P.stx.sems if s.n]

    if wprep_cb is not None:
        stx_fin = stx_fin + wprep_cb()
        P.tag = "a"

    QW = P.sbuf("QW", [128, H * 1024], BF16)
    KW = P.sbuf("KW", [128, H * 1024], BF16)
    Q16 = [QW[:, h * 1024:(h + 1) * 1024] for h in range(H)]
    K16 = [KW[:, h * 1024:(h + 1) * 1024] for h in range(H)]
    wqt = [P.sbuf(f"wqt{i}", [128, 512], BF16) for i in range(2)]
    qtall = P.sbuf("qtall", [128, H * 128], BF16)
    qdall = P.sbuf("qdall", [128, H * 128], BF16)
    ktall = P.sbuf("ktall", [128, H * 128], BF16)
    ktmall = P.sbuf("ktmall", [128, H * 128], BF16)
    smaskall = P.sbuf("smaskall", [128, H * 128], BF16)
    otall = P.sbuf("otall", [128, H * 128], BF16)

    ld_i = [0]
    ldv = {}
    mm_pe = {}

    def wload(base_f, kt):
        i = ld_i[0]
        bi = i % 2

        @P.block.sync
        def _(sy):
            if i == 0:
                ccg = io.get("cc_qkv", 0)
                if ccg:
                    sy.wait_ge(cc.h, ccg)
                sy.wait_ge(ve.h, v_h)
                for h_, v_ in stx_fin:
                    sy.wait_ge(h_, v_)
            if i >= 2:
                sy.wait_ge(pe.h, mm_pe[i - 2])
            ldv[i] = P.wl.issue(sy, wqt[bi][:],
                                wq_q[kt * 128:(kt + 1) * 128, base_f:base_f + 512], i)
        return i, bi

    evac_gate = [0]
    for role in range(2):
        for grp in range(4):
            base_f = role * DM + grp * 512
            for kt in range(DT):
                i, bi = wload(base_f, kt)

                @P.block.tensor
                def _(t):
                    t.wait_ge(ldv[i][0].h, ldv[i][1])
                    if kt == 0 and evac_gate[0]:
                        t.wait_ge(ve.h, evac_gate[0])
                    mm = None
                    for k in range(4):
                        for off in (0, 512):
                            mm = t.matmul(
                                P.PSF[:, k * 1024 + off:k * 1024 + off + 512],
                                wqt[bi][:, k * 128:(k + 1) * 128],
                                h16[kt][:, off:off + 512],
                                start=(kt == 0), stop=(kt == DT - 1))
                    mm.then_inc(pe.h, 1)
                mm_pe[i] = pe.inc(1)
                ld_i[0] += 1
            v_grp = pe.n

            def vfn_ev(V, grp=grp, role=role, v_grp=v_grp):
                V.wait_ge(pe.h, v_grp)
                if role == 0 and grp == 0:
                    for h_, v_ in stx_fin:
                        V.wait_ge(h_, v_)
                for k in range(4):
                    hh = grp * 4 + k
                    dst = Q16[hh] if role == 0 else K16[hh]
                    V.tensor_copy(dst[:], P.PSF[:, k * 1024:(k + 1) * 1024])
            evac_gate[0] = P.vbody(vfn_ev)

    for vg in range(4):
        base_f = 2 * DM + vg * 512
        for kt in range(DT):
            i, bi = wload(base_f, kt)

            @P.block.tensor
            def _(t):
                t.wait_ge(ldv[i][0].h, ldv[i][1])
                if kt == 0:
                    t.wait_ge(ve.h, evac_gate[0])
                mm = None
                for c in range(CPC):
                    mm = t.matmul(P.bank(c), h16[kt][:, c * 128:(c + 1) * 128],
                                  wqt[bi][:, 0:512],
                                  start=(kt == 0), stop=(kt == DT - 1))
                mm.then_inc(pe.h, 1)
            mm_pe[i] = pe.inc(1)
            ld_i[0] += 1
        v_vg = pe.n

        def vfn_vev(V, vg=vg, v_vg=v_vg):
            V.wait_ge(pe.h, v_vg)
            for c in range(CPC):
                V.tensor_copy(V16[c][:, vg * 512:(vg + 1) * 512], P.bank(c))
        evac_gate[0] = P.vbody(vfn_vev)

    v_qkv = evac_gate[0]

    def vfn_s0(V):
        V.wait_ge(ve.h, v_qkv)
        V.memset(S[:], 0.0)
        V.memset(S16[:], 0.0)
    v_s0 = P.vbody(vfn_s0)

    @P.block.tensor
    def _(t):
        t.wait_ge(ve.h, v_masks)
        t.wait_ge(ve.h, v_s0)

    prev = {"qdec_gq": None, "oloc_gq": None}

    def v3(ap2d):
        return ap2d.rearrange("p (h j) -> p h j", h=H)

    for c in range(CPC):
        qap = v3(QW[:])[:, :, c * 128:(c + 1) * 128]
        kap = v3(KW[:])[:, :, c * 128:(c + 1) * 128]

        @P.block.tensor
        def _(t):
            mm = None
            for hh in range(H):
                cs = slice(c * 128, (c + 1) * 128)
                mm = t.matmul(P.PSF[:, hh * 128:(hh + 1) * 128],
                              K16[hh][:, cs], Q16[hh][:, cs],
                              start=True, stop=True)
            mm.then_inc(pe.h, 1)
        v_sc = pe.inc(1)

        qd_prev = prev["qdec_gq"]

        def vfn_a(V, c=c, v_sc=v_sc, qd_prev=qd_prev, qap=qap, kap=kap):
            V.wait_ge(pe.h, v_sc)
            if qd_prev is not None:
                V.wait_ge(qd_prev[0].h, qd_prev[1])
            V.tensor_mul(smaskall[:], P.PSF[:, 0:2048], maskTall[:])
            V.tensor_mul(v3(qtall[:]), qap, v3(lam_btall[:]))
            V.tensor_mul(v3(ktall[:]), kap, v3(lamrev_btall[:]))
            for hh in range(H):
                sl = slice(hh * 128, (hh + 1) * 128)
                V.tensor_single_scalar(qdall[:, sl], qtall[:, sl],
                                       powj_bc[:, hh * 8 + c:hh * 8 + c + 1],
                                       ALU.mult)
        v_sm = P.vbody(vfn_a)

        @P.block.gpsimd
        def _(g):
            g.wait_ge(ve.h, v_sm)
            prev["qdec_gq"] = P.sqd.issue(g, qdec_out[c], qdall[:], c)

        @P.block.tensor
        def _(t):
            t.wait_ge(ve.h, v_sm)
            mm = None
            for hh in range(H):
                hs = slice(hh * 128, (hh + 1) * 128)
                t.matmul(P.PSF[:, 2048 + hh * 128:2048 + (hh + 1) * 128],
                         V16[c][:, hs], smaskall[:, hs],
                         start=True, stop=(c == 0))
                if c > 0:
                    t.matmul(P.PSF[:, 2048 + hh * 128:2048 + (hh + 1) * 128],
                             S16[:, hs], qtall[:, hs], start=False, stop=True)
                mm = t.matmul(P.PSF[:, hh * 128:(hh + 1) * 128], ktall[:, hs],
                              C["ident16"][:], start=True, stop=True)
            mm.then_inc(pe.h, 1)
        v_oc = pe.inc(1)

        ol_prev = prev["oloc_gq"]

        def vfn_b(V, v_oc=v_oc, ol_prev=ol_prev):
            V.wait_ge(pe.h, v_oc)
            if ol_prev is not None:
                V.wait_ge(ol_prev[0].h, ol_prev[1])
            V.tensor_copy(otall[:], P.PSF[:, 2048:4096])
            V.tensor_copy(ktmall[:], P.PSF[:, 0:2048])
        v_ev = P.vbody(vfn_b)

        @P.block.gpsimd
        def _(g):
            g.wait_ge(ve.h, v_ev)
            prev["oloc_gq"] = P.sol.issue(g, oloc_out[c], otall[:], c)

        @P.block.tensor
        def _(t):
            t.wait_ge(ve.h, v_ev)
            mm = None
            for hh in range(H):
                hs = slice(hh * 128, (hh + 1) * 128)
                mm = t.matmul(P.PSF[:, 2048 + hh * 128:2048 + (hh + 1) * 128],
                              ktmall[:, hs], V16[c][:, hs], start=True, stop=True)
            mm.then_inc(pe.h, 1)
        v_kv = pe.inc(1)

        def vfn_c(V, c=c, v_kv=v_kv):
            V.wait_ge(pe.h, v_kv)
            V.tensor_mul(S[:], S[:], lamT_ball[:])
            V.tensor_add(S[:], S[:], P.PSF[:, 2048:4096])
            if c < CPC - 1:
                V.tensor_copy(S16[:], S[:])
        P.vbody(vfn_c)

    v_scan = ve.n

    @P.block.gpsimd
    def _(g):
        g.wait_ge(ve.h, v_scan)
        P.mvec.issue(g, kvagg_out, S[:])

    sc_main.close()
    P.default_sc = None


def build_l0a():
    P = LB("l0a").start()
    io = {
        "xT": P.din("xT", [DM, TOK], F32).ap(),
        "qkv_ag": P.din("qkv_ag", [DM, 3 * DM], BF16).ap(),
        "qkvnrm": P.din("qkvnrm", [DM, 1], F32).ap(),
        "s_col": P.din("s_col", [H, 1], F32).ap(),
        "s_row": P.din("s_row", [1, H], F32).ap(),
        "xn0": P.dout("xn_out", [DM, TOK], BF16).ap(),
        "oloc": P.dout("oloc_out", [CPC, 128, H * 128], BF16).ap(),
        "qdec": P.dout("qdec_out", [CPC, 128, H * 128], BF16).ap(),
        "kvagg": P.dout("kvagg_out", [128, H * 128], F32).ap(),
        "mq": P.dout("mq_out", [1, 1], F32).ap(),
    }
    C = emit_consts(P)
    emit_l0a(P, C, io)

    @P.block.gpsimd
    def _(g):
        P.wait_all_streams(g)

    @P.block.sync
    def _(sy):
        P.wait_all_streams(sy)

    return P.finish()


def emit_rms_apply(P, C, sc, tiles_in, out_tiles, normw_sb, eps, tag,
                   entry=(), bank_row=0, bank_bc=2):
    pe, ac, ve, gp = (P.sem(s) for s in ("pe", "ac", "ve", "gp"))
    ps_row = P.PSF[0:1, bank_row * 512:bank_row * 512 + 1024]
    ps_bc = P.PSF[:, bank_bc * 512:bank_bc * 512 + 1024]
    sq = [P.sbuf(f"{tag}_sq{i}", [128, 1024], BF16, sc) for i in range(2)]
    row = P.sbuf(f"{tag}_row", [1, 1024], F32, sc)
    epst = P.sbuf(f"{tag}_eps", [1, 1], F32, sc)
    rb = P.sbuf(f"{tag}_rb", [128, 1024], F32, sc)

    def gfn(g, G):
        G.memset(epst[:], eps)
    v_eps = P.gbody(gfn)

    ac_done, pe_done = [], []
    for i in range(DT):
        bi = i % 2

        def sfn(S_, i=i, bi=bi):
            if i == 0:
                for es, evv in entry:
                    S_.wait_ge(es.h, evv)
            if i >= 2:
                S_.wait_ge(pe.h, pe_done[i - 2])
            S_.activation(sq[bi][:], tiles_in[i][:], AF.Square)
        ac_done.append(P.sbody(sfn))

        @P.block.tensor
        def _(t):
            t.wait_ge(ac.h, ac_done[i])
            mm = None
            for off in (0, 512):
                mm = t.matmul(ps_row[0:1, off:off + 512], C["ones16"][:, 0:1],
                              sq[bi][:, off:off + 512], start=(i == 0),
                              stop=(i == DT - 1))
            mm.then_inc(pe.h, 1)
        pe_done.append(pe.inc(1))

    def vfn1(V):
        V.wait_ge(pe.h, pe_done[-1])
        V.tensor_single_scalar(row[:], ps_row, 1.0 / DM, ALU.mult)
    v1 = P.vbody(vfn1)

    def sfn1(S_):
        S_.wait_ge(ve.h, v1)
        S_.wait_ge(gp.h, v_eps)
        S_.activation(row[:], row[:], AF.Ln, bias=epst[0:1, 0:1])
        S_.activation(row[:], row[:], AF.Exp, scale=-0.5)
    v2 = P.sbody(sfn1)

    @P.block.tensor
    def _(t):
        t.wait_ge(ac.h, v2)
        mm = None
        for off in (0, 512):
            mm = t.matmul(ps_bc[:, off:off + 512], C["ones1r32"][0:1, :],
                          row[0:1, off:off + 512], start=True, stop=True)
        mm.then_inc(pe.h, 1)
    v3 = pe.inc(1)

    def vfn2(V):
        V.wait_ge(pe.h, v3)
        V.tensor_copy(rb[:], ps_bc)
    v4 = P.vbody(vfn2)

    out_ready = []
    for i in range(DT):
        def vfn3(V, i=i):
            V.scalar_tensor_tensor(out_tiles[i][:], tiles_in[i][:],
                                   normw_sb[:, i:i + 1], rb[:], ALU.mult, ALU.mult)
        out_ready.append(P.vbody(vfn3))
    return out_ready



def emit_l0b(P, C, io):
    P.tag = "b"
    sc_main = ExitStack()
    P.default_sc = sc_main
    xT = io["xT"]
    xnT = io["xn0"]
    oloc_in = io["oloc"]
    qdec_in = io["qdec"]
    kvpre = io["kvall"]
    wfac_row = io["wfac_row"]
    gap_row = io["gap_row"]
    s_col = io["s_col"]
    normw_in = io["normw_in"]
    pnormw_in = io["pnormw_in"]
    x1_out = io["x1"]
    xn1_out = io["xn1"]

    wp_q = io["prj_ag"]
    w1_q = io["w1_ag"]
    w3_q = io["w3_ag"]
    w2_q = io["w2_ag"]
    mrow_in = io["mrow"]
    rowscr = P.dram("rowscr", [1, 4096], F32).ap()

    pe, ac, ve, gp = (P.sem(s) for s in ("pe", "ac", "ve", "gp"))

    s_sb = P.sbuf("s_sb", [H, 1], F32)
    mq_sb = P.sbuf("mq_sb", [1, 1], F32)
    nrmw = P.sbuf("nrmw", [128, DT], F32)
    pnrmw = P.sbuf("pnrmw", [128, DT], F32)
    facs16 = P.sbuf("facs16", [H, 3], F32)
    facs_bc = P.sbuf("facs_bc", [128, 48], F32)
    alpha_vec = P.sbuf("alpha_vec", [128, 1], F32)
    s23_vec = P.sbuf("s23_vec", [128, 1], F32)
    S_in = P.sbuf("S_in", [128, H * 128], F32)
    S_in16 = P.sbuf("S_in16", [128, H * 128], BF16)
    xn16 = [P.sbuf(f"xn16_{i}", [128, 1024], BF16) for i in range(DT)]
    sbr = [P.sbuf(f"sbr_{i}", [128, 1024], BF16) for i in range(DT)]
    mrow_sb = P.sbuf("mrow_sb", [1, 8], F32)
    mvec_bc = P.sbuf("mvec_bc", [128, 8], F32)
    OA = [P.sbuf(f"OA{i}", [128, 1024], F32) for i in range(DT)]
    o_acc = OA
    sc_o2 = ExitStack()
    o2t = [P.sbuf(f"o2t_{i}", [128, 1024], BF16, sc_o2) for i in range(DT)]


    @P.block.gpsimd
    def _(g):
        P.mvec.issue(g, s_sb[:], s_col)
        P.mvec.issue(g, mq_sb[:], mrow_in[0:1, 0:1])
        P.mvec.issue(g, mrow_sb[:], mrow_in)
        for i in range(DT):
            P.mvec.issue(g, nrmw[:, i:i + 1], normw_in[i * 128:(i + 1) * 128, 0:1])
            P.mvec.issue(g, pnrmw[:, i:i + 1], pnormw_in[i * 128:(i + 1) * 128, 0:1])
        for i in range(DT):
            P.sxn.issue(g, xn16[i][:], xnT[i * 128:(i + 1) * 128, :])
    v_vl = (P.mvec.sems[0], P.mvec.sems[0].n)
    v_xnld = (P.sxn.sems[0], P.sxn.sems[0].n)

    sc_fac = ExitStack()
    s128 = P.sbuf("s128", [H, 1], F32, sc_fac)
    gap_bc = P.sbuf("gap_bc", [H, 3], F32, sc_fac)
    w_bc = P.sbuf("w_bc", [H, 3], F32, sc_fac)
    arow = P.sbuf("arow", [1, 1], F32, sc_fac)
    facrow = P.sbuf("facrow", [1, 48], F32, sc_fac)
    gapw = P.sbuf("gapw", [1, 6], F32, sc_fac)

    @P.block.gpsimd
    def _(g):
        P.sgw.issue(g, gapw[0:1, 0:3], gap_row)
        P.sgw.issue(g, gapw[0:1, 3:6], wfac_row)
    v_gw = (P.sgw.sems[0], P.sgw.sems[0].n)

    @P.block.tensor
    def _(t):
        t.wait_ge(v_gw[0].h, v_gw[1])
        t.matmul(P.bankc(6, 6, rows=H), C["ones1r32"][0:1, 0:H], gapw[0:1, 0:6],
                 start=True, stop=True).then_inc(pe.h, 1)
    v_gb = pe.inc(1)

    def vfn_f1(V):
        V.wait_ge(pe.h, v_gb)
        V.wait_ge(v_vl[0].h, v_vl[1])
        V.tensor_copy(gap_bc[:], P.PSF[0:H, 6 * 512:6 * 512 + 3])
        V.tensor_copy(w_bc[:], P.PSF[0:H, 6 * 512 + 3:6 * 512 + 6])
        V.tensor_single_scalar(s128[:], s_sb[:], -float(T), ALU.mult)
    v_f1 = P.vbody(vfn_f1)

    def sfn_f(S_):
        S_.wait_ge(ve.h, v_f1)
        S_.activation(facs16[:], gap_bc[:], AF.Exp, scale=s128[:, 0:1])
    v_f2 = P.sbody(sfn_f)

    def vfn_f2(V):
        V.wait_ge(ac.h, v_f2)
        V.tensor_mul(facs16[:], facs16[:], w_bc[:])
        V.tensor_copy(arow[:], mq_sb[:])
        V.tensor_mul(gapw[0:1, 0:1], arow[:], arow[:])
        V.tensor_mul(arow[:], gapw[0:1, 0:1], arow[:])
        V.tensor_single_scalar(arow[:], arow[:], INV_SQRT_HD, ALU.mult)
    v_f3 = P.vbody(vfn_f2)

    @P.block.gpsimd
    def _(g):
        g.wait_ge(ve.h, v_f3)
        for hh in range(H):
            P.mvec.issue(g, rowscr[0:1, hh * 3:(hh + 1) * 3], facs16[hh:hh + 1, :])
    v_sp = (P.mvec.sems[0], P.mvec.sems[0].n)

    @P.block.gpsimd
    def _(g):
        g.wait_ge(v_sp[0].h, v_sp[1])
        P.mvec.issue(g, facrow[:], rowscr[0:1, 0:48])
    v_fr = (P.mvec.sems[0], P.mvec.sems[0].n)

    @P.block.tensor
    def _(t):
        t.wait_ge(v_fr[0].h, v_fr[1])
        t.matmul(P.bankc(6, 48), C["ones1r32"][0:1, :], facrow[0:1, 0:48],
                 start=True, stop=True)
        t.matmul(P.bankc(7, 1), C["ones1r32"][0:1, :], arow[0:1, 0:1],
                 start=True, stop=True)
        t.matmul(P.PSF[:, 7 * 512 + 8:7 * 512 + 16], C["ones1r32"][0:1, :],
                 mrow_sb[0:1, 0:8], start=True, stop=True).then_inc(pe.h, 1)
    v_fb = pe.inc(1)

    def vfn_f4(V):
        V.wait_ge(pe.h, v_fb)
        V.tensor_copy(facs_bc[:, 0:48], P.bankc(6, 48))
        V.tensor_copy(alpha_vec[:], P.bankc(7, 1))
        V.tensor_copy(mvec_bc[:], P.PSF[:, 7 * 512 + 8:7 * 512 + 16])
    v_facs = P.vbody(vfn_f4)
    sc_fac.close()

    sc_kv = ExitStack()
    kvb = [P.sbuf(f"kvb{i}", [128, H * 128], F32, sc_kv) for i in range(3)]
    kv_ld = []

    @P.block.gpsimd
    def _(g):
        g.wait_ge(ve.h, v_facs)
        kv_ld.append(P.mld.issue(g, kvb[0][:], kvpre[0], 0))
        kv_ld.append(P.mld.issue(g, kvb[1][:], kvpre[1], 1))
        kv_ld.append(P.sqd.issue(g, kvb[2][:], kvpre[2], 0))

    def vfn_s(V):
        V.wait_ge(ve.h, v_facs)
        V.memset(S_in[:], 0.0)
        for p in range(3):
            V.wait_ge(kv_ld[p][0].h, kv_ld[p][1])
            for hh in range(H):
                hs = slice(hh * 128, (hh + 1) * 128)
                V.scalar_tensor_tensor(S_in[:, hs], kvb[p][:, hs],
                                       facs_bc[:, hh * 3 + p:hh * 3 + p + 1],
                                       S_in[:, hs], ALU.mult, ALU.add)
        V.tensor_copy(S_in16[:], S_in[:])
    v_sin = P.vbody(vfn_s)
    sc_kv.close()

    sc_o = ExitStack()
    o_sb = OA
    qdb = [P.sbuf(f"qdb{i}", [128, 1024], BF16, sc_o) for i in range(2)]
    olb = [P.sbuf(f"olb{i}", [128, 1024], BF16, sc_o) for i in range(2)]
    corr_pe, comb_ve, qld, old = [], [], [], []
    for hh in range(H):
        bi = hh % 2

        @P.block.gpsimd
        def _(g):
            if hh >= 2:
                g.wait_ge(ve.h, comb_ve[hh - 2])
            else:
                g.wait_ge(ve.h, v_sin)
            qsrc = qdec_in[:, :, hh * 128:(hh + 1) * 128].rearrange(
                "c p j -> p c j")
            osrc = oloc_in[:, :, hh * 128:(hh + 1) * 128].rearrange(
                "c p j -> p c j")
            qld.append(P.sqd.issue(
                g, qdb[bi][:].rearrange("p (c j) -> p c j", c=CPC), qsrc, hh))
            old.append(P.sol.issue(
                g, olb[bi][:].rearrange("p (c j) -> p c j", c=CPC), osrc, hh))

        @P.block.tensor
        def _(t):
            t.wait_ge(qld[hh][0].h, qld[hh][1])
            if hh == 0:
                t.wait_ge(ve.h, v_sin)
            if hh >= 2:
                t.wait_ge(ve.h, comb_ve[hh - 2])
            bp = (hh % 2) * 2
            mm = None
            for off in (0, 512):
                mm = t.matmul(P.PSF[:, bp * 512 + off:bp * 512 + off + 512],
                              S_in16[:, hh * 128:(hh + 1) * 128],
                              qdb[bi][:, off:off + 512], start=True, stop=True)
            mm.then_inc(pe.h, 1)
        corr_pe.append(pe.inc(1))

        def vfn_cb(V, hh=hh, bi=bi):
            V.wait_ge(pe.h, corr_pe[hh])
            V.wait_ge(old[hh][0].h, old[hh][1])
            bp = (hh % 2) * 2
            V.tensor_add(o_sb[hh][:], P.PSF[:, bp * 512:bp * 512 + 1024], olb[bi][:])
            V.tensor_single_scalar(o_sb[hh][:], o_sb[hh][:], alpha_vec[:, 0:1],
                                   ALU.mult)
        comb_ve.append(P.vbody(vfn_cb))

    sc_rms = ExitStack()
    r1 = emit_rms_apply(P, C, sc_rms, o_sb, o_sb, nrmw, 1e-6, "rm1",
                        entry=[(ve, comb_ve[-1])], bank_row=4, bank_bc=6)
    r2 = emit_rms_apply(P, C, sc_rms, o_sb, o2t, pnrmw, EPS32, "rm2",
                        entry=[(ve, r1[-1])], bank_row=4, bank_bc=6)
    sc_rms.close()
    sc_o.close()
    v_o2 = r2[-1]

    sc_mp = ExitStack()
    wpt = [P.sbuf(f"wpt{i}", [128, 512], BF16, sc_mp) for i in range(2)]
    ldv = {}
    mm_pe = {}
    li = [0]

    def pload(g_or_sy_unused):
        pass

    evac = [0]
    for og in range(4):
        for kt in range(DT):
            i = li[0]
            bi = i % 2

            @P.block.sync
            def _(sy):
                if i == 0:
                    sy.wait_ge(ve.h, v_o2)
                if i >= 2:
                    sy.wait_ge(pe.h, mm_pe[i - 2])
                ldv[i] = P.wl.issue(sy, wpt[bi][:],
                                    wp_q[kt * 128:(kt + 1) * 128,
                                         og * 512:(og + 1) * 512], i)

            @P.block.tensor
            def _(t):
                t.wait_ge(ldv[i][0].h, ldv[i][1])
                if kt == 0 and evac[0]:
                    t.wait_ge(ve.h, evac[0])
                mm = None
                for k in range(4):
                    for off in (0, 512):
                        mm = t.matmul(P.PSF[:, k * 1024 + off:k * 1024 + off + 512],
                                      wpt[bi][:, k * 128:(k + 1) * 128],
                                      o2t[kt][:, off:off + 512],
                                      start=(kt == 0), stop=(kt == DT - 1))
                mm.then_inc(pe.h, 1)
            mm_pe[i] = pe.inc(1)
            li[0] += 1
        v_g = pe.n

        def vfn_pe(V, og=og, v_g=v_g):
            V.wait_ge(pe.h, v_g)
            for k in range(4):
                ot = og * 4 + k
                V.tensor_single_scalar(sbr[ot][:], P.PSF[:, k * 1024:(k + 1) * 1024],
                                       mvec_bc[:, 1:2], ALU.mult)
        evac[0] = P.vbody(vfn_pe)
    v_sbr = evac[0]
    sc_mp.close()
    sc_o2.close()

    sc_mf = ExitStack()
    stq_tot_f = []
    s23row = P.sbuf("s23row", [1, 1], F32, sc_mf)

    def vfn_s23(V):
        V.wait_ge(ve.h, v_sbr)
        V.tensor_mul(s23row[:], mrow_sb[0:1, 3:4], mrow_sb[0:1, 4:5])
    v_s23 = P.vbody(vfn_s23)

    @P.block.tensor
    def _(t):
        t.wait_ge(ve.h, v_s23)
        t.matmul(P.bankc(7, 1), C["ones1r32"][0:1, :], s23row[0:1, 0:1],
                 start=True, stop=True).then_inc(pe.h, 1)
    v_s23b = pe.inc(1)

    def vfn_s23c(V):
        V.wait_ge(pe.h, v_s23b)
        V.tensor_copy(s23_vec[:], P.bankc(7, 1))
    v_s23v = P.vbody(vfn_s23c)
    s1_vec = mvec_bc[:, 2:3]

    GW = 3
    sc_ff = ExitStack()
    w1b = [P.sbuf(f"w1b{k}", [128, GW * 128], BF16, sc_ff) for k in range(DT)]
    w3b = [P.sbuf(f"w3b{k}", [128, GW * 128], BF16, sc_ff) for k in range(DT)]
    w2b = [P.sbuf(f"w2b{l}", [128, 2048], BF16, sc_ff) for l in range(GW)]
    g16 = [P.sbuf(f"g16_{l}", [128, 1024], BF16, sc_ff) for l in range(GW)]
    silu_t = P.sbuf("silu_t", [128, 1024], BF16, sc_ff)

    groups = []
    nht = (HID + 127) // 128
    gstart = 0
    while gstart < nht:
        groups.append(list(range(gstart, min(gstart + GW, nht))))
        gstart += GW

    prev_gate = [v_s23v]
    for gi, grp in enumerate(groups):
        h0 = grp[0] * 128
        gw = sum(min(128, HID - ht * 128) for ht in grp)
        lds = []

        @P.block.gpsimd
        def _(g):
            g.wait_ge(ve.h, prev_gate[0])
            if gi == 0:
                for s_, vv in stq_tot_f:
                    g.wait_ge(s_.h, vv)
            for kt in range(DT):
                lds.append(P.mld.issue(g, w1b[kt][0:128, 0:gw],
                                       w1_q[kt * 128:(kt + 1) * 128, h0:h0 + gw], 0))
                lds.append(P.sqd.issue(g, w3b[kt][0:128, 0:gw],
                                       w3_q[kt * 128:(kt + 1) * 128, h0:h0 + gw], 0))

        ld_done = [(s, s.n) for st in (P.mld, P.sqd) for s in st.sems]

        ph_a_pe = []
        ph_a_ve = []
        for li2, ht in enumerate(grp):
            hw = min(128, HID - ht * 128)
            co = sum(min(128, HID - grp[m] * 128) for m in range(li2))
            bp = (li2 % 2) * 4

            @P.block.tensor
            def _(t):
                if li2 == 0:
                    for s_, vv in ld_done:
                        t.wait_ge(s_.h, vv)
                    t.wait_ge(v_xnld[0].h, v_xnld[1])
                if li2 >= 2:
                    t.wait_ge(ve.h, ph_a_ve[li2 - 2])
                mm = None
                for kt in range(DT):
                    for off in (0, 512):
                        t.matmul(P.PSF[0:hw, bp * 512 + off:bp * 512 + off + 512],
                                 w1b[kt][:, co:co + hw],
                                 xn16[kt][:, off:off + 512],
                                 start=(kt == 0), stop=(kt == DT - 1))
                        mm = t.matmul(
                            P.PSF[0:hw, (bp + 2) * 512 + off:(bp + 2) * 512 + off + 512],
                            w3b[kt][:, co:co + hw],
                            xn16[kt][:, off:off + 512],
                            start=(kt == 0), stop=(kt == DT - 1))
                mm.then_inc(pe.h, 1)
            ph_a_pe.append(pe.inc(1))

            def sfn_si(S_, li2=li2, hw=hw, bp=bp):
                S_.wait_ge(pe.h, ph_a_pe[li2])
                if li2 >= 1:
                    S_.wait_ge(ve.h, ph_a_ve[li2 - 1])
                S_.activation(silu_t[0:hw, :], P.PSF[0:hw, bp * 512:bp * 512 + 1024],
                              AF.Silu, scale=mvec_bc[0:hw, 2:3])
            v_si = P.sbody(sfn_si)

            def vfn_g(V, li2=li2, hw=hw, bp=bp, v_si=v_si):
                V.wait_ge(ac.h, v_si)
                V.tensor_mul(g16[li2][0:hw, :], silu_t[0:hw, :],
                             P.PSF[0:hw, (bp + 2) * 512:(bp + 2) * 512 + 1024])
            ph_a_ve.append(P.vbody(vfn_g))

        ldb = []

        @P.block.gpsimd
        def _(g):
            g.wait_ge(ve.h, ph_a_ve[-1])
            for li2, ht in enumerate(grp):
                hw = min(128, HID - ht * 128)
                ldb.append(P.sol.issue(g, w2b[li2][0:hw, :],
                                       w2_q[ht * 128:ht * 128 + hw, :], 0))
        ldb_done = [(s, s.n) for s in P.sol.sems]

        ot_pe, ot_ve = [], []
        for ot in range(DT):
            bp = (ot % 2) * 2

            @P.block.tensor
            def _(t):
                if ot == 0:
                    for s_, vv in ldb_done:
                        t.wait_ge(s_.h, vv)
                    t.wait_ge(ve.h, ph_a_ve[-1])
                if ot >= 2:
                    t.wait_ge(ve.h, ot_ve[ot - 2])
                mm = None
                for li2, ht in enumerate(grp):
                    hw = min(128, HID - ht * 128)
                    for off in (0, 512):
                        mm = t.matmul(P.PSF[:, bp * 512 + off:bp * 512 + off + 512],
                                      w2b[li2][0:hw, ot * 128:(ot + 1) * 128],
                                      g16[li2][0:hw, off:off + 512],
                                      start=(li2 == 0), stop=(li2 == len(grp) - 1))
                mm.then_inc(pe.h, 1)
            ot_pe.append(pe.inc(1))

            def vfn_oa(V, ot=ot, gi=gi, bp=bp):
                V.wait_ge(pe.h, ot_pe[ot])
                if gi == 0:
                    V.tensor_copy(o_acc[ot][:], P.PSF[:, bp * 512:bp * 512 + 1024])
                else:
                    V.tensor_add(o_acc[ot][:], o_acc[ot][:],
                                 P.PSF[:, bp * 512:bp * 512 + 1024])
            ot_ve.append(P.vbody(vfn_oa))
        prev_gate[0] = ot_ve[-1]

    v_ffn = prev_gate[0]
    sc_ff.close()
    sc_mf.close()


    sc_cb = ExitStack()
    xb = [P.sbuf(f"xb{i}", [128, 1024], F32, sc_cb) for i in range(2)]
    gt = P.sbuf("gt", [128, 1024], BF16, sc_cb)
    tmpf = P.sbuf("tmpf", [128, 1024], F32, sc_cb)
    x1st = []
    cb_ve = []
    for i in range(DT):
        bi = i % 2
        xldi = []

        @P.block.gpsimd
        def _(g):
            if i >= 2:
                g.wait_ge(ve.h, cb_ve[i - 2])
                g.wait_ge(x1st[i - 2][0].h, x1st[i - 2][1])
            xldi.append(P.mld.issue(g, xb[bi][:], xT[i * 128:(i + 1) * 128, :], i))

        def sfn_cb(S_, i=i):
            if i == 0:
                S_.wait_ge(ve.h, v_ffn)
            else:
                S_.wait_ge(ve.h, cb_ve[i - 1])
            S_.activation(gt[:], o_acc[i][:], AF.Silu, scale=s23_vec[:, 0:1])
        v_gt = P.sbody(sfn_cb)

        def vfn_cb2(V, i=i, bi=bi, v_gt=v_gt, xldi=xldi):
            V.wait_ge(ac.h, v_gt)
            V.wait_ge(xldi[0][0].h, xldi[0][1])
            V.tensor_mul(tmpf[:], gt[:], sbr[i][:])
            V.tensor_add(xb[bi][:], xb[bi][:], tmpf[:])
        cb_ve.append(P.vbody(vfn_cb2))

        @P.block.gpsimd
        def _(g):
            g.wait_ge(ve.h, cb_ve[i])
            x1st.append(P.stx.issue(g, x1_out[i * 128:(i + 1) * 128, :], xb[bi][:], i))
    x1_totals = [(s, s.n) for s in P.stx.sems]
    sc_cb.close()

    sc_ln = ExitStack()
    alloc_pools(P, sc_ln)
    xn_ready, _hr, _st = emit_ln_h(P, C, sc_ln, x1_out, None, None, "ln1",
                                   xn_dram=xn1_out, entry=x1_totals + [(ve, cb_ve[-1])],
                                   with_h=False)
    sc_ln.close()
    sc_main.close()
    P.default_sc = None


def build_l0b():
    P = LB("l0b").start()
    io = {
        "xT": P.din("xT", [DM, TOK], F32).ap(),
        "xn0": P.din("xnT", [DM, TOK], BF16).ap(),
        "oloc": P.din("oloc_in", [CPC, 128, H * 128], BF16).ap(),
        "qdec": P.din("qdec_in", [CPC, 128, H * 128], BF16).ap(),
        "kvall": P.din("kvpre", [3, 128, H * 128], F32).ap(),
        "wfac_row": P.din("wfac_row", [1, 3], F32).ap(),
        "gap_row": P.din("gap_row", [1, 3], F32).ap(),
        "mq": P.din("mq_in", [1, 1], F32).ap(),
        "s_col": P.din("s_col", [H, 1], F32).ap(),
        "normw_in": P.din("normw_in", [DM, 1], F32).ap(),
        "pnormw_in": P.din("pnormw_in", [DM, 1], F32).ap(),
        "prj_ag": P.din("prj_ag", [DM, DM], BF16).ap(),
        "w1_ag": P.din("w1_ag", [DM, HID], BF16).ap(),
        "w3_ag": P.din("w3_ag", [DM, HID], BF16).ap(),
        "w2_ag": P.din("w2_ag", [8 * 683, DM], BF16).ap(),
        "mrow": P.din("mrow", [1, 8], F32).ap(),
        "x1": P.dout("x1_out", [DM, TOK], F32).ap(),
        "xn1": P.dout("xn1_out", [DM, TOK], BF16).ap(),
    }
    C = emit_consts(P)
    emit_l0b(P, C, io)

    @P.block.gpsimd
    def _(g):
        P.wait_all_streams(g)

    @P.block.sync
    def _(sy):
        P.wait_all_streams(sy)

    return P.finish()



def emit_l1q(P, C, io):
    P.tag = "q"
    sc_main = ExitStack()
    P.default_sc = sc_main
    xnT = io["xn1"]
    wattT = io["watt_ag"]
    pos_row = io["pos_row"]
    qkvpub = io["qkvpub"]

    pe, ac, ve, gp = (P.sem(s) for s in ("pe", "ac", "ve", "gp"))

    xn16 = [P.sbuf(f"xn16_{i}", [128, 1024], BF16) for i in range(DT)]
    Q16 = [P.sbuf(f"Q16_{h}", [128, 1024], BF16) for h in range(H)]
    K16 = [P.sbuf(f"K16_{h}", [128, 1024], BF16) for h in range(H)]
    V16 = [P.sbuf(f"V16_{c}", [128, 2048], BF16) for c in range(CPC)]
    wqt = [P.sbuf(f"wqt{i}", [128, 512], BF16) for i in range(2)]
    cos_t = P.sbuf("cos_t", [128, 1024], F32)
    sin_sgn = P.sbuf("sin_sgn", [128, 1024], F32)
    swapm16 = P.sbuf("swapm16", [128, 128], BF16)
    tmpA = P.sbuf("tmpA", [128, 1024], F32)
    tmpB = P.sbuf("tmpB", [128, 1024], F32)
    posr = P.sbuf("posr", [1, TOK], F32)

    @P.block.gpsimd
    def _(g):
        for i in range(DT):
            P.sxn.issue(g, xn16[i][:], xnT[i * 128:(i + 1) * 128, :])
        P.srp.issue(g, posr[:], pos_row)
    v_xnld = (P.sxn.sems[0], P.sxn.sems[0].n)
    v_pos = (P.srp.sems[0], P.srp.sems[0].n)

    rowscr = P.dram("rowscr", [1, 256], F32).ap()
    sc_r = ExitStack()
    idxrow = P.sbuf("idxrow", [1, 128], F32, sc_r)
    parrow = P.sbuf("parrow", [1, 128], F32, sc_r)
    i2 = P.sbuf("i2", [128, 1], F32, sc_r)
    theta = P.sbuf("theta", [128, 1], F32, sc_r)
    e_col = P.sbuf("e_col", [128, 1], F32, sc_r)
    ev_col = P.sbuf("ev_col", [128, 1], F32, sc_r)
    sgn_col = P.sbuf("sgn_col", [128, 1], F32, sc_r)
    ang = P.sbuf("ang", [128, 1024], F32, sc_r)
    zf = P.sbuf("zf", [128, 1024], F32, sc_r)
    ibig = P.sbuf("ibig", [128, 1024], mybir.dt.int32, sc_r)
    gtp = P.sbuf("gtp", [128, 1024], F32, sc_r)
    a1t = P.sbuf("a1t", [128, 128], F32, sc_r)
    a2t = P.sbuf("a2t", [128, 128], F32, sc_r)

    def gfn(g, G):
        G.iota(idxrow[:], pattern=[[1, 64], [0, 2]], base=0, channel_multiplier=0,
               allow_small_or_imprecise_dtypes=True)
        G.iota(parrow[:], pattern=[[0, 64], [1, 2]], base=0, channel_multiplier=0,
               allow_small_or_imprecise_dtypes=True)
    v_pcol = P.gbody(gfn)

    @P.block.gpsimd
    def _(g):
        g.wait_ge(gp.h, v_pcol)
        P.srr.issue(g, rowscr[0:1, 0:128], idxrow[:])
        P.srr.issue(g, rowscr[0:1, 128:256], parrow[:])
    v_rs1 = (P.srr.sems[0], P.srr.sems[0].n)

    @P.block.gpsimd
    def _(g):
        g.wait_ge(v_rs1[0].h, v_rs1[1])
        P.srr.issue(g, i2[:], rowscr[0:1, 0:128])
        P.srr.issue(g, e_col[:], rowscr[0:1, 128:256])
    v_rs2 = (P.srr.sems[0], P.srr.sems[0].n)

    def vfn_th(V):
        V.wait_ge(v_rs2[0].h, v_rs2[1])
        V.tensor_scalar(sgn_col[:], e_col[:], 2.0, 1.0, ALU.mult, ALU.subtract)
        V.tensor_scalar(ev_col[:], e_col[:], -1.0, 1.0, ALU.mult, ALU.add)
    v_th1 = P.vbody(vfn_th)

    def sfn_th(S_):
        S_.wait_ge(ve.h, v_th1)
        S_.activation(theta[:], i2[:], AF.Exp,
                      scale=-2.0 * math.log(10000.0) / HD)
    v_th2 = P.sbody(sfn_th)

    @P.block.tensor
    def _(t):
        t.wait_ge(v_pos[0].h, v_pos[1])
        mm = None
        for off in (0, 512):
            mm = t.matmul(P.PSF[:, off:off + 512], C["ones1r32"][0:1, :],
                          posr[0:1, off:off + 512], start=True, stop=True)
        mm.then_inc(pe.h, 1)
    v_pb = pe.inc(1)

    TWO_PI = 2.0 * math.pi

    def emit_trig(dst, phase_add, tag, prev_ac=None):
        def vfn(V):
            V.wait_ge(pe.h, v_pb)
            V.wait_ge(ac.h, v_th2)
            if prev_ac is not None:
                V.wait_ge(ac.h, prev_ac)
            V.tensor_single_scalar(ang[:], P.PSF[:, 0:1024], theta[:, 0:1], ALU.mult)
            if phase_add:
                V.tensor_single_scalar(ang[:], ang[:], phase_add, ALU.add)
            V.tensor_single_scalar(zf[:], ang[:], 1.0 / TWO_PI, ALU.mult)
            V.tensor_copy(ibig[:], zf[:])
            V.tensor_copy(zf[:], ibig[:])
            V.scalar_tensor_tensor(ang[:], zf[:], -TWO_PI, ang[:], ALU.mult, ALU.add)
            V.tensor_single_scalar(gtp[:], ang[:], math.pi, ALU.is_gt)
            V.scalar_tensor_tensor(ang[:], gtp[:], -TWO_PI, ang[:], ALU.mult, ALU.add)
        v1 = P.vbody(vfn)

        def sfn(S_):
            S_.wait_ge(ve.h, v1)
            S_.activation(dst[:], ang[:], AF.Sin)
        return P.sbody(sfn)

    v_sin = emit_trig(sin_sgn, 0.0, "s")
    v_cos = emit_trig(cos_t, math.pi / 2.0, "c", prev_ac=v_sin)

    def vfn_sg(V):
        V.wait_ge(ac.h, v_cos)
        V.tensor_single_scalar(sin_sgn[:], sin_sgn[:], sgn_col[:, 0:1], ALU.mult)
        V.tensor_single_scalar(a1t[:], C["iota_pf"][:], -1.0, ALU.is_equal)
        V.tensor_single_scalar(a2t[:], C["iota_pf"][:], 1.0, ALU.is_equal)
        V.tensor_single_scalar(a1t[:], a1t[:], ev_col[:, 0:1], ALU.mult)
        V.tensor_single_scalar(a2t[:], a2t[:], e_col[:, 0:1], ALU.mult)
        V.tensor_add(swapm16[:], a1t[:], a2t[:])
    v_rope_tabs = P.vbody(vfn_sg)
    sc_r.close()

    li = [0]
    ldv = {}
    mm_pe = {}

    def wload(base_f, kt):
        i = li[0]
        bi = i % 2

        @P.block.gpsimd
        def _(g):
            if i == 0:
                g.wait_ge(v_xnld[0].h, v_xnld[1])
                g.wait_ge(ve.h, v_rope_tabs)
            if i >= 2:
                g.wait_ge(pe.h, mm_pe[i - 2])
            ldv[i] = P.wlg.issue(g, wqt[bi][:],
                                 wattT[kt * 128:(kt + 1) * 128, base_f:base_f + 512], i)
        return i, bi

    evac = [0]
    for vg in range(4):
        base_f = 2 * DM + vg * 512
        for kt in range(DT):
            i, bi = wload(base_f, kt)

            @P.block.tensor
            def _(t):
                t.wait_ge(ldv[i][0].h, ldv[i][1])
                if kt == 0 and evac[0]:
                    t.wait_ge(ve.h, evac[0])
                mm = None
                for c in range(CPC):
                    mm = t.matmul(P.bank(c), xn16[kt][:, c * 128:(c + 1) * 128],
                                  wqt[bi][:, 0:512],
                                  start=(kt == 0), stop=(kt == DT - 1))
                mm.then_inc(pe.h, 1)
            mm_pe[i] = pe.inc(1)
            li[0] += 1
        v_vg = pe.n

        def vfn_vev(V, vg=vg, v_vg=v_vg):
            V.wait_ge(pe.h, v_vg)
            for c in range(CPC):
                V.tensor_copy(V16[c][:, vg * 512:(vg + 1) * 512], P.bank(c))
        evac[0] = P.vbody(vfn_vev)
    v_vdone = evac[0]

    @P.block.gpsimd
    def _(g):
        g.wait_ge(ve.h, v_vdone)
        for c in range(CPC):
            for hh in range(H):
                P.svo.issue(g,
                            qkvpub[4096 + hh * 128:4096 + (hh + 1) * 128,
                                   c * 128:(c + 1) * 128],
                            V16[c][:, hh * 128:(hh + 1) * 128], c)

    if "qkvall" in io:
        cc = P.sem("cc")
        _vg = [(s.h, s.n) for s in P.svo.sems if s.n]

        @P.block.gpsimd
        def _(g, _vg=_vg):
            for h_, v_ in _vg:
                g.wait_ge(h_, v_)
            g.collective_compute(
                "AllGather", ALU.bypass, replica_groups=io["cc_groups"],
                ins=[qkvpub[4096:6144, :]],
                outs=[io["qkvall"][0:8192, :]]).then_inc(cc.h, 1)
        cc.inc(1)

    st_prev = {}
    rope_ve = []
    it = [0]
    for grp4 in range(4):
        for role in range(2):
            base_f = role * DM + grp4 * 512
            for half in (0, 1):
                for kt in range(DT):
                    i, bi = wload(base_f, kt)

                    @P.block.tensor
                    def _(t, half=half):
                        t.wait_ge(ldv[i][0].h, ldv[i][1])
                        if kt == 0 and evac[0]:
                            t.wait_ge(ve.h, evac[0])
                        mm = None
                        for k in range(4):
                            mm = t.matmul(
                                P.PSF[:, k * 512:(k + 1) * 512],
                                wqt[bi][:, k * 128:(k + 1) * 128],
                                xn16[kt][:, half * 512:half * 512 + 512],
                                start=(kt == 0), stop=(kt == DT - 1))
                        mm.then_inc(pe.h, 1)
                    mm_pe[i] = pe.inc(1)
                    li[0] += 1
                v_g = pe.n

                def vfn_ev(V, grp4=grp4, role=role, half=half, v_g=v_g):
                    V.wait_ge(pe.h, v_g)
                    for k in range(4):
                        hh = grp4 * 4 + k
                        dst = Q16[hh] if role == 0 else K16[hh]
                        V.tensor_copy(dst[:, half * 512:half * 512 + 512],
                                      P.PSF[:, k * 512:(k + 1) * 512])
                evac[0] = P.vbody(vfn_ev)
            v_grp_ev = evac[0]

            for k4 in range(4):
                hh = grp4 * 4 + k4
                i = it[0]
                bp = 4 + (i % 2) * 2
                src_t = Q16[hh] if role == 0 else K16[hh]
                dst_rows = (0 if role == 0 else 2048) + hh * 128

                @P.block.tensor
                def _(t, i=i, k4=k4, v_grp_ev=v_grp_ev, bp=bp, src_t=src_t):
                    if i == 0:
                        t.wait_ge(ve.h, v_rope_tabs)
                    if k4 == 0:
                        t.wait_ge(ve.h, v_grp_ev)
                    if i >= 2:
                        t.wait_ge(ve.h, rope_ve[i - 2])
                    mm = None
                    for off in (0, 512):
                        mm = t.matmul(P.PSF[:, bp * 512 + off:bp * 512 + off + 512],
                                      swapm16[:], src_t[:, off:off + 512],
                                      start=True, stop=True)
                    mm.then_inc(pe.h, 1)
                v_sw = pe.inc(1)

                def vfn_r(V, i=i, bp=bp, src_t=src_t, v_sw=v_sw):
                    V.wait_ge(pe.h, v_sw)
                    if i >= 2 and (i - 2) in st_prev:
                        V.wait_ge(st_prev[i - 2][0].h, st_prev[i - 2][1])
                    V.tensor_mul(tmpA[:], P.PSF[:, bp * 512:bp * 512 + 1024],
                                 sin_sgn[:])
                    V.tensor_mul(tmpB[:], src_t[:], cos_t[:])
                    V.tensor_add(src_t[:], tmpA[:], tmpB[:])
                rope_ve.append(P.vbody(vfn_r))

                @P.block.gpsimd
                def _(g, i=i, dst_rows=dst_rows, src_t=src_t):
                    g.wait_ge(ve.h, rope_ve[i])
                    st_prev[i] = P.sqo.issue(g, qkvpub[dst_rows:dst_rows + 128, :],
                                             src_t[:], i)
                it[0] += 1

                if k4 == 3 and "qkvall" in io:
                    cc = P.sem("cc")
                    _pg = [(s.h, s.n) for s in P.sqo.sems if s.n]
                    src_r = (0 if role == 0 else 2048) + grp4 * 512
                    dst_r = (8192 if role == 0 else 16384) + grp4 * 2048

                    @P.block.gpsimd
                    def _(g, _pg=_pg, src_r=src_r, dst_r=dst_r):
                        for h_, v_ in _pg:
                            g.wait_ge(h_, v_)
                        g.collective_compute(
                            "AllGather", ALU.bypass,
                            replica_groups=io["cc_groups"],
                            ins=[qkvpub[src_r:src_r + 512, :]],
                            outs=[io["qkvall"][dst_r:dst_r + 2048, :]]).then_inc(
                                cc.h, 1)
                    cc.inc(1)

    sc_main.close()
    P.default_sc = None


def build_l1q():
    P = LB("l1q").start()
    io = {
        "xn1": P.din("xnT", [DM, TOK], BF16).ap(),
        "watt_ag": P.din("wattT", [DM, 3 * DM], BF16).ap(),
        "pos_row": P.din("pos_row", [1, TOK], F32).ap(),
        "qkvpub": P.dout("qkvpub", [3 * DM, TOK], BF16).ap(),
    }
    C = emit_consts(P)
    emit_l1q(P, C, io)

    @P.block.gpsimd
    def _(g):
        P.wait_all_streams(g)

    @P.block.sync
    def _(sy):
        P.wait_all_streams(sy)

    return P.finish()



def emit_l1a(P, C, io):
    P.tag = "t"
    sc_main = ExitStack()
    P.default_sc = sc_main
    qkvall = io["qkvall"]
    idx_in = io["idx_l1a"]
    ypub = io["ypub"]

    pe, ac, ve, gp = (P.sem(s) for s in ("pe", "ac", "ve", "gp"))
    idxsb = P.sbuf("idxsb", [128, 48], mybir.dt.int32)

    @P.block.gpsimd
    def _(g):
        P.srr.issue(g, idxsb[:], idx_in)
    v_idx = (P.srr.sems[0], P.srr.sems[0].n)

    qh = [P.sbuf(f"qh{i}", [128, N], BF16) for i in range(2)]
    kh = [P.sbuf(f"kh{i}", [128, N], BF16) for i in range(2)]
    vh = [P.sbuf(f"vh{i}", [128, NCH * 128], BF16) for i in range(2)]
    pstr = [P.sbuf(f"pstr{i}", [128, 1024], BF16) for i in range(2)]
    zc16 = P.sbuf("zc16", [128, 128], BF16)
    rs_row = P.sbuf("rs_row", [1, 1024], F32)
    rb = P.sbuf("rb", [128, 1024], F32)
    ytile = [P.sbuf(f"ytile{i}", [128, 1024], BF16) for i in range(2)]

    def gfn_z(g, G):
        G.memset(zc16[:], 0.0)
    v_z = P.gbody(gfn_z)

    @P.block.tensor
    def _(t):
        t.wait_ge(P.sem("gp").h, v_z)

    bh_done = []
    ld_bh = []
    y_stores = {}

    for bh in range(4):
        p = bh % 2

        @P.block.gpsimd
        def _(g):
            if bh == 0:
                g.wait_ge(v_idx[0].h, v_idx[1])
            if bh >= 2:
                g.wait_ge(ve.h, bh_done[bh - 2])
            l1 = l2 = l3 = None
            for j2 in range(4):
                cidx = bh * 12 + 0 * 4 + j2
                g.indirect_dma_start(
                    qh[p][:, j2 * 1024:(j2 + 1) * 1024], None, qkvall,
                    IndirectOffsetOnAxis(idxsb[:, cidx:cidx + 1], 0)
                ).then_inc(P.lq.sems[bh % 2].h, 16)
                l1 = (P.lq.sems[bh % 2], P.lq.sems[bh % 2].inc(16))
                cidx = bh * 12 + 1 * 4 + j2
                g.indirect_dma_start(
                    kh[p][:, j2 * 1024:(j2 + 1) * 1024], None, qkvall,
                    IndirectOffsetOnAxis(idxsb[:, cidx:cidx + 1], 0)
                ).then_inc(P.lk.sems[bh % 2].h, 16)
                l2 = (P.lk.sems[bh % 2], P.lk.sems[bh % 2].inc(16))
                cidx = bh * 12 + 2 * 4 + j2
                g.indirect_dma_start(
                    vh[p][:, j2 * 1024:(j2 + 1) * 1024], None, qkvall,
                    IndirectOffsetOnAxis(idxsb[:, cidx:cidx + 1], 0)
                ).then_inc(P.lv.sems[bh % 2].h, 16)
                l3 = (P.lv.sems[bh % 2], P.lv.sems[bh % 2].inc(16))
            ld_bh.append((l1, l2, l3))

        for tblk in range(4):
            nkc = 8 * tblk + 8
            kc_pe, kc_ac, kc_ve = [], [], []
            for kc in range(nkc):
                sb = (kc % 2) * 2
                swid = 1024 if kc < 8 * tblk else 1024 - 128 * (kc - 8 * tblk)
                toff = 1024 * tblk + (1024 - swid)
                loff = 1024 - swid
                diag = kc >= 8 * tblk

                @P.block.tensor
                def _(t):
                    if kc == 0:
                        l1, l2, l3 = ld_bh[bh]
                        t.wait_ge(l1[0].h, l1[1])
                        t.wait_ge(l2[0].h, l2[1])
                        t.wait_ge(l3[0].h, l3[1])
                        if tblk == 0 and bh_done:
                            t.wait_ge(ve.h, bh_done[-1])
                        if tblk > 0:
                            t.wait_ge(ve.h, tblk_gate[0])
                    if kc >= 2:
                        t.wait_ge(ac.h, kc_ac[kc - 2])
                        if (kc - 2) < len(kc_ve) and kc_ve[kc - 2]:
                            t.wait_ge(ve.h, kc_ve[kc - 2])
                        else:
                            t.wait_ge(pe.h, kc_pe[kc - 2] + 2)
                    mm = None
                    for off in range(0, swid, 512):
                        w = min(512, swid - off)
                        mm = t.matmul(P.PSF[:, sb * 512 + off:sb * 512 + off + w],
                                      kh[p][:, kc * 128:(kc + 1) * 128],
                                      qh[p][:, toff + off:toff + off + w],
                                      start=True, stop=True)
                    mm.then_inc(pe.h, 1)
                kc_pe.append(pe.inc(1))

                def sfn_e(S_, kc=kc, sb=sb, swid=swid, p=p):
                    S_.wait_ge(pe.h, kc_pe[kc])
                    S_.activation(pstr[kc % 2][:, 0:swid],
                                  P.PSF[:, sb * 512:sb * 512 + swid],
                                  AF.Exp, scale=INV_SQRT_HD)
                kc_ac.append(P.sbody(sfn_e))

                if diag:
                    def vfn_d(V, kc=kc, p=p):
                        V.wait_ge(ac.h, kc_ac[kc])
                        V.tensor_mul(pstr[kc % 2][:, 0:128], pstr[kc % 2][:, 0:128],
                                     C["trilT"][:])
                    kc_ve.append(P.vbody(vfn_d))
                else:
                    kc_ve.append(0)

                @P.block.tensor
                def _(t):
                    if kc_ve[kc]:
                        t.wait_ge(ve.h, kc_ve[kc])
                    else:
                        t.wait_ge(ac.h, kc_ac[kc])
                    mm = None
                    lo = loff
                    while lo < 1024:
                        w = min(512 - (lo % 512), 1024 - lo)
                        t.matmul(P.PSF[0:1, 3072 + lo:3072 + lo + w],
                                 C["ones16"][:, 0:1],
                                 pstr[kc % 2][:, lo - loff:lo - loff + w],
                                 start=(kc == 0), stop=False)
                        mm = t.matmul(
                            P.PSF[:, 2048 + lo:2048 + lo + w],
                            vh[p][:, kc * 128:(kc + 1) * 128],
                            pstr[kc % 2][:, lo - loff:lo - loff + w],
                            start=(kc == 0), stop=False)
                        lo += w
                    mm.then_inc(pe.h, 2)
                pe.inc(2)

            @P.block.tensor
            def _(t):
                mm = None
                for off in (0, 512):
                    t.matmul(P.PSF[0:1, 3072 + off:3072 + off + 512],
                             zc16[:, 0:1], pstr[0][:, off:off + 512],
                             start=False, stop=True)
                    mm = t.matmul(P.PSF[:, 2048 + off:2048 + off + 512],
                                  zc16[:], pstr[0][:, off:off + 512],
                                  start=False, stop=True)
                mm.then_inc(pe.h, 1)
            pe.inc(1)
            v_last = pe.n

            def vfn_rs(V, v_last=v_last):
                V.wait_ge(pe.h, v_last)
                V.tensor_copy(rs_row[:], P.PSF[0:1, 3072:4096])
            v_rs = P.vbody(vfn_rs)

            def sfn_rs(S_):
                S_.wait_ge(ve.h, v_rs)
                S_.activation(rs_row[:], rs_row[:], AF.Ln)
                S_.activation(rs_row[:], rs_row[:], AF.Exp, scale=-1.0)
            v_rc = P.sbody(sfn_rs)

            @P.block.tensor
            def _(t):
                t.wait_ge(ac.h, v_rc)
                mm = None
                for off in (0, 512):
                    mm = t.matmul(P.PSF[:, off:off + 512], C["ones1r32"][0:1, :],
                                  rs_row[0:1, off:off + 512], start=True, stop=True)
                mm.then_inc(pe.h, 1)
            v_rb = pe.inc(1)

            yst = {}

            def vfn_y(V, tblk=tblk, p=p, v_rb=v_rb, bh=bh):
                V.wait_ge(pe.h, v_rb)
                key = tblk % 2
                if key in y_stores:
                    V.wait_ge(y_stores[key][0].h, y_stores[key][1])
                V.tensor_copy(rb[:], P.PSF[:, 0:1024])
                V.tensor_mul(ytile[tblk % 2][:], P.PSF[:, 2048:3072], rb[:])
            v_y = P.vbody(vfn_y)
            tblk_gate = [v_y]

            @P.block.gpsimd
            def _(g):
                g.wait_ge(ve.h, v_y)
                rr = (bh * 4 + tblk) * 128
                y_stores[tblk % 2] = P.sy.issue(
                    g, ypub[rr:rr + 128, :], ytile[tblk % 2][:], tblk)
        bh_done.append(tblk_gate[0])

        if "ypc" in io:
            cc = P.sem("cc")
            st_gates = [(s.h, s.n) for s in P.sy.sems if s.n]

            @P.block.gpsimd
            def _(g, st_gates=st_gates, bh=bh):
                for h_, v_ in st_gates:
                    g.wait_ge(h_, v_)
                g.collective_compute(
                    "AllGather", ALU.bypass, replica_groups=io["cc_groups"],
                    ins=[ypub[bh * 512:(bh + 1) * 512, :]],
                    outs=[io["ypc"][bh]]).then_inc(cc.h, 1)
            cc.inc(1)

    sc_main.close()
    P.default_sc = None


def build_l1a():
    P = LB("l1a").start()
    io = {
        "qkvall": P.din("qkvall", [4 * 3 * DM, TOK], BF16).ap(),
        "idx_l1a": P.din("idx_l1a", [128, 48], mybir.dt.int32).ap(),
        "ypub": P.dout("ypub", [2048, 1024], BF16).ap(),
    }
    C = emit_consts(P)
    emit_l1a(P, C, io)

    @P.block.gpsimd
    def _(g):
        P.wait_all_streams(g)

    @P.block.sync
    def _(sy_):
        P.wait_all_streams(sy_)

    return P.finish()



def emit_l1f(P, C, io):
    P.tag = "f"
    sc_main = ExitStack()
    P.default_sc = sc_main
    xn1T = io["xn1"]
    ypc = io["ypc"]
    idx_y = io["idx_y"]
    x1T = io["x1"]
    wprojT = io["wp1_ag"]
    wffnT = io["wf_ag"]
    bvec = io["bvec"]
    AT = io["AT"]
    BT = io["BT"]
    gwT = io["gwT"]
    gb_row = io["gb_row"]
    x2_out = io["x2"]

    pe, ac, ve, gp = (P.sem(s) for s in ("pe", "ac", "ve", "gp"))
    idxy = P.sbuf("idxy", [128, 16], mybir.dt.int32)

    xn16 = [P.sbuf(f"xn16_{i}", [128, 1024], BF16) for i in range(DT)]
    y16 = [P.sbuf(f"y16_{i}", [128, 1024], BF16) for i in range(DT)]
    sbr = [P.sbuf(f"sbr_{i}", [128, 1024], BF16) for i in range(DT)]
    nb = P.sbuf("nb", [128, DT], F32)
    BT16 = P.sbuf("BT16", [128, DM], BF16)
    xA16 = P.sbuf("xA16", [128, 1024], BF16)
    xAw = P.sbuf("xAw", [128, 1024], BF16)
    wqt = [P.sbuf(f"wqt{i}", [128, 512], BF16) for i in range(2)]

    sc_g = ExitStack()
    gw16 = [P.sbuf(f"gw16_{k}", [128, E], BF16, sc_g) for k in range(DT)]
    at16 = [P.sbuf(f"at16_{k}", [128, E * RLO], BF16, sc_g) for k in range(2)]
    lg = P.sbuf("lg", [128, 64], F32, sc_g)
    gb_b = P.sbuf("gb_b", [128, E], F32, sc_g)
    gbr = P.sbuf("gbr", [1, E], F32, sc_g)
    m1c = P.sbuf("m1c", [128, 8], F32, sc_g)
    m2c = P.sbuf("m2c", [128, 8], F32, sc_g)
    eq1 = P.sbuf("eq1", [128, E], F32, sc_g)
    eq2 = P.sbuf("eq2", [128, E], F32, sc_g)
    mskd = P.sbuf("mskd", [128, E], F32, sc_g)
    g1s = P.sbuf("g1s", [128, 8], F32, sc_g)
    g2s = P.sbuf("g2s", [128, 8], F32, sc_g)
    wd = P.sbuf("wd", [128, 64], F32, sc_g)
    wfm = P.sbuf("wfm", [8, 1024], F32, sc_g)
    ident32 = P.sbuf("ident32", [128, 128], F32, sc_g)
    expd = P.sbuf("expd", [8, 128], F32, sc_g)
    exp2 = P.sbuf("exp2", [8, 128], F32, sc_g)

    @P.block.gpsimd
    def _(g):
        _iv = P.srr.issue(g, idxy[:], idx_y)
        g.wait_ge(_iv[0].h, _iv[1])
        for i in range(DT):
            P.sxn.issue(g, xn16[i][:], xn1T[i * 128:(i + 1) * 128, :])
        for i in range(H):
            g.indirect_dma_start(
                y16[i][:], None, ypc[i % 4],
                IndirectOffsetOnAxis(idxy[:, i:i + 1], 0)
            ).then_inc(P.syl.sems[0].h, 16)
            P.syl.sems[0].inc(16)
        for i in range(DT):
            P.sml.issue(g, nb[:, i:i + 1], bvec[i * 128:(i + 1) * 128, 0:1])
        P.sml.issue(g, BT16[:], BT)
        P.sml.issue(g, gbr[:], gb_row)
        for k in range(DT):
            P.sml.issue(g, gw16[k][:], gwT[k * 128:(k + 1) * 128, :])
    v_xn = (P.sxn.sems[0], P.sxn.sems[0].n)
    v_y = (P.syl.sems[0], P.syl.sems[0].n)
    v_ml = (P.sml.sems[0], P.sml.sems[0].n)
    v_gl = v_ml

    def gfn_e(g, G):
        G.iota(expd[:], pattern=[[1, 128]], base=0, channel_multiplier=-16,
               allow_small_or_imprecise_dtypes=True)
    v_ei = P.gbody(gfn_e)

    def vfn_e(V):
        V.wait_ge(gp.h, v_ei)
        V.tensor_single_scalar(ident32[:], C["iota_pf"][:], 0.0, ALU.is_equal)
        V.tensor_single_scalar(exp2[:], expd[:], 0.0, ALU.is_ge)
        V.tensor_single_scalar(expd[:], expd[:], 16.0, ALU.is_lt)
        V.tensor_mul(expd[:], expd[:], exp2[:])
    v_em = P.vbody(vfn_e)

    @P.block.tensor
    def _(t):
        t.wait_ge(v_gl[0].h, v_gl[1])
        t.wait_ge(v_xn[0].h, v_xn[1])
        mm = None
        for kt in range(DT):
            for tt in range(8):
                mm = t.matmul(P.PSF[:, tt * 512:tt * 512 + E],
                              xn16[kt][:, tt * 128:(tt + 1) * 128], gw16[kt][:],
                              start=(kt == 0), stop=(kt == DT - 1))
        mm.then_inc(pe.h, 1)
    v_lg = pe.inc(1)

    def vfn_lgc(V):
        V.wait_ge(pe.h, v_lg)
        for tt in range(8):
            V.tensor_copy(lg[:, tt * 8:(tt + 1) * 8], P.PSF[:, tt * 512:tt * 512 + E])
    v_lgc = P.vbody(vfn_lgc)

    @P.block.tensor
    def _(t):
        t.wait_ge(ve.h, v_lgc)
        t.matmul(P.bankc(7, E), C["ones1r32"][0:1, :], gbr[0:1, 0:E],
                 start=True, stop=True).then_inc(pe.h, 1)
    v_gbb = pe.inc(1)

    def vfn_lg(V):
        V.wait_ge(pe.h, v_gbb)
        V.tensor_copy(gb_b[:], P.bankc(7, E))
        for tt in range(8):
            sl = slice(tt * 8, (tt + 1) * 8)
            V.tensor_add(lg[:, sl], lg[:, sl], gb_b[:])
            V.tensor_reduce(m1c[:, tt:tt + 1], lg[:, sl], AX, ALU.max)
            V.tensor_single_scalar(eq1[:], lg[:, sl], m1c[:, tt:tt + 1], ALU.is_equal)
            V.scalar_tensor_tensor(mskd[:], eq1[:], -1e30, lg[:, sl],
                                   ALU.mult, ALU.add)
            V.tensor_reduce(m2c[:, tt:tt + 1], mskd[:], AX, ALU.max)
            V.tensor_single_scalar(eq2[:], mskd[:], m2c[:, tt:tt + 1], ALU.is_equal)
            V.tensor_copy(wd[:, sl], eq1[:])
            V.tensor_copy(mskd[:], eq2[:])
            V.tensor_copy(eq2[:], mskd[:])
            V.tensor_copy(lg[:, sl], eq2[:])
        V.tensor_sub(m1c[:], m1c[:], m2c[:])
    v_top = P.vbody(vfn_lg)

    def sfn_sig(S_):
        S_.wait_ge(ve.h, v_top)
        S_.activation(g1s[:], m1c[:], AF.Sigmoid)
    v_sig = P.sbody(sfn_sig)

    def vfn_wd(V):
        V.wait_ge(ac.h, v_sig)
        V.tensor_scalar(g2s[:], g1s[:], -1.0, 1.0, ALU.mult, ALU.add)
        for tt in range(8):
            sl = slice(tt * 8, (tt + 1) * 8)
            V.tensor_single_scalar(wd[:, sl], wd[:, sl], g1s[:, tt:tt + 1], ALU.mult)
            V.tensor_single_scalar(lg[:, sl], lg[:, sl], g2s[:, tt:tt + 1], ALU.mult)
            V.tensor_add(wd[:, sl], wd[:, sl], lg[:, sl])
    v_wd = P.vbody(vfn_wd)

    tr_pe = []
    for tt in range(8):
        @P.block.tensor
        def _(t):
            if tt == 0:
                t.wait_ge(ve.h, v_wd)
                t.wait_ge(ve.h, v_em)
            t.matmul(P.PSF[0:8, 5 * 512 + tt * 128:5 * 512 + (tt + 1) * 128],
                     wd[:, tt * 8:(tt + 1) * 8], ident32[:],
                     start=True, stop=True).then_inc(pe.h, 1)
        tr_pe.append(pe.inc(1))

    def vfn_wf(V):
        V.wait_ge(pe.h, tr_pe[-1])
        for tt in range(8):
            V.tensor_copy(wfm[:, tt * 128:(tt + 1) * 128],
                          P.PSF[0:8, 5 * 512 + tt * 128:5 * 512 + (tt + 1) * 128])
    v_wf = P.vbody(vfn_wf)

    xal = []
    xa_pe = []
    for kt in range(DT):
        @P.block.gpsimd
        def _(g):
            if kt == 0:
                g.wait_ge(v_xn[0].h, v_xn[1])
            if kt >= 2:
                g.wait_ge(pe.h, xa_pe[kt - 2])
            xal.append(P.sxl.issue(g, at16[kt % 2][:],
                                   AT[kt * 128:(kt + 1) * 128, :], kt))

        @P.block.tensor
        def _(t):
            t.wait_ge(xal[kt][0].h, xal[kt][1])
            if kt == 0:
                t.wait_ge(ve.h, v_lgc)
            mm = None
            for off in (0, 512):
                mm = t.matmul(P.PSF[:, off:off + 512],
                              at16[kt % 2][:], xn16[kt][:, off:off + 512],
                              start=(kt == 0), stop=(kt == DT - 1))
            mm.then_inc(pe.h, 1)
        xa_pe.append(pe.inc(1))

    def vfn_xa(V):
        V.wait_ge(pe.h, xa_pe[-1])
        V.tensor_copy(xA16[:], P.PSF[:, 0:1024])
    v_xa = P.vbody(vfn_xa)

    @P.block.tensor
    def _(t):
        t.wait_ge(ve.h, v_wf)
        t.wait_ge(ve.h, v_xa)
        mm = None
        for off in (0, 512):
            mm = t.matmul(P.PSF[:, 1024 + off:1024 + off + 512], expd[0:8, :],
                          wfm[0:8, off:off + 512], start=True, stop=True)
        mm.then_inc(pe.h, 1)
    v_we = pe.inc(1)

    def vfn_xaw(V):
        V.wait_ge(pe.h, v_we)
        V.tensor_mul(xAw[:], xA16[:], P.PSF[:, 1024:2048])
    v_xaw = P.vbody(vfn_xaw)
    sc_g.close()

    li = [0]
    ldv = {}
    mm_pe = {}
    evac = [0]
    for og in range(4):
        for kt in range(DT):
            i = li[0]
            bi = i % 2

            @P.block.gpsimd
            def _(g):
                if i == 0:
                    g.wait_ge(v_y[0].h, v_y[1])
                    g.wait_ge(ve.h, v_xaw)
                if i >= 2:
                    g.wait_ge(pe.h, mm_pe[i - 2])
                ldv[i] = P.wlg.issue(g, wqt[bi][:],
                                     wprojT[kt * 128:(kt + 1) * 128,
                                            og * 512:(og + 1) * 512], i)

            @P.block.tensor
            def _(t):
                t.wait_ge(ldv[i][0].h, ldv[i][1])
                if kt == 0 and evac[0]:
                    t.wait_ge(ve.h, evac[0])
                if kt == 0 and og == 0:
                    t.wait_ge(ve.h, v_xaw)
                mm = None
                for k in range(4):
                    for off in (0, 512):
                        mm = t.matmul(P.PSF[:, k * 1024 + off:k * 1024 + off + 512],
                                      wqt[bi][:, k * 128:(k + 1) * 128],
                                      y16[kt][:, off:off + 512],
                                      start=(kt == 0), stop=(kt == DT - 1))
                mm.then_inc(pe.h, 1)
            mm_pe[i] = pe.inc(1)
            li[0] += 1
        v_g = pe.n

        def vfn_pev(V, og=og, v_g=v_g):
            V.wait_ge(pe.h, v_g)
            for k in range(4):
                V.tensor_copy(sbr[og * 4 + k][:], P.PSF[:, k * 1024:(k + 1) * 1024])
        evac[0] = P.vbody(vfn_pev)
    v_sbr = evac[0]

    sc_cb = ExitStack()
    xb = [P.sbuf(f"xb{i}", [128, 1024], F32, sc_cb) for i in range(2)]
    gt = [P.sbuf(f"gt{i}", [128, 1024], BF16, sc_cb) for i in range(2)]
    tmpf = P.sbuf("tmpf", [128, 1024], F32, sc_cb)
    x2st = []
    cb_ve = []
    ffn_pe = []
    li2 = [0]
    for og in range(4):
        for kt in range(DT):
            i = li2[0]
            bi = i % 2

            @P.block.gpsimd
            def _(g):
                if i == 0:
                    g.wait_ge(ve.h, v_sbr)
                if i >= 2:
                    g.wait_ge(pe.h, mm_pe[(64, i - 2)])
                ldv[(64, i)] = P.wlg.issue(g, wqt[bi][:],
                                           wffnT[kt * 128:(kt + 1) * 128,
                                                 og * 512:(og + 1) * 512], i)

            @P.block.tensor
            def _(t):
                t.wait_ge(ldv[(64, i)][0].h, ldv[(64, i)][1])
                if kt == 0 and og > 0:
                    t.wait_ge(ve.h, cb_ve[-1])
                mm = None
                for k in range(4):
                    for off in (0, 512):
                        mm = t.matmul(P.PSF[:, k * 1024 + off:k * 1024 + off + 512],
                                      wqt[bi][:, k * 128:(k + 1) * 128],
                                      xn16[kt][:, off:off + 512],
                                      start=(kt == 0), stop=False)
                mm.then_inc(pe.h, 1)
            mm_pe[(64, i)] = pe.inc(1)
            li2[0] += 1

        @P.block.tensor
        def _(t):
            mm = None
            for k in range(4):
                ot = og * 4 + k
                for off in (0, 512):
                    mm = t.matmul(P.PSF[:, k * 1024 + off:k * 1024 + off + 512],
                                  BT16[:, ot * 128:(ot + 1) * 128],
                                  xAw[:, off:off + 512],
                                  start=False, stop=True)
            mm.then_inc(pe.h, 1)
        v_lora = pe.inc(1)

        for k in range(4):
            ot = og * 4 + k
            bi = ot % 2

            def sfn_g(S_, ot=ot, k=k, bi=bi, v_lora=v_lora):
                S_.wait_ge(pe.h, v_lora)
                if ot >= 2:
                    S_.wait_ge(ve.h, cb_ve[ot - 2])
                S_.wait_ge(v_ml[0].h, v_ml[1])
                S_.activation(gt[bi][:], P.PSF[:, k * 1024:(k + 1) * 1024], AF.Silu,
                              bias=nb[:, ot:ot + 1], scale=1.0)
            v_gt = P.sbody(sfn_g)

            xldi = []

            @P.block.gpsimd
            def _(g):
                if ot >= 2:
                    g.wait_ge(ve.h, cb_ve[ot - 2])
                    g.wait_ge(x2st[ot - 2][0].h, x2st[ot - 2][1])
                xldi.append(P.sxl.issue(g, xb[bi][:],
                                        x1T[ot * 128:(ot + 1) * 128, :], ot))

            def vfn_c(V, ot=ot, bi=bi, v_gt=v_gt, xldi=xldi):
                V.wait_ge(ac.h, v_gt)
                V.wait_ge(xldi[0][0].h, xldi[0][1])
                V.tensor_mul(tmpf[:], gt[bi][:], sbr[ot][:])
                V.tensor_add(xb[bi][:], xb[bi][:], tmpf[:])
            cb_ve.append(P.vbody(vfn_c))

            @P.block.gpsimd
            def _(g):
                g.wait_ge(ve.h, cb_ve[ot])
                x2st.append(P.sxs.issue(g, x2_out[ot * 128:(ot + 1) * 128, :],
                                        xb[bi][:], ot))
    sc_cb.close()
    sc_main.close()
    P.default_sc = None


def build_l1f():
    P = LB("l1f").start()
    io = {
        "xn1": P.din("xn1T", [DM, TOK], BF16).ap(),
        "ypc": [P.din(f"ypc{b}", [4 * 512, 1024], BF16).ap() for b in range(4)],
        "idx_y": P.din("idx_y", [128, H], mybir.dt.int32).ap(),
        "x1": P.din("x1T", [DM, TOK], F32).ap(),
        "wp1_ag": P.din("wprojT", [DM, DM], BF16).ap(),
        "wf_ag": P.din("wffnT", [DM, DM], BF16).ap(),
        "bvec": P.din("bvec", [DM, 1], F32).ap(),
        "AT": P.din("AT", [DM, E * RLO], F32).ap(),
        "BT": P.din("BT", [E * RLO, DM], F32).ap(),
        "gwT": P.din("gwT", [DM, E], F32).ap(),
        "gb_row": P.din("gb_row", [1, E], F32).ap(),
        "x2": P.dout("x2_out", [DM, TOK], F32).ap(),
    }
    C = emit_consts(P)
    emit_l1f(P, C, io)

    @P.block.gpsimd
    def _(g):
        P.wait_all_streams(g)

    @P.block.sync
    def _(sy_):
        P.wait_all_streams(sy_)

    return P.finish()



def build_fused():
    P = LB("fused").start()
    io = {
        "xT": P.din("xT", [DM, TOK], F32).ap(),
        "qkv_sl": P.din("qkv_sl", [256, 6144], F32).ap(),
        "prj_sl": P.din("prj_sl", [256, 2048], F32).ap(),
        "w1_sl": P.din("w1_sl", [256, HID], F32).ap(),
        "w3_sl": P.din("w3_sl", [256, HID], F32).ap(),
        "w2_sl": P.din("w2_sl", [683, 2048], F32).ap(),
        "qkvnrm": P.din("qkvnrm", [DM, 1], F32).ap(),
        "s_col": P.din("s_col", [H, 1], F32).ap(),
        "s_row": P.din("s_row", [1, H], F32).ap(),
        "normw_in": P.din("normw_in", [DM, 1], F32).ap(),
        "pnormw_in": P.din("pnormw_in", [DM, 1], F32).ap(),
        "watt_sl": P.din("watt_sl", [256, 3 * DM], F32).ap(),
        "pos_row": P.din("pos_row", [1, TOK], F32).ap(),
        "wfac_row": P.din("wfac_row", [1, 3], F32).ap(),
        "gap_row": P.din("gap_row", [1, 3], F32).ap(),
        "wp1_sl": P.din("wp1_sl", [256, DM], F32).ap(),
        "wf_sl": P.din("wf_sl", [256, DM], F32).ap(),
        "bvec": P.din("bvec", [DM, 1], F32).ap(),
        "AT": P.din("AT", [DM, E * RLO], F32).ap(),
        "BT": P.din("BT", [E * RLO, DM], F32).ap(),
        "gwT": P.din("gwT", [DM, E], F32).ap(),
        "gb_row": P.din("gb_row", [1, E], F32).ap(),
        "idx_l1a": P.din("idx_l1a", [128, 48], mybir.dt.int32).ap(),
        "idx_y": P.din("idx_y", [128, H], mybir.dt.int32).ap(),
        "x2": P.dout("x2_out", [DM, TOK], F32).ap(),
    }
    P.tag = "z"
    for nm, R, Cc, _, _c in WPREP:
        io[nm + "_pub"] = P.dram(nm + "_pub", [R, Cc], BF16).ap()
        io[nm + "_ag"] = P.dram(nm + "_ag", [8 * R, Cc], BF16).ap()
    for nm, R, Cc in WCAST:
        io[nm + "_pub"] = P.dram(nm + "_pub", [R, Cc], BF16).ap()
        io[nm + "_ag"] = P.dram(nm + "_ag", [8 * R, Cc], BF16).ap()
    io["mrow"] = P.dram("mrow", [1, 8], F32).ap()
    io["xn0"] = P.dram("xn0", [DM, TOK], BF16).ap()
    io["oloc"] = P.dram("oloc", [CPC, 128, H * 128], BF16).ap()
    io["qdec"] = P.dram("qdec", [CPC, 128, H * 128], BF16).ap()
    io["kvagg"] = P.dram("kvagg", [128, H * 128], F32).ap()
    kvall_h = P.dram("kvall", [4, 128, H * 128], F32)
    io["kvall"] = kvall_h.ap()
    io["x1"] = P.dram("x1", [DM, TOK], F32).ap()
    io["xn1"] = P.dram("xn1", [DM, TOK], BF16).ap()
    io["qkvpub"] = P.dram("qkvpub", [3 * DM, TOK], BF16).ap()
    io["qkvall"] = P.dram("qkvall", [4 * 3 * DM, TOK], BF16).ap()
    io["ypub"] = P.dram("ypub", [2048, 1024], BF16).ap()
    io["ypc"] = [P.dram(f"ypc{b}", [4 * 512, 1024], BF16).ap() for b in range(4)]

    cc = P.sem("cc")
    GROUPS = [[0, 1, 2, 3], [4, 5, 6, 7]]

    C = emit_consts(P)
    wstate = {}

    def walloc_cb():
        wstate["w"] = alloc_wprep(P, P.default_sc)

    def wprep_cb():
        w = wstate["w"]
        tag0 = P.tag
        P.tag = "w"
        emit_wprep_cast(P, C, io, w, ["watt", "wp1", "wf"], stage=0)
        emit_wprep_means(P, C, io, w)
        g1 = emit_wprep_quant(P, C, io, w, ["qkv"])
        io["cc_qkv"] = g1["qkv"]
        emit_wprep_quant(P, C, io, w, ["prj", "w1", "w3", "w2"])
        emit_wprep_cast(P, C, io, w, ["watt", "wp1", "wf"], stage=1)
        P.tag = tag0
        return []

    emit_l0a(P, C, io, walloc_cb=walloc_cb, wprep_cb=wprep_cb)

    @P.block.gpsimd
    def _(g):
        g.wait_ge(P.mvec.sems[0].h, P.mvec.sems[0].n)
        g.collective_compute(
            "AllGather", ALU.bypass, replica_groups=GROUPS,
            ins=[io["kvagg"]],
            outs=[io["kvall"].rearrange("a b c -> (a b) c")]).then_inc(cc.h, 1)
    cc.inc(1)
    P.phase_barrier()

    emit_l0b(P, C, io)
    P.phase_barrier()

    io["cc_groups"] = GROUPS
    emit_l1q(P, C, io)
    P.phase_barrier()

    emit_l1a(P, C, io)
    P.phase_barrier()

    emit_l1f(P, C, io)

    @P.block.gpsimd
    def _(g):
        P.wait_all_streams(g)

    @P.block.sync
    def _(sy_):
        P.wait_all_streams(sy_)

    @P.block.vector
    def _(v):
        P.wait_all_streams(v)

    return P.finish()


def host_inputs(inp):
    x = _f32c(inp["L0_attn_x"])
    s_np = _f32c(inp["L0_attn_s"]).reshape(H, 1)
    lA = _f32c(inp["L1_ffn_lora_A"])
    lB = _f32c(inp["L1_ffn_lora_B"])
    wqkvT = np.ascontiguousarray(_f32c(inp["L0_attn_qkv_w"]).T)
    wprojT = np.ascontiguousarray(_f32c(inp["L0_attn_proj_w"]).T)
    w1T = np.ascontiguousarray(_f32c(inp["L0_ffn_w1"]).T)
    w3T = np.ascontiguousarray(_f32c(inp["L0_ffn_w3"]).T)
    w2T = np.ascontiguousarray(_f32c(inp["L0_ffn_w2"]).T)
    w2Tp = np.zeros((8 * 683, DM), np.float32)
    w2Tp[:HID] = w2T
    wattT = np.ascontiguousarray(_f32c(inp["L1_attn_w_atten"]).T)
    wp1T = np.ascontiguousarray(_f32c(inp["L1_attn_w_proj"]).T)
    wfT = np.ascontiguousarray(_f32c(inp["L1_ffn_W"]).T)
    shared = {
        "qkvnrm": _f32c(inp["L0_attn_qkv_norm_w"]).reshape(DM, 1),
        "s_col": s_np,
        "s_row": np.ascontiguousarray(s_np.reshape(1, H)),
        "normw_in": _f32c(inp["L0_attn_norm_w"]).reshape(DM, 1),
        "pnormw_in": _f32c(inp["L0_attn_proj_norm_w"]).reshape(DM, 1),

        "bvec": _f32c(inp["L1_ffn_b"]).reshape(DM, 1),
        "AT": np.ascontiguousarray(lA.reshape(E * RLO, DM).T),
        "BT": np.ascontiguousarray(np.transpose(lB, (0, 2, 1)).reshape(E * RLO, DM)),
        "gwT": np.ascontiguousarray(_f32c(inp["L1_ffn_gate_w"]).T),
        "gb_row": _f32c(inp["L1_ffn_gate_b"]).reshape(1, E),
    }
    maps = []
    pp = np.arange(128, dtype=np.int64)
    for c in range(NCORES):
        b, j = c // 4, c % 4
        m = dict(shared)
        m["xT"] = np.ascontiguousarray(x[b, j * TOK:(j + 1) * TOK, :].T)
        m["qkv_sl"] = np.ascontiguousarray(wqkvT[c * 256:(c + 1) * 256])
        m["prj_sl"] = np.ascontiguousarray(wprojT[c * 256:(c + 1) * 256])
        m["w1_sl"] = np.ascontiguousarray(w1T[c * 256:(c + 1) * 256])
        m["w3_sl"] = np.ascontiguousarray(w3T[c * 256:(c + 1) * 256])
        m["w2_sl"] = np.ascontiguousarray(w2Tp[c * 683:(c + 1) * 683])
        m["watt_sl"] = np.ascontiguousarray(wattT[c * 256:(c + 1) * 256])
        m["wp1_sl"] = np.ascontiguousarray(wp1T[c * 256:(c + 1) * 256])
        m["wf_sl"] = np.ascontiguousarray(wfT[c * 256:(c + 1) * 256])
        m["pos_row"] = np.arange(j * TOK, (j + 1) * TOK,
                                 dtype=np.float32).reshape(1, TOK)
        wf = np.zeros((1, 3), np.float32)
        gp_ = np.zeros((1, 3), np.float32)
        for p in range(j):
            wf[0, p] = 1.0
            gp_[0, p] = 8.0 * (j - p - 1)
        m["wfac_row"] = wf
        m["gap_row"] = gp_
        idx_a = np.zeros((128, 48), np.int32)
        for bh in range(4):
            hh = 4 * j + bh
            g_ = hh // 4
            for role in range(3):
                for j2 in range(4):
                    if role == 2:
                        base = j2 * 2048 + hh * 128
                    else:
                        base = (8192 if role == 0 else 16384) + \
                            g_ * 2048 + j2 * 512 + (hh % 4) * 128
                    idx_a[:, bh * 12 + role * 4 + j2] = base + pp
        m["idx_l1a"] = idx_a
        idx_y = np.zeros((128, H), np.int32)
        for hh in range(H):
            base = (hh // 4) * 512 + j * 128
            idx_y[:, hh] = base + pp
        m["idx_y"] = idx_y
        maps.append(m)
    return maps


def _f32c(a):
    return np.ascontiguousarray(np.asarray(a, dtype=np.float32))


_BUILD_CACHE = {}


def _get(name, fn):
    if name not in _BUILD_CACHE:
        _BUILD_CACHE[name] = fn()
    return _BUILD_CACHE[name]


def kernel(**inp):
    maps = host_inputs(inp)
    nc = _get("fused", build_fused)
    res = run_bass_kernel_spmd(nc, maps, core_ids=list(range(NCORES))).results
    out = np.zeros((B, N, DM), np.float32)
    for c in range(NCORES):
        b, j = c // 4, c % 4
        out[b, j * TOK:(j + 1) * TOK, :] = np.asarray(
            res[c]["x2_out"], np.float32).reshape(DM, TOK).T
    return out


# version 15
# speedup vs baseline: 1.0418x; 1.0418x over previous
import math
import numpy as np
import ml_dtypes
from contextlib import ExitStack

import concourse.bass as bass
import concourse.mybir as mybir
from concourse.bass_utils import run_bass_kernel_spmd
from concourse.bass_core_rust import IndirectOffsetOnAxis

F32 = mybir.dt.float32
BF16 = mybir.dt.bfloat16
AF = mybir.ActivationFunctionType
ALU = mybir.AluOpType
AX = mybir.AxisListType.X
BFNP = ml_dtypes.bfloat16

B, N, DM, H, HD, T = 2, 4096, 2048, 16, 128, 128
NCH = N // T
CPC = 8
TOK = CPC * T
DT = DM // 128
HID = 5461
E, RLO = 8, 16
EPS32 = float(np.finfo(np.float32).eps)
INV_SQRT_HD = 1.0 / math.sqrt(HD)
NCORES = 8


class SemC:
    def __init__(self, h):
        self.h = h
        self.n = 0

    def inc(self, k=1):
        self.n += k
        return self.n


class SEng:

    PASS = ("wait_ge",)

    def __init__(self, e, sem):
        self._e = e
        self._s = sem

    def __getattr__(self, name):
        tgt = getattr(self._e, name)
        if name in SEng.PASS or not callable(tgt):
            return tgt

        def call(*a, **k):
            if self._s.n:
                self._e.wait_ge(self._s.h, self._s.n)
            ins = tgt(*a, **k)
            ins.then_inc(self._s.h, 1)
            self._s.inc(1)
            return ins
        return call


class Stream:

    def __init__(self, P, name, nslots=2):
        self.sems = [SemC(P.ctx.enter_context(P.nc.semaphore(f"{name}{i}")))
                     for i in range(nslots)]
        self.k = nslots

    def issue(self, g, out, in_, i=0):
        s = self.sems[i % self.k]
        g.dma_start(out=out, in_=in_).then_inc(s.h, 16)
        return (s, s.inc(16))


class LB:
    def __init__(self, name):
        self.nc = bass.Bass("TRN2", target_bir_lowering=False)
        self.name = name
        self.ctx = ExitStack()
        self.sems = {}
        self.block = None

    def start(self):
        for s in ("pe", "ac", "ve", "gp", "cc"):
            self.sems[s] = SemC(self.ctx.enter_context(self.nc.semaphore(s)))
        self.block = self.ctx.enter_context(self.nc.Block())
        self.PSF = self.ctx.enter_context(self.nc.psum_tensor("PSF", [128, 4096], F32))
        self.streams = []
        self.tag = "x"
        self.default_sc = None
        for nm in ("mvec", "mld", "stq", "stx", "sqd", "sol", "sxn",
                   "sgw", "srp", "sqo", "svo", "lq", "lk", "lv", "sy",
                   "syl", "sxl", "sxs", "sml", "srr"):
            setattr(self, nm, self.stream(nm))
        self.mld4 = self.stream("mld4", 8)
        self.wl = self.stream("wl", 4)
        self.wlg = self.stream("wlg", 4)
        return self

    def phase_barrier(self, include_cc=True):
        engs = ("pe", "ac", "ve", "gp", "cc") if include_cc else ("pe", "ac", "ve", "gp")
        vals = [(self.sems[s].h, self.sems[s].n)
                for s in engs if self.sems[s].n]
        for st in self.streams:
            for s in st.sems:
                if s.n:
                    vals.append((s.h, s.n))

        for eng in ("tensor", "scalar", "sync", "gpsimd", "vector"):
            def fgate(e, vals=vals):
                for h, v in vals:
                    e.wait_ge(h, v)
            getattr(self.block, eng)(fgate)

    def stream(self, name, k=2):
        st = Stream(self, name, k)
        self.streams.append(st)
        return st

    def wait_all_streams(self, e):
        for st in self.streams:
            for s in st.sems:
                if s.n:
                    e.wait_ge(s.h, s.n)

    def bank(self, i, w=1, rows=128):
        return self.PSF[0:rows, i * 512:(i + w) * 512]

    def bankc(self, i, cols, rows=128):
        return self.PSF[0:rows, i * 512:i * 512 + cols]

    def sem(self, name):
        return self.sems[name]

    def sbuf(self, name, shape, dtype, sc=None):
        return (sc or self.default_sc or self.ctx).enter_context(
            self.nc.sbuf_tensor(f"{self.tag}_{name}", shape, dtype))

    def dram(self, name, shape, dtype):
        return self.nc.dram_tensor(f"{self.tag}_{name}", shape, dtype, kind="Internal")

    def din(self, name, shape, dtype):
        return self.nc.dram_tensor(name, shape, dtype, kind="ExternalInput")

    def dout(self, name, shape, dtype):
        return self.nc.dram_tensor(name, shape, dtype, kind="ExternalOutput")

    def vbody(self, fn):
        ve = self.sem("ve")

        def wrap(v):
            fn(SEng(v, ve))
        self.block.vector(wrap)
        return ve.n

    def sbody(self, fn):
        ac = self.sem("ac")

        def wrap(s):
            fn(SEng(s, ac))
        self.block.scalar(wrap)
        return ac.n

    def gbody(self, fn):
        gp = self.sem("gp")

        def wrap(g):
            fn(g, SEng(g, gp))
        self.block.gpsimd(wrap)
        return gp.n

    def finish(self):
        self.ctx.close()
        return self.nc


def tile_list(rows, cols, tr=128, tc=1024):
    out = []
    r = 0
    while r < rows:
        pr = min(tr, rows - r)
        c = 0
        while c < cols:
            pc = min(tc, cols - c)
            out.append((r, pr, c, pc))
            c += pc
        r += pr
    return out


def emit_consts(P):
    C = {}
    C["ones16"] = P.sbuf("c_ones16", [128, 1], BF16)
    C["ones1r32"] = P.sbuf("c_ones1r32", [1, 128], F32)
    C["ones1r16"] = P.sbuf("c_ones1r16", [1, 128], BF16)
    C["onescol32"] = P.sbuf("c_onescol32", [128, 1], F32)
    C["iota_pf"] = P.sbuf("c_iotapf", [128, 128], F32)
    C["trilT"] = P.sbuf("c_trilT", [128, 128], F32)
    C["ident16"] = P.sbuf("c_ident16", [128, 128], BF16)
    gp, ve = P.sem("gp"), P.sem("ve")

    def gfn(g, G):
        G.memset(C["ones16"][:], 1.0)
        G.memset(C["ones1r32"][:], 1.0)
        G.memset(C["ones1r16"][:], 1.0)
        G.memset(C["onescol32"][:], 1.0)
        G.iota(C["iota_pf"][:], pattern=[[-1, 128]], base=0, channel_multiplier=1,
               allow_small_or_imprecise_dtypes=True)
    v_gp = P.gbody(gfn)

    def vfn(V):
        V.wait_ge(gp.h, v_gp)
        V.tensor_single_scalar(C["trilT"][:], C["iota_pf"][:], 0.0, ALU.is_le)
        V.tensor_single_scalar(C["ident16"][:], C["iota_pf"][:], 0.0, ALU.is_equal)
    v_ve = P.vbody(vfn)

    for eng in ("tensor", "scalar", "sync", "gpsimd", "vector"):
        def fgate(e, v_gp=v_gp, v_ve=v_ve):
            e.wait_ge(gp.h, v_gp)
            e.wait_ge(ve.h, v_ve)
        getattr(P.block, eng)(fgate)
    return C


_POOL_N = [0]


def alloc_pools(P, sc):
    k = _POOL_N[0]
    _POOL_N[0] += 1
    P.wb32 = [P.sbuf(f"wb32_{k}_{i}", [128, 1024], F32, sc) for i in range(2)]
    P.ba16 = [P.sbuf(f"ba16_{k}_{i}", [128, 1024], BF16, sc) for i in range(2)]
    P.bb16 = [P.sbuf(f"bb16_{k}_{i}", [128, 1024], BF16, sc) for i in range(2)]



def emit_mean_pass(P, C, sc, wsrc, rows, cols, clip_lo, tag, mq_store=None,
                   entry=()):
    pe, ve = P.sem("pe"), P.sem("ve")
    acc = P.sbuf(f"{tag}_acc", [128, 1], F32, sc)
    red = P.sbuf(f"{tag}_red", [128, 1], F32, sc)
    m_sb = P.sbuf(f"{tag}_m", [1, 1], F32, sc)
    th_pos = P.sbuf(f"{tag}_thp", [128, 1], F32, sc)
    th_neg = P.sbuf(f"{tag}_thn", [128, 1], F32, sc)
    tiles = tile_list(rows, cols)
    nelem = rows * cols

    def vfn0(V):
        for es, evv in entry:
            V.wait_ge(es.h, evv)
        V.memset(acc[:], 0.0)
    P.vbody(vfn0)

    ld_v, ve_done = [], []
    for i, (r, pr, c, pc) in enumerate(tiles):
        bi = i % 2

        @P.block.gpsimd
        def _(g):
            if i >= 2:
                g.wait_ge(ve.h, ve_done[i - 2])
            else:
                for es, evv in entry:
                    g.wait_ge(es.h, evv)
            ld_v.append(P.mld.issue(g, P.wb32[bi][0:pr, 0:pc],
                                    wsrc[r:r + pr, c:c + pc], i))

        def vfn(V, i=i, bi=bi, pr=pr, pc=pc):
            V.wait_ge(ld_v[i][0].h, ld_v[i][1])
            V.tensor_reduce(red[0:pr, 0:1], P.wb32[bi][0:pr, 0:pc], AX, ALU.add,
                            apply_absolute_value=True)
            V.tensor_add(acc[0:pr, 0:1], acc[0:pr, 0:1], red[0:pr, 0:1])
        ve_done.append(P.vbody(vfn))

    @P.block.tensor
    def _(t):
        t.wait_ge(ve.h, ve_done[-1])
        t.matmul(P.bankc(7, 1, rows=1), acc[:, 0:1], C["onescol32"][:, 0:1],
                 start=True, stop=True).then_inc(pe.h, 1)
    v_pe = pe.inc(1)

    def vfn2(V):
        V.wait_ge(pe.h, v_pe)
        if clip_lo is None:
            V.tensor_single_scalar(m_sb[:], P.bankc(7, 1, rows=1), 1.0 / nelem, ALU.mult)
        else:
            V.tensor_scalar(m_sb[:], P.bankc(7, 1, rows=1), 1.0 / nelem, clip_lo,
                            ALU.mult, ALU.max)
    v_m = P.vbody(vfn2)

    @P.block.tensor
    def _(t):
        t.wait_ge(ve.h, v_m)
        t.matmul(P.bankc(7, 1), C["ones1r32"][0:1, :], m_sb[0:1, 0:1],
                 start=True, stop=True).then_inc(pe.h, 1)
    v_pe2 = pe.inc(1)

    def vfn3(V):
        V.wait_ge(pe.h, v_pe2)
        V.tensor_single_scalar(th_pos[:], P.bankc(7, 1), 0.5, ALU.mult)
        V.tensor_single_scalar(th_neg[:], th_pos[:], -1.0, ALU.mult)
    done = P.vbody(vfn3)

    if mq_store is not None:
        @P.block.gpsimd
        def _(g):
            g.wait_ge(ve.h, done)
            P.mvec.issue(g, mq_store, m_sb[:])
    return m_sb, th_pos, th_neg, done


def emit_quantize_pass(P, wsrc, wdst, rows, cols, th_pos, th_neg, th_ready, tag):
    ac, ve = P.sem("ac"), P.sem("ve")
    tiles = tile_list(rows, cols)
    ld_v, ac_done, ve_done, st_v = [], [], [], []
    pre_stq = [(s, s.n) for s in P.stq.sems]
    for i, (r, pr, c, pc) in enumerate(tiles):
        bi = i % 2

        @P.block.gpsimd
        def _(g):
            if i >= 2:
                g.wait_ge(ac.h, ac_done[i - 2])
            else:
                g.wait_ge(ve.h, th_ready)
            ld_v.append(P.mld.issue(g, P.wb32[bi][0:pr, 0:pc],
                                    wsrc[r:r + pr, c:c + pc], i))

        def sfn(S_, i=i, bi=bi, pr=pr, pc=pc):
            S_.wait_ge(ld_v[i][0].h, ld_v[i][1])
            if i == 0:
                S_.wait_ge(ve.h, th_ready)
            if i < 2:
                for s_, vv in pre_stq:
                    if vv:
                        S_.wait_ge(s_.h, vv)
            if i >= 2:
                S_.wait_ge(st_v[i - 2][0].h, st_v[i - 2][1])
            S_.activation(P.ba16[bi][0:pr, 0:pc], P.wb32[bi][0:pr, 0:pc], AF.Sign,
                          bias=th_neg[0:pr, 0:1])
            S_.activation(P.bb16[bi][0:pr, 0:pc], P.wb32[bi][0:pr, 0:pc], AF.Sign,
                          bias=th_pos[0:pr, 0:1])
        ac_done.append(P.sbody(sfn))

        def vfn(V, i=i, bi=bi, pr=pr, pc=pc):
            V.wait_ge(ac.h, ac_done[i])
            if i >= 2:
                V.wait_ge(st_v[i - 2][0].h, st_v[i - 2][1])
            V.tensor_add(P.bb16[bi][0:pr, 0:pc], P.bb16[bi][0:pr, 0:pc],
                         P.ba16[bi][0:pr, 0:pc])
        ve_done.append(P.vbody(vfn))

        @P.block.gpsimd
        def _(g):
            g.wait_ge(ve.h, ve_done[i])
            st_v.append(P.stq.issue(g, wdst[r:r + pr, c:c + pc],
                                    P.bb16[bi][0:pr, 0:pc], i))
    return st_v, ve_done[-1]


def emit_ln_h(P, C, sc, xsrc, h16, nrm_sb, tag, xn_dram=None, entry=(),
              eps_ln=1e-5, eps_h=EPS32, with_h=True, xn_tiles=None):
    pe, ac, ve, gp = (P.sem(s) for s in ("pe", "ac", "ve", "gp"))
    ps_sum = P.PSF[0:1, 0:1024]
    ps_sq = P.PSF[0:1, 1024:2048]
    A = P.sbuf(f"{tag}_A", [1, 1024], F32, sc)
    Bv = P.sbuf(f"{tag}_B", [1, 1024], F32, sc)
    Cv = P.sbuf(f"{tag}_C", [1, 1024], F32, sc)
    Dv = P.sbuf(f"{tag}_D", [1, 1024], F32, sc)
    ep1 = P.sbuf(f"{tag}_e1", [1, 1], F32, sc)
    ep2 = P.sbuf(f"{tag}_e2", [1, 1], F32, sc)
    bm = P.sbuf(f"{tag}_bm", [128, 1024], F32, sc)
    br = P.sbuf(f"{tag}_br", [128, 1024], F32, sc)
    brh = P.sbuf(f"{tag}_brh", [128, 1024], F32, sc) if with_h else None

    def gfn(g, G):
        G.memset(ep1[:], eps_ln)
        G.memset(ep2[:], eps_h)
    v_eps = P.gbody(gfn)

    ld_v, ac_done, ve_done, pe_done = [], [], [], []
    for i in range(DT):
        bi = i % 2

        @P.block.gpsimd
        def _(g):
            if i >= 2:
                g.wait_ge(ac.h, ac_done[i - 2])
            else:
                for es, evv in entry:
                    g.wait_ge(es.h, evv)
            ld_v.append(P.mld.issue(g, P.wb32[bi][:],
                                    xsrc[i * 128:(i + 1) * 128, :], i))

        def vfn(V, i=i, bi=bi):
            V.wait_ge(ld_v[i][0].h, ld_v[i][1])
            if i >= 2:
                V.wait_ge(pe.h, pe_done[i - 2])
            V.tensor_copy(P.ba16[bi][:], P.wb32[bi][:])
        ve_done.append(P.vbody(vfn))

        def sfn(S_, i=i, bi=bi):
            S_.wait_ge(ve.h, ve_done[i])
            S_.activation(P.bb16[bi][:], P.wb32[bi][:], AF.Square)
        ac_done.append(P.sbody(sfn))

        @P.block.tensor
        def _(t):
            t.wait_ge(ac.h, ac_done[i])
            for off in (0, 512):
                t.matmul(ps_sum[0:1, off:off + 512], C["ones16"][:, 0:1],
                         P.ba16[bi][:, off:off + 512], start=(i == 0), stop=(i == DT - 1))
                mm = t.matmul(ps_sq[0:1, off:off + 512], C["ones16"][:, 0:1],
                              P.bb16[bi][:, off:off + 512], start=(i == 0),
                              stop=(i == DT - 1))
            mm.then_inc(pe.h, 1)
        pe_done.append(pe.inc(1))

    def vfn1(V):
        V.wait_ge(pe.h, pe_done[-1])
        V.tensor_single_scalar(A[:], ps_sum, 1.0 / DM, ALU.mult)
        V.tensor_single_scalar(Bv[:], ps_sq, 1.0 / DM, ALU.mult)
    v1 = P.vbody(vfn1)

    def sfn1(S_):
        S_.wait_ge(ve.h, v1)
        S_.activation(Dv[:], A[:], AF.Square)
    v2 = P.sbody(sfn1)

    def vfn2(V):
        V.wait_ge(ac.h, v2)
        V.tensor_sub(Bv[:], Bv[:], Dv[:])
    v3 = P.vbody(vfn2)

    def sfn2(S_):
        S_.wait_ge(ve.h, v3)
        S_.wait_ge(gp.h, v_eps)
        S_.activation(Cv[:], Bv[:], AF.Ln, bias=ep1[0:1, 0:1])
        if with_h:
            S_.activation(Dv[:], Cv[:], AF.Exp, scale=-1.0)
        S_.activation(Cv[:], Cv[:], AF.Exp, scale=-0.5)
    v4 = P.sbody(sfn2)

    def vfn3(V):
        V.wait_ge(ac.h, v4)
        if with_h:
            V.tensor_mul(Bv[:], Bv[:], Dv[:])
        V.tensor_mul(A[:], A[:], Cv[:])
    v5 = P.vbody(vfn3)

    if with_h:
        def sfn3(S_):
            S_.wait_ge(ve.h, v5)
            S_.activation(Bv[:], Bv[:], AF.Ln, bias=ep2[0:1, 0:1])
            S_.activation(Bv[:], Bv[:], AF.Exp, scale=-0.5)
        v6 = P.sbody(sfn3)
    else:
        v6 = None

    @P.block.tensor
    def _(t):
        t.wait_ge(ve.h, v5)
        if v6 is not None:
            t.wait_ge(ac.h, v6)
        for off in (0, 512):
            t.matmul(P.PSF[:, 2048 + off:2048 + off + 512], C["ones1r32"][0:1, :],
                     A[0:1, off:off + 512], start=True, stop=True)
            mm = t.matmul(P.PSF[:, 3072 + off:3072 + off + 512], C["ones1r32"][0:1, :],
                          Cv[0:1, off:off + 512], start=True, stop=True)
            if with_h:
                mm = t.matmul(P.PSF[:, 1024 + off:1024 + off + 512],
                              C["ones1r32"][0:1, :],
                              Bv[0:1, off:off + 512], start=True, stop=True)
        mm.then_inc(pe.h, 1)
    v7 = pe.inc(1)

    def vfn4(V):
        V.wait_ge(pe.h, v7)
        V.tensor_copy(bm[:], P.PSF[:, 2048:3072])
        V.tensor_copy(br[:], P.PSF[:, 3072:4096])
        if with_h:
            V.tensor_copy(brh[:], P.PSF[:, 1024:2048])
    v8 = P.vbody(vfn4)

    ld2, xn_ready, h_ready, xn_st = [], [], [], []
    for i in range(DT):
        bi = i % 2

        @P.block.gpsimd
        def _(g):
            if i < 2:
                g.wait_ge(ve.h, v8)
            else:
                g.wait_ge(ve.h, (h_ready if with_h else xn_ready)[i - 2])
                if xn_st:
                    g.wait_ge(xn_st[i - 2][0].h, xn_st[i - 2][1])
            ld2.append(P.mld.issue(g, P.wb32[bi][:],
                                   xsrc[i * 128:(i + 1) * 128, :], i))

        dst_t = xn_tiles[i] if xn_tiles is not None else P.ba16[bi]

        def vfn5(V, i=i, bi=bi, dst_t=dst_t):
            V.wait_ge(ld2[i][0].h, ld2[i][1])
            V.tensor_mul(P.wb32[bi][:], P.wb32[bi][:], br[:])
            V.tensor_sub(dst_t[:], P.wb32[bi][:], bm[:])
        xn_ready.append(P.vbody(vfn5))

        if xn_dram is not None:
            @P.block.gpsimd
            def _(g):
                g.wait_ge(ve.h, xn_ready[i])
                xn_st.append(P.stx.issue(g, xn_dram[i * 128:(i + 1) * 128, :],
                                         dst_t[:], i))

        if with_h:
            def vfn6(V, i=i, dst_t=dst_t):
                V.scalar_tensor_tensor(h16[i][:], dst_t[:], nrm_sb[:, i:i + 1],
                                       brh[:], ALU.mult, ALU.mult)
            h_ready.append(P.vbody(vfn6))
    return xn_ready, h_ready, xn_st


WPREP = [
    ("qkv", 256, 6144, 6144 * 2048, 1e-5),
    ("prj", 256, 2048, 2048 * 2048, 1e-5),
    ("w1", 256, HID, 2048 * HID, None),
    ("w3", 256, HID, 2048 * HID, None),
    ("w2", 683, 2048, HID * 2048, None),
]


def alloc_wprep(P, sc):
    w = {}
    w["acc"] = P.sbuf("w_acc", [128, 1], F32, sc)
    w["red"] = P.sbuf("w_red", [128, 1], F32, sc)
    w["prow"] = P.sbuf("w_prow", [1, 8], F32, sc)
    w["mrow_sb"] = P.sbuf("w_mrow", [1, 8], F32, sc)
    w["throw"] = P.sbuf("w_throw", [1, 8], F32, sc)
    w["th_bc"] = P.sbuf("w_thbc", [128, 16], F32, sc)
    w["wb"] = [P.sbuf(f"w_wb{i}", [128, 1024], F32, sc) for i in range(2)]
    w["q16"] = [P.sbuf(f"w_q16{i}", [128, 1024], BF16, sc) for i in range(2)]
    w["gtt"] = P.sbuf("w_gtt", [128, 1024], BF16, sc)
    w["prev_st"] = []
    w["pr_d"] = P.dram("w_pr_d", [1, 8], F32).ap()
    w["sums_d"] = P.dram("w_sums_d", [1, 8], F32).ap()
    w["prev_q_ve"] = 0
    return w


def emit_wprep_means(P, C, io, w):
    pe, ve, cc = P.sem("pe"), P.sem("ve"), P.sem("cc")
    GROUPS8 = [list(range(8))]
    acc, red, prow = w["acc"], w["red"], w["prow"]
    mrow_sb, throw, th_bc = w["mrow_sb"], w["throw"], w["th_bc"]
    wb = w["wb"]
    nb = len(wb)
    pr_d, sums_d = w["pr_d"], w["sums_d"]

    part_ve = []

    def vfn_z(V):
        V.memset(prow[:], 0.0)
        V.memset(mrow_sb[:], 0.0)
        V.memset(throw[:], 0.0)
    P.vbody(vfn_z)
    gi = [0]
    ve_done = {}
    for wi, (nm, R, Cc, nelem, clip) in enumerate(WPREP):
        wsl = io[nm + "_sl"]
        tiles = tile_list(R, Cc)

        def vfn0(V, wi=wi):
            V.memset(acc[:], 0.0)
        P.vbody(vfn0)
        for ti, (r, pr_, c, pc) in enumerate(tiles):
            i = gi[0]
            bi = i % nb

            @P.block.gpsimd
            def _(g):
                if i >= nb:
                    g.wait_ge(ve.h, ve_done[i - nb])
                w.setdefault("ld_v", {})[i] = P.mld4.issue(
                    g, wb[bi][0:pr_, 0:pc], wsl[r:r + pr_, c:c + pc], i % nb)

            def vfn(V, i=i, bi=bi, pr_=pr_, pc=pc):
                lv = w["ld_v"][i]
                V.wait_ge(lv[0].h, lv[1])
                V.tensor_reduce(red[0:pr_, 0:1], wb[bi][0:pr_, 0:pc], AX, ALU.add,
                                apply_absolute_value=True)
                V.tensor_add(acc[0:pr_, 0:1], acc[0:pr_, 0:1], red[0:pr_, 0:1])
            ve_done[i] = P.vbody(vfn)
            gi[0] += 1

        @P.block.tensor
        def _(t):
            t.wait_ge(ve.h, ve_done[gi[0] - 1])
            t.matmul(P.PSF[0:1, 7 * 512 + wi:7 * 512 + wi + 1], acc[:, 0:1],
                     C["onescol32"][:, 0:1], start=True, stop=True).then_inc(pe.h, 1)
        v_pe = pe.inc(1)

        def vfn2(V, wi=wi, v_pe=v_pe):
            V.wait_ge(pe.h, v_pe)
            V.tensor_copy(prow[0:1, wi:wi + 1], P.PSF[0:1, 7 * 512 + wi:7 * 512 + wi + 1])
        part_ve.append(P.vbody(vfn2))

    @P.block.gpsimd
    def _(g):
        g.wait_ge(ve.h, part_ve[-1])
        P.sgw.issue(g, pr_d, prow[:])
        g.wait_ge(P.sgw.sems[0].h, P.sgw.sems[0].n)
        g.collective_compute("AllReduce", ALU.add, replica_groups=GROUPS8,
                             ins=[pr_d], outs=[sums_d]).then_inc(cc.h, 1)
    v_cc_ar = cc.inc(1)

    @P.block.gpsimd
    def _(g):
        g.wait_ge(cc.h, v_cc_ar)
        P.sgw.issue(g, mrow_sb[:], sums_d)
    v_sums = (P.sgw.sems[0], P.sgw.sems[0].n)

    def vfn_m(V):
        V.wait_ge(v_sums[0].h, v_sums[1])
        for wi, (nm, R, Cc, nelem, clip) in enumerate(WPREP):
            sl = slice(wi, wi + 1)
            if clip is None:
                V.tensor_single_scalar(mrow_sb[0:1, sl], mrow_sb[0:1, sl],
                                       1.0 / nelem, ALU.mult)
            else:
                V.tensor_scalar(mrow_sb[0:1, sl], mrow_sb[0:1, sl], 1.0 / nelem,
                                clip, ALU.mult, ALU.max)
        V.tensor_single_scalar(throw[:], mrow_sb[:], 0.5, ALU.mult)
    v_m = P.vbody(vfn_m)

    @P.block.gpsimd
    def _(g):
        g.wait_ge(ve.h, v_m)
        P.sgw.issue(g, io["mrow"], mrow_sb[:])

    @P.block.tensor
    def _(t):
        t.wait_ge(ve.h, v_m)
        t.matmul(P.PSF[:, 7 * 512:7 * 512 + 8], C["ones1r32"][0:1, :],
                 throw[0:1, 0:8], start=True, stop=True).then_inc(pe.h, 1)
    v_thb = pe.inc(1)

    def vfn_tb(V):
        V.wait_ge(pe.h, v_thb)
        V.tensor_copy(th_bc[:, 0:8], P.PSF[:, 7 * 512:7 * 512 + 8])
        V.tensor_single_scalar(th_bc[:, 8:16], th_bc[:, 0:8], -1.0, ALU.mult)
    w["v_tb"] = P.vbody(vfn_tb)


def emit_wprep_quant(P, C, io, w, names):
    ve, cc = P.sem("ve"), P.sem("cc")
    GROUPS8 = [list(range(8))]
    wb, q16, gtt, th_bc = w["wb"], w["q16"], w["gtt"], w["th_bc"]
    cc_gates = {}
    widx = {nm: i for i, (nm, _, _, _, _) in enumerate(WPREP)}
    ag_jobs = []
    for nm in names:
        wi = widx[nm]
        R, Cc = WPREP[wi][1], WPREP[wi][2]
        wsl = io[nm + "_sl"]
        pub = io[nm + "_pub"]
        tiles = tile_list(R, Cc)
        ld_v, ve_done, st_v = [], [], []
        prev_st = w["prev_st"]
        for i, (r, pr_, c, pc) in enumerate(tiles):
            bi = i % 2

            @P.block.gpsimd
            def _(g):
                if i >= 2:
                    g.wait_ge(ve.h, ve_done[i - 2])
                else:
                    g.wait_ge(ve.h, w["prev_q_ve"] if w["prev_q_ve"] else w["v_tb"])
                ld_v.append(P.sqd.issue(g, wb[bi][0:pr_, 0:pc],
                                        wsl[r:r + pr_, c:c + pc], i))

            def vfn(V, i=i, bi=bi, pr_=pr_, pc=pc, wi=wi, prev_st=prev_st):
                V.wait_ge(ld_v[i][0].h, ld_v[i][1])
                if i == 0:
                    V.wait_ge(ve.h, w["v_tb"])
                if i < 2:
                    for s_, v_ in prev_st[-2:]:
                        V.wait_ge(s_.h, v_)
                if i >= 2:
                    V.wait_ge(st_v[i - 2][0].h, st_v[i - 2][1])
                V.tensor_single_scalar(q16[bi][0:pr_, 0:pc], wb[bi][0:pr_, 0:pc],
                                       th_bc[0:pr_, wi:wi + 1], ALU.is_gt)
                V.tensor_single_scalar(gtt[0:pr_, 0:pc], wb[bi][0:pr_, 0:pc],
                                       th_bc[0:pr_, 8 + wi:9 + wi], ALU.is_lt)
                V.tensor_sub(q16[bi][0:pr_, 0:pc], q16[bi][0:pr_, 0:pc],
                             gtt[0:pr_, 0:pc])
            ve_done.append(P.vbody(vfn))

            @P.block.gpsimd
            def _(g):
                g.wait_ge(ve.h, ve_done[i])
                st_v.append(P.sol.issue(g, pub[r:r + pr_, c:c + pc],
                                        q16[bi][0:pr_, 0:pc], i))

        w["prev_q_ve"] = ve_done[-1]
        w["prev_st"] = st_v[-2:]
        ag_jobs.append((nm, pub, [(s.h, s.n) for s in P.sol.sems if s.n]))

    for nm, pub, gates in ag_jobs:
        @P.block.gpsimd
        def _(g, pub=pub, gates=gates):
            for h_, v_ in gates:
                g.wait_ge(h_, v_)
            g.collective_compute("AllGather", ALU.bypass, replica_groups=GROUPS8,
                                 ins=[pub], outs=[io[nm + "_ag"]]).then_inc(cc.h, 1)
        cc_gates[nm] = cc.inc(1)
    return cc_gates


WCAST = [("watt", 256, 6144), ("wp1", 256, 2048), ("wf", 256, 2048)]


def emit_wprep_cast(P, C, io, w, names, stage):
    cc = P.sem("cc")
    GROUPS8 = [list(range(8))]
    cc_gates = {}
    if stage == 0:
        w["cast_st"] = {}

        @P.block.gpsimd
        def _(g):
            for nm in names:
                w["cast_st"][nm] = P.srr.issue(g, io[nm + "_pub"], io[nm + "_sl"], 1)
        return cc_gates

    for nm in names:
        st = w["cast_st"][nm]

        @P.block.gpsimd
        def _(g, st=st, nm=nm):
            g.wait_ge(st[0].h, st[1])
            g.collective_compute("AllGather", ALU.bypass, replica_groups=GROUPS8,
                                 ins=[io[nm + "_pub"]],
                                 outs=[io[nm + "_ag"]]).then_inc(cc.h, 1)
        cc_gates[nm] = cc.inc(1)
    return cc_gates


def _finish_early(P):
    @P.block.gpsimd
    def _(g):
        P.wait_all_streams(g)

    @P.block.sync
    def _(sy):
        P.wait_all_streams(sy)
    return P.finish()

def emit_l0a(P, C, io, upto=99, walloc_cb=None, wprep_cb=None):
    P.tag = "a"
    sc_main = ExitStack()
    P.default_sc = sc_main
    xT = io["xT"]
    qkvnrm = io["qkvnrm"]
    s_col = io["s_col"]
    s_row = io["s_row"]
    xn_out = io["xn0"]
    oloc_out = io["oloc"]
    qdec_out = io["qdec"]
    kvagg_out = io["kvagg"]

    wq_q = io["qkv_ag"]
    cc_qkv = io.get("cc_qkv", 0)
    rowscr = P.dram("rowscr", [1, 4096], F32).ap()
    rowscr2 = P.dram("rowscr2", [1, 4096], F32).ap()

    pe, ac, ve, gp, cc = (P.sem(s) for s in ("pe", "ac", "ve", "gp", "cc"))

    s_sb = P.sbuf("s_sb", [H, 1], F32)
    srow_sb = P.sbuf("srow_sb", [1, H], F32)
    nrm_sb = P.sbuf("nrm_sb", [128, DT], F32)
    neg_s = P.sbuf("neg_s", [H, 1], F32)
    sT_bias = P.sbuf("sT_bias", [H, 1], F32)
    lam_t16 = P.sbuf("lam_t16", [H, 128], F32)
    lam_rev16 = P.sbuf("lam_rev16", [H, 128], F32)
    lamT16 = P.sbuf("lamT16", [H, 1], F32)
    powj16 = P.sbuf("powj16", [H, 8], F32)
    iota_t = P.sbuf("iota_t", [H, 128], F32)
    iota_j = P.sbuf("iota_j", [H, 8], F32)
    lam_btall = P.sbuf("lam_btall", [128, H * 128], BF16)
    lamrev_btall = P.sbuf("lamrev_btall", [128, H * 128], BF16)
    lam_bt = [lam_btall[:, h * 128:(h + 1) * 128] for h in range(H)]
    lamrev_bt = [lamrev_btall[:, h * 128:(h + 1) * 128] for h in range(H)]
    powj_bc = P.sbuf("powj_bc", [128, 128], F32)
    lamT_bc = P.sbuf("lamT_bc", [128, 16], F32)
    lamT_ball = P.sbuf("lamT_ball", [128, H * 128], F32)
    negs_bc = P.sbuf("negs_bc", [128, 16], F32)
    diffpos = P.sbuf("diffpos", [128, 128], F32)
    maskTall = P.sbuf("maskTall", [128, H * 128], BF16)
    maskT = [maskTall[:, h * 128:(h + 1) * 128] for h in range(H)]
    h16 = [P.sbuf(f"h16_{i}", [128, 1024], BF16) for i in range(DT)]
    V16 = [P.sbuf(f"V16_{c}", [128, 2048], BF16) for c in range(CPC)]
    S = P.sbuf("S", [128, 2048], F32)
    S16 = P.sbuf("S16", [128, 2048], BF16)

    @P.block.gpsimd
    def _(g):
        P.mvec.issue(g, s_sb[:], s_col)
        P.mvec.issue(g, srow_sb[:], s_row)
        for i in range(DT):
            P.mvec.issue(g, nrm_sb[:, i:i + 1], qkvnrm[i * 128:(i + 1) * 128, 0:1])
    v_vecs = (P.mvec.sems[0], P.mvec.sems[0].n)

    def gfn_iota(g, G):
        G.iota(iota_t[:], pattern=[[1, 128]], base=0, channel_multiplier=0,
               allow_small_or_imprecise_dtypes=True)
        G.iota(iota_j[:], pattern=[[128, 8]], base=0, channel_multiplier=0,
               allow_small_or_imprecise_dtypes=True)
    v_iot = P.gbody(gfn_iota)

    def vfn_negs(V):
        V.wait_ge(v_vecs[0].h, v_vecs[1])
        V.tensor_single_scalar(neg_s[:], s_sb[:], -1.0, ALU.mult)
        V.tensor_single_scalar(sT_bias[:], s_sb[:], -float(T), ALU.mult)
    v_negs = P.vbody(vfn_negs)

    def sfn_tabs(S_):
        S_.wait_ge(gp.h, v_iot)
        S_.wait_ge(ve.h, v_negs)
        S_.activation(lam_t16[:], iota_t[:], AF.Exp, scale=neg_s[:, 0:1])
        S_.activation(lam_rev16[:], iota_t[:], AF.Exp, scale=s_sb[:, 0:1],
                      bias=sT_bias[:, 0:1])
        S_.activation(lamT16[:], s_sb[:], AF.Exp, scale=-float(T))
        S_.activation(powj16[:], iota_j[:], AF.Exp, scale=neg_s[:, 0:1])
    v_tabs = P.sbody(sfn_tabs)

    @P.block.gpsimd
    def _(g):
        g.wait_ge(ac.h, v_tabs)
        for hh in range(H):
            P.mvec.issue(g, rowscr[0:1, hh * 128:(hh + 1) * 128],
                         lam_t16[hh:hh + 1, :])
            P.mvec.issue(g, rowscr2[0:1, hh * 128:(hh + 1) * 128],
                         lam_rev16[hh:hh + 1, :])
            P.mvec.issue(g, rowscr[0:1, 2048 + hh * 8:2048 + (hh + 1) * 8],
                         powj16[hh:hh + 1, :])
            P.mvec.issue(g, rowscr2[0:1, 2048 + hh:2049 + hh],
                         lamT16[hh:hh + 1, 0:1])
    v_spread = (P.mvec.sems[0], P.mvec.sems[0].n)

    sc_rows = ExitStack()
    lamrow = P.sbuf("lamrow", [1, 2048], BF16, sc_rows)
    lamrevrow = P.sbuf("lamrevrow", [1, 2048], BF16, sc_rows)
    powrow = P.sbuf("powrow", [1, 128], F32, sc_rows)
    lamTrow = P.sbuf("lamTrow", [1, 16], F32, sc_rows)

    @P.block.gpsimd
    def _(g):
        g.wait_ge(v_spread[0].h, v_spread[1])
        P.mvec.issue(g, lamrow[:], rowscr[0:1, 0:2048])
        P.mvec.issue(g, lamrevrow[:], rowscr2[0:1, 0:2048])
        P.mvec.issue(g, powrow[:], rowscr[0:1, 2048:2176])
        P.mvec.issue(g, lamTrow[:], rowscr2[0:1, 2048:2064])
    v_rowsld = (P.mvec.sems[0], P.mvec.sems[0].n)

    last_copy = [0]
    for grp in range(4):
        for which, (src, dstl) in enumerate(((lamrow, lam_bt), (lamrevrow, lamrev_bt))):
            @P.block.tensor
            def _(t):
                if grp == 0 and which == 0:
                    t.wait_ge(v_rowsld[0].h, v_rowsld[1])
                else:
                    t.wait_ge(ve.h, last_copy[0])
                t.matmul(P.bank(6), C["ones1r16"][0:1, :],
                         src[0:1, grp * 512:(grp + 1) * 512],
                         start=True, stop=True).then_inc(pe.h, 1)
            v1 = pe.inc(1)

            def vfn(V, grp=grp, dstl=dstl, v1=v1):
                V.wait_ge(pe.h, v1)
                for k in range(4):
                    hh = grp * 4 + k
                    V.tensor_copy(dstl[hh][:],
                                  P.PSF[:, 6 * 512 + k * 128:6 * 512 + (k + 1) * 128])
            last_copy[0] = P.vbody(vfn)

    @P.block.tensor
    def _(t):
        t.wait_ge(ve.h, last_copy[0])
        t.matmul(P.bankc(6, 128), C["ones1r32"][0:1, :], powrow[0:1, 0:128],
                 start=True, stop=True)
        t.matmul(P.PSF[:, 6 * 512 + 128:6 * 512 + 144], C["ones1r32"][0:1, :],
                 lamTrow[0:1, 0:16], start=True, stop=True)
        t.matmul(P.PSF[:, 6 * 512 + 144:6 * 512 + 160], C["ones1r32"][0:1, :],
                 srow_sb[0:1, 0:16], start=True, stop=True).then_inc(pe.h, 1)
    v_bt = pe.inc(1)

    def vfn_bc(V):
        V.wait_ge(pe.h, v_bt)
        V.tensor_copy(powj_bc[:], P.bankc(6, 128))
        V.tensor_copy(lamT_bc[:], P.PSF[:, 6 * 512 + 128:6 * 512 + 144])
        V.tensor_single_scalar(negs_bc[:], P.PSF[:, 6 * 512 + 144:6 * 512 + 160],
                               -1.0, ALU.mult)
        V.memset(lamT_ball[:], 1.0)
        for hh in range(H):
            sl = slice(hh * 128, (hh + 1) * 128)
            V.tensor_single_scalar(lamT_ball[:, sl], lamT_ball[:, sl],
                                   lamT_bc[:, hh:hh + 1], ALU.mult)
    v_bc_done = P.vbody(vfn_bc)
    sc_rows.close()

    def vfn_dp(V):
        V.tensor_single_scalar(diffpos[:], C["iota_pf"][:], -1.0, ALU.mult)
        V.tensor_scalar_max(diffpos[:], diffpos[:], 0.0)
    v_dp = P.vbody(vfn_dp)

    mask_ve = []
    for hh in range(H):
        def sfn_m(S_, hh=hh):
            if hh == 0:
                S_.wait_ge(ve.h, v_bc_done)
                S_.wait_ge(ve.h, v_dp)
            S_.activation(maskT[hh][:], diffpos[:], AF.Exp,
                          scale=negs_bc[:, hh:hh + 1])
        v_m = P.sbody(sfn_m)

        def vfn_m(V, hh=hh, v_m=v_m):
            V.wait_ge(ac.h, v_m)
            V.tensor_mul(maskT[hh][:], maskT[hh][:], C["trilT"][:])
        mask_ve.append(P.vbody(vfn_m))
    v_masks = mask_ve[-1]

    if walloc_cb is not None:
        walloc_cb()

    sc_pool = ExitStack()
    alloc_pools(P, sc_pool)

    sc_ln = ExitStack()
    xn_ready, h_ready, xn_st = emit_ln_h(
        P, C, sc_ln, xT, h16, nrm_sb, "ln0", xn_dram=xn_out,
        entry=[(ve, v_bc_done)])
    v_h = h_ready[-1]
    sc_ln.close()
    sc_pool.close()
    stx_fin = [(s.h, s.n) for s in P.stx.sems if s.n]

    if wprep_cb is not None:
        stx_fin = stx_fin + wprep_cb()
        P.tag = "a"

    QW = P.sbuf("QW", [128, H * 1024], BF16)
    KW = P.sbuf("KW", [128, H * 1024], BF16)
    Q16 = [QW[:, h * 1024:(h + 1) * 1024] for h in range(H)]
    K16 = [KW[:, h * 1024:(h + 1) * 1024] for h in range(H)]
    wqt = [P.sbuf(f"wqt{i}", [128, 512], BF16) for i in range(4)]
    qtall = P.sbuf("qtall", [128, H * 128], BF16)
    qdall = P.sbuf("qdall", [128, H * 128], BF16)
    ktall = P.sbuf("ktall", [128, H * 128], BF16)
    ktmall = P.sbuf("ktmall", [128, H * 128], BF16)
    smaskall = P.sbuf("smaskall", [128, H * 128], BF16)
    otall = P.sbuf("otall", [128, H * 128], BF16)

    ld_i = [0]
    ldv = {}
    mm_pe = {}

    def wload(base_f, kt):
        i = ld_i[0]
        bi = i % 4

        @P.block.sync
        def _(sy):
            if i == 0:
                ccg = io.get("cc_qkv", 0)
                if ccg:
                    sy.wait_ge(cc.h, ccg)
                sy.wait_ge(ve.h, v_h)
                for h_, v_ in stx_fin:
                    sy.wait_ge(h_, v_)
            if i >= 4:
                sy.wait_ge(pe.h, mm_pe[i - 4])
            ldv[i] = P.wl.issue(sy, wqt[bi][:],
                                wq_q[kt * 128:(kt + 1) * 128, base_f:base_f + 512], i)
        return i, bi

    evac_gate = [0]
    for role in range(2):
        for grp in range(4):
            base_f = role * DM + grp * 512
            for kt in range(DT):
                i, bi = wload(base_f, kt)

                @P.block.tensor
                def _(t):
                    t.wait_ge(ldv[i][0].h, ldv[i][1])
                    if kt == 0 and evac_gate[0]:
                        t.wait_ge(ve.h, evac_gate[0])
                    mm = None
                    for k in range(4):
                        for off in (0, 512):
                            mm = t.matmul(
                                P.PSF[:, k * 1024 + off:k * 1024 + off + 512],
                                wqt[bi][:, k * 128:(k + 1) * 128],
                                h16[kt][:, off:off + 512],
                                start=(kt == 0), stop=(kt == DT - 1))
                    mm.then_inc(pe.h, 1)
                mm_pe[i] = pe.inc(1)
                ld_i[0] += 1
            v_grp = pe.n

            def vfn_ev(V, grp=grp, role=role, v_grp=v_grp):
                V.wait_ge(pe.h, v_grp)
                if role == 0 and grp == 0:
                    for h_, v_ in stx_fin:
                        V.wait_ge(h_, v_)
                for k in range(4):
                    hh = grp * 4 + k
                    dst = Q16[hh] if role == 0 else K16[hh]
                    V.tensor_copy(dst[:], P.PSF[:, k * 1024:(k + 1) * 1024])
            evac_gate[0] = P.vbody(vfn_ev)

    for vg in range(4):
        base_f = 2 * DM + vg * 512
        for kt in range(DT):
            i, bi = wload(base_f, kt)

            @P.block.tensor
            def _(t):
                t.wait_ge(ldv[i][0].h, ldv[i][1])
                if kt == 0:
                    t.wait_ge(ve.h, evac_gate[0])
                mm = None
                for c in range(CPC):
                    mm = t.matmul(P.bank(c), h16[kt][:, c * 128:(c + 1) * 128],
                                  wqt[bi][:, 0:512],
                                  start=(kt == 0), stop=(kt == DT - 1))
                mm.then_inc(pe.h, 1)
            mm_pe[i] = pe.inc(1)
            ld_i[0] += 1
        v_vg = pe.n

        def vfn_vev(V, vg=vg, v_vg=v_vg):
            V.wait_ge(pe.h, v_vg)
            for c in range(CPC):
                V.tensor_copy(V16[c][:, vg * 512:(vg + 1) * 512], P.bank(c))
        evac_gate[0] = P.vbody(vfn_vev)

    v_qkv = evac_gate[0]

    def vfn_s0(V):
        V.wait_ge(ve.h, v_qkv)
        V.memset(S[:], 0.0)
        V.memset(S16[:], 0.0)
    v_s0 = P.vbody(vfn_s0)

    @P.block.tensor
    def _(t):
        t.wait_ge(ve.h, v_masks)
        t.wait_ge(ve.h, v_s0)

    prev = {"qdec_gq": None, "oloc_gq": None}

    def v3(ap2d):
        return ap2d.rearrange("p (h j) -> p h j", h=H)

    for c in range(CPC):
        qap = v3(QW[:])[:, :, c * 128:(c + 1) * 128]
        kap = v3(KW[:])[:, :, c * 128:(c + 1) * 128]

        @P.block.tensor
        def _(t):
            mm = None
            for hh in range(H):
                cs = slice(c * 128, (c + 1) * 128)
                mm = t.matmul(P.PSF[:, hh * 128:(hh + 1) * 128],
                              K16[hh][:, cs], Q16[hh][:, cs],
                              start=True, stop=True)
            mm.then_inc(pe.h, 1)
        v_sc = pe.inc(1)

        qd_prev = prev["qdec_gq"]

        def vfn_a(V, c=c, v_sc=v_sc, qd_prev=qd_prev, qap=qap, kap=kap):
            V.wait_ge(pe.h, v_sc)
            if qd_prev is not None:
                V.wait_ge(qd_prev[0].h, qd_prev[1])
            V.tensor_mul(smaskall[:], P.PSF[:, 0:2048], maskTall[:])
            V.tensor_mul(v3(qtall[:]), qap, v3(lam_btall[:]))
            V.tensor_mul(v3(ktall[:]), kap, v3(lamrev_btall[:]))
            for hh in range(H):
                sl = slice(hh * 128, (hh + 1) * 128)
                V.tensor_single_scalar(qdall[:, sl], qtall[:, sl],
                                       powj_bc[:, hh * 8 + c:hh * 8 + c + 1],
                                       ALU.mult)
        v_sm = P.vbody(vfn_a)

        @P.block.gpsimd
        def _(g):
            g.wait_ge(ve.h, v_sm)
            prev["qdec_gq"] = P.sqd.issue(g, qdec_out[c], qdall[:], c)

        @P.block.tensor
        def _(t):
            t.wait_ge(ve.h, v_sm)
            mm = None
            for hh in range(H):
                hs = slice(hh * 128, (hh + 1) * 128)
                t.matmul(P.PSF[:, 2048 + hh * 128:2048 + (hh + 1) * 128],
                         V16[c][:, hs], smaskall[:, hs],
                         start=True, stop=(c == 0))
                if c > 0:
                    t.matmul(P.PSF[:, 2048 + hh * 128:2048 + (hh + 1) * 128],
                             S16[:, hs], qtall[:, hs], start=False, stop=True)
                mm = t.matmul(P.PSF[:, hh * 128:(hh + 1) * 128], ktall[:, hs],
                              C["ident16"][:], start=True, stop=True)
            mm.then_inc(pe.h, 1)
        v_oc = pe.inc(1)

        ol_prev = prev["oloc_gq"]

        def vfn_b(V, v_oc=v_oc, ol_prev=ol_prev):
            V.wait_ge(pe.h, v_oc)
            if ol_prev is not None:
                V.wait_ge(ol_prev[0].h, ol_prev[1])
            V.tensor_copy(otall[:], P.PSF[:, 2048:4096])
            V.tensor_copy(ktmall[:], P.PSF[:, 0:2048])
        v_ev = P.vbody(vfn_b)

        @P.block.gpsimd
        def _(g):
            g.wait_ge(ve.h, v_ev)
            prev["oloc_gq"] = P.sol.issue(g, oloc_out[c], otall[:], c)

        @P.block.tensor
        def _(t):
            t.wait_ge(ve.h, v_ev)
            mm = None
            for hh in range(H):
                hs = slice(hh * 128, (hh + 1) * 128)
                mm = t.matmul(P.PSF[:, 2048 + hh * 128:2048 + (hh + 1) * 128],
                              ktmall[:, hs], V16[c][:, hs], start=True, stop=True)
            mm.then_inc(pe.h, 1)
        v_kv = pe.inc(1)

        def vfn_c(V, c=c, v_kv=v_kv):
            V.wait_ge(pe.h, v_kv)
            V.tensor_mul(S[:], S[:], lamT_ball[:])
            V.tensor_add(S[:], S[:], P.PSF[:, 2048:4096])
            if c < CPC - 1:
                V.tensor_copy(S16[:], S[:])
        P.vbody(vfn_c)

    v_scan = ve.n

    @P.block.gpsimd
    def _(g):
        g.wait_ge(ve.h, v_scan)
        P.mvec.issue(g, kvagg_out, S[:])

    sc_main.close()
    P.default_sc = None


def build_l0a():
    P = LB("l0a").start()
    io = {
        "xT": P.din("xT", [DM, TOK], F32).ap(),
        "qkv_ag": P.din("qkv_ag", [DM, 3 * DM], BF16).ap(),
        "qkvnrm": P.din("qkvnrm", [DM, 1], F32).ap(),
        "s_col": P.din("s_col", [H, 1], F32).ap(),
        "s_row": P.din("s_row", [1, H], F32).ap(),
        "xn0": P.dout("xn_out", [DM, TOK], BF16).ap(),
        "oloc": P.dout("oloc_out", [CPC, 128, H * 128], BF16).ap(),
        "qdec": P.dout("qdec_out", [CPC, 128, H * 128], BF16).ap(),
        "kvagg": P.dout("kvagg_out", [128, H * 128], F32).ap(),
        "mq": P.dout("mq_out", [1, 1], F32).ap(),
    }
    C = emit_consts(P)
    emit_l0a(P, C, io)

    @P.block.gpsimd
    def _(g):
        P.wait_all_streams(g)

    @P.block.sync
    def _(sy):
        P.wait_all_streams(sy)

    return P.finish()


def emit_rms_apply(P, C, sc, tiles_in, out_tiles, normw_sb, eps, tag,
                   entry=(), bank_row=0, bank_bc=2):
    pe, ac, ve, gp = (P.sem(s) for s in ("pe", "ac", "ve", "gp"))
    ps_row = P.PSF[0:1, bank_row * 512:bank_row * 512 + 1024]
    ps_bc = P.PSF[:, bank_bc * 512:bank_bc * 512 + 1024]
    sq = [P.sbuf(f"{tag}_sq{i}", [128, 1024], BF16, sc) for i in range(2)]
    row = P.sbuf(f"{tag}_row", [1, 1024], F32, sc)
    epst = P.sbuf(f"{tag}_eps", [1, 1], F32, sc)
    rb = P.sbuf(f"{tag}_rb", [128, 1024], F32, sc)

    def gfn(g, G):
        G.memset(epst[:], eps)
    v_eps = P.gbody(gfn)

    ac_done, pe_done = [], []
    for i in range(DT):
        bi = i % 2

        def sfn(S_, i=i, bi=bi):
            if i == 0:
                for es, evv in entry:
                    S_.wait_ge(es.h, evv)
            if i >= 2:
                S_.wait_ge(pe.h, pe_done[i - 2])
            S_.activation(sq[bi][:], tiles_in[i][:], AF.Square)
        ac_done.append(P.sbody(sfn))

        @P.block.tensor
        def _(t):
            t.wait_ge(ac.h, ac_done[i])
            mm = None
            for off in (0, 512):
                mm = t.matmul(ps_row[0:1, off:off + 512], C["ones16"][:, 0:1],
                              sq[bi][:, off:off + 512], start=(i == 0),
                              stop=(i == DT - 1))
            mm.then_inc(pe.h, 1)
        pe_done.append(pe.inc(1))

    def vfn1(V):
        V.wait_ge(pe.h, pe_done[-1])
        V.tensor_single_scalar(row[:], ps_row, 1.0 / DM, ALU.mult)
    v1 = P.vbody(vfn1)

    def sfn1(S_):
        S_.wait_ge(ve.h, v1)
        S_.wait_ge(gp.h, v_eps)
        S_.activation(row[:], row[:], AF.Ln, bias=epst[0:1, 0:1])
        S_.activation(row[:], row[:], AF.Exp, scale=-0.5)
    v2 = P.sbody(sfn1)

    @P.block.tensor
    def _(t):
        t.wait_ge(ac.h, v2)
        mm = None
        for off in (0, 512):
            mm = t.matmul(ps_bc[:, off:off + 512], C["ones1r32"][0:1, :],
                          row[0:1, off:off + 512], start=True, stop=True)
        mm.then_inc(pe.h, 1)
    v3 = pe.inc(1)

    def vfn2(V):
        V.wait_ge(pe.h, v3)
        V.tensor_copy(rb[:], ps_bc)
    v4 = P.vbody(vfn2)

    out_ready = []
    for i in range(DT):
        def vfn3(V, i=i):
            V.scalar_tensor_tensor(out_tiles[i][:], tiles_in[i][:],
                                   normw_sb[:, i:i + 1], rb[:], ALU.mult, ALU.mult)
        out_ready.append(P.vbody(vfn3))
    return out_ready



def emit_l0b(P, C, io):
    P.tag = "b"
    sc_main = ExitStack()
    P.default_sc = sc_main
    xT = io["xT"]
    xnT = io["xn0"]
    oloc_in = io["oloc"]
    qdec_in = io["qdec"]
    kvpre = io["kvall"]
    wfac_row = io["wfac_row"]
    gap_row = io["gap_row"]
    s_col = io["s_col"]
    normw_in = io["normw_in"]
    pnormw_in = io["pnormw_in"]
    x1_out = io["x1"]
    xn1_out = io["xn1"]

    wp_q = io["prj_ag"]
    w1_q = io["w1_ag"]
    w3_q = io["w3_ag"]
    w2_q = io["w2_ag"]
    mrow_in = io["mrow"]
    rowscr = P.dram("rowscr", [1, 4096], F32).ap()

    pe, ac, ve, gp = (P.sem(s) for s in ("pe", "ac", "ve", "gp"))

    s_sb = P.sbuf("s_sb", [H, 1], F32)
    mq_sb = P.sbuf("mq_sb", [1, 1], F32)
    nrmw = P.sbuf("nrmw", [128, DT], F32)
    pnrmw = P.sbuf("pnrmw", [128, DT], F32)
    facs16 = P.sbuf("facs16", [H, 3], F32)
    facs_bc = P.sbuf("facs_bc", [128, 48], F32)
    alpha_vec = P.sbuf("alpha_vec", [128, 1], F32)
    s23_vec = P.sbuf("s23_vec", [128, 1], F32)
    S_in = P.sbuf("S_in", [128, H * 128], F32)
    S_in16 = P.sbuf("S_in16", [128, H * 128], BF16)
    xn16 = [P.sbuf(f"xn16_{i}", [128, 1024], BF16) for i in range(DT)]
    sbr = [P.sbuf(f"sbr_{i}", [128, 1024], BF16) for i in range(DT)]
    mrow_sb = P.sbuf("mrow_sb", [1, 8], F32)
    mvec_bc = P.sbuf("mvec_bc", [128, 8], F32)
    OA = [P.sbuf(f"OA{i}", [128, 1024], F32) for i in range(DT)]
    o_acc = OA
    sc_o2 = ExitStack()
    o2t = [P.sbuf(f"o2t_{i}", [128, 1024], BF16, sc_o2) for i in range(DT)]


    @P.block.gpsimd
    def _(g):
        P.mvec.issue(g, s_sb[:], s_col)
        P.mvec.issue(g, mq_sb[:], mrow_in[0:1, 0:1])
        P.mvec.issue(g, mrow_sb[:], mrow_in)
        for i in range(DT):
            P.mvec.issue(g, nrmw[:, i:i + 1], normw_in[i * 128:(i + 1) * 128, 0:1])
            P.mvec.issue(g, pnrmw[:, i:i + 1], pnormw_in[i * 128:(i + 1) * 128, 0:1])
        for i in range(DT):
            P.sxn.issue(g, xn16[i][:], xnT[i * 128:(i + 1) * 128, :])
    v_vl = (P.mvec.sems[0], P.mvec.sems[0].n)
    v_xnld = (P.sxn.sems[0], P.sxn.sems[0].n)

    sc_fac = ExitStack()
    s128 = P.sbuf("s128", [H, 1], F32, sc_fac)
    gap_bc = P.sbuf("gap_bc", [H, 3], F32, sc_fac)
    w_bc = P.sbuf("w_bc", [H, 3], F32, sc_fac)
    arow = P.sbuf("arow", [1, 1], F32, sc_fac)
    facrow = P.sbuf("facrow", [1, 48], F32, sc_fac)
    gapw = P.sbuf("gapw", [1, 6], F32, sc_fac)

    @P.block.gpsimd
    def _(g):
        P.sgw.issue(g, gapw[0:1, 0:3], gap_row)
        P.sgw.issue(g, gapw[0:1, 3:6], wfac_row)
    v_gw = (P.sgw.sems[0], P.sgw.sems[0].n)

    @P.block.tensor
    def _(t):
        t.wait_ge(v_gw[0].h, v_gw[1])
        t.matmul(P.bankc(6, 6, rows=H), C["ones1r32"][0:1, 0:H], gapw[0:1, 0:6],
                 start=True, stop=True).then_inc(pe.h, 1)
    v_gb = pe.inc(1)

    def vfn_f1(V):
        V.wait_ge(pe.h, v_gb)
        V.wait_ge(v_vl[0].h, v_vl[1])
        V.tensor_copy(gap_bc[:], P.PSF[0:H, 6 * 512:6 * 512 + 3])
        V.tensor_copy(w_bc[:], P.PSF[0:H, 6 * 512 + 3:6 * 512 + 6])
        V.tensor_single_scalar(s128[:], s_sb[:], -float(T), ALU.mult)
    v_f1 = P.vbody(vfn_f1)

    def sfn_f(S_):
        S_.wait_ge(ve.h, v_f1)
        S_.activation(facs16[:], gap_bc[:], AF.Exp, scale=s128[:, 0:1])
    v_f2 = P.sbody(sfn_f)

    def vfn_f2(V):
        V.wait_ge(ac.h, v_f2)
        V.tensor_mul(facs16[:], facs16[:], w_bc[:])
        V.tensor_copy(arow[:], mq_sb[:])
        V.tensor_mul(gapw[0:1, 0:1], arow[:], arow[:])
        V.tensor_mul(arow[:], gapw[0:1, 0:1], arow[:])
        V.tensor_single_scalar(arow[:], arow[:], INV_SQRT_HD, ALU.mult)
    v_f3 = P.vbody(vfn_f2)

    @P.block.gpsimd
    def _(g):
        g.wait_ge(ve.h, v_f3)
        for hh in range(H):
            P.mvec.issue(g, rowscr[0:1, hh * 3:(hh + 1) * 3], facs16[hh:hh + 1, :])
    v_sp = (P.mvec.sems[0], P.mvec.sems[0].n)

    @P.block.gpsimd
    def _(g):
        g.wait_ge(v_sp[0].h, v_sp[1])
        P.mvec.issue(g, facrow[:], rowscr[0:1, 0:48])
    v_fr = (P.mvec.sems[0], P.mvec.sems[0].n)

    @P.block.tensor
    def _(t):
        t.wait_ge(v_fr[0].h, v_fr[1])
        t.matmul(P.bankc(6, 48), C["ones1r32"][0:1, :], facrow[0:1, 0:48],
                 start=True, stop=True)
        t.matmul(P.bankc(7, 1), C["ones1r32"][0:1, :], arow[0:1, 0:1],
                 start=True, stop=True)
        t.matmul(P.PSF[:, 7 * 512 + 8:7 * 512 + 16], C["ones1r32"][0:1, :],
                 mrow_sb[0:1, 0:8], start=True, stop=True).then_inc(pe.h, 1)
    v_fb = pe.inc(1)

    def vfn_f4(V):
        V.wait_ge(pe.h, v_fb)
        V.tensor_copy(facs_bc[:, 0:48], P.bankc(6, 48))
        V.tensor_copy(alpha_vec[:], P.bankc(7, 1))
        V.tensor_copy(mvec_bc[:], P.PSF[:, 7 * 512 + 8:7 * 512 + 16])
    v_facs = P.vbody(vfn_f4)
    sc_fac.close()

    sc_kv = ExitStack()
    kvb = [P.sbuf(f"kvb{i}", [128, H * 128], F32, sc_kv) for i in range(3)]
    kv_ld = []

    @P.block.gpsimd
    def _(g):
        g.wait_ge(ve.h, v_facs)
        kv_ld.append(P.mld.issue(g, kvb[0][:], kvpre[0], 0))
        kv_ld.append(P.mld.issue(g, kvb[1][:], kvpre[1], 1))
        kv_ld.append(P.sqd.issue(g, kvb[2][:], kvpre[2], 0))

    def vfn_s(V):
        V.wait_ge(ve.h, v_facs)
        V.memset(S_in[:], 0.0)
        for p in range(3):
            V.wait_ge(kv_ld[p][0].h, kv_ld[p][1])
            for hh in range(H):
                hs = slice(hh * 128, (hh + 1) * 128)
                V.scalar_tensor_tensor(S_in[:, hs], kvb[p][:, hs],
                                       facs_bc[:, hh * 3 + p:hh * 3 + p + 1],
                                       S_in[:, hs], ALU.mult, ALU.add)
        V.tensor_copy(S_in16[:], S_in[:])
    v_sin = P.vbody(vfn_s)
    sc_kv.close()

    sc_o = ExitStack()
    o_sb = OA
    qdb = [P.sbuf(f"qdb{i}", [128, 1024], BF16, sc_o) for i in range(2)]
    olb = [P.sbuf(f"olb{i}", [128, 1024], BF16, sc_o) for i in range(2)]
    corr_pe, comb_ve, qld, old = [], [], [], []
    for hh in range(H):
        bi = hh % 2

        @P.block.gpsimd
        def _(g):
            if hh >= 2:
                g.wait_ge(ve.h, comb_ve[hh - 2])
            else:
                g.wait_ge(ve.h, v_sin)
            qsrc = qdec_in[:, :, hh * 128:(hh + 1) * 128].rearrange(
                "c p j -> p c j")
            osrc = oloc_in[:, :, hh * 128:(hh + 1) * 128].rearrange(
                "c p j -> p c j")
            qld.append(P.sqd.issue(
                g, qdb[bi][:].rearrange("p (c j) -> p c j", c=CPC), qsrc, hh))
            old.append(P.sol.issue(
                g, olb[bi][:].rearrange("p (c j) -> p c j", c=CPC), osrc, hh))

        @P.block.tensor
        def _(t):
            t.wait_ge(qld[hh][0].h, qld[hh][1])
            if hh == 0:
                t.wait_ge(ve.h, v_sin)
            if hh >= 2:
                t.wait_ge(ve.h, comb_ve[hh - 2])
            bp = (hh % 2) * 2
            mm = None
            for off in (0, 512):
                mm = t.matmul(P.PSF[:, bp * 512 + off:bp * 512 + off + 512],
                              S_in16[:, hh * 128:(hh + 1) * 128],
                              qdb[bi][:, off:off + 512], start=True, stop=True)
            mm.then_inc(pe.h, 1)
        corr_pe.append(pe.inc(1))

        def vfn_cb(V, hh=hh, bi=bi):
            V.wait_ge(pe.h, corr_pe[hh])
            V.wait_ge(old[hh][0].h, old[hh][1])
            bp = (hh % 2) * 2
            V.tensor_add(o_sb[hh][:], P.PSF[:, bp * 512:bp * 512 + 1024], olb[bi][:])
            V.tensor_single_scalar(o_sb[hh][:], o_sb[hh][:], alpha_vec[:, 0:1],
                                   ALU.mult)
        comb_ve.append(P.vbody(vfn_cb))

    sc_rms = ExitStack()
    r1 = emit_rms_apply(P, C, sc_rms, o_sb, o_sb, nrmw, 1e-6, "rm1",
                        entry=[(ve, comb_ve[-1])], bank_row=4, bank_bc=6)
    r2 = emit_rms_apply(P, C, sc_rms, o_sb, o2t, pnrmw, EPS32, "rm2",
                        entry=[(ve, r1[-1])], bank_row=4, bank_bc=6)
    sc_rms.close()
    sc_o.close()
    v_o2 = r2[-1]

    sc_mp = ExitStack()
    wpt = [P.sbuf(f"wpt{i}", [128, 512], BF16, sc_mp) for i in range(2)]
    ldv = {}
    mm_pe = {}
    li = [0]

    def pload(g_or_sy_unused):
        pass

    evac = [0]
    for og in range(4):
        for kt in range(DT):
            i = li[0]
            bi = i % 2

            @P.block.sync
            def _(sy):
                if i == 0:
                    sy.wait_ge(ve.h, v_o2)
                if i >= 2:
                    sy.wait_ge(pe.h, mm_pe[i - 2])
                ldv[i] = P.wl.issue(sy, wpt[bi][:],
                                    wp_q[kt * 128:(kt + 1) * 128,
                                         og * 512:(og + 1) * 512], i)

            @P.block.tensor
            def _(t):
                t.wait_ge(ldv[i][0].h, ldv[i][1])
                if kt == 0 and evac[0]:
                    t.wait_ge(ve.h, evac[0])
                mm = None
                for k in range(4):
                    for off in (0, 512):
                        mm = t.matmul(P.PSF[:, k * 1024 + off:k * 1024 + off + 512],
                                      wpt[bi][:, k * 128:(k + 1) * 128],
                                      o2t[kt][:, off:off + 512],
                                      start=(kt == 0), stop=(kt == DT - 1))
                mm.then_inc(pe.h, 1)
            mm_pe[i] = pe.inc(1)
            li[0] += 1
        v_g = pe.n

        def vfn_pe(V, og=og, v_g=v_g):
            V.wait_ge(pe.h, v_g)
            for k in range(4):
                ot = og * 4 + k
                V.tensor_single_scalar(sbr[ot][:], P.PSF[:, k * 1024:(k + 1) * 1024],
                                       mvec_bc[:, 1:2], ALU.mult)
        evac[0] = P.vbody(vfn_pe)
    v_sbr = evac[0]
    sc_mp.close()
    sc_o2.close()

    sc_mf = ExitStack()
    stq_tot_f = []
    s23row = P.sbuf("s23row", [1, 1], F32, sc_mf)

    def vfn_s23(V):
        V.wait_ge(ve.h, v_sbr)
        V.tensor_mul(s23row[:], mrow_sb[0:1, 3:4], mrow_sb[0:1, 4:5])
    v_s23 = P.vbody(vfn_s23)

    @P.block.tensor
    def _(t):
        t.wait_ge(ve.h, v_s23)
        t.matmul(P.bankc(7, 1), C["ones1r32"][0:1, :], s23row[0:1, 0:1],
                 start=True, stop=True).then_inc(pe.h, 1)
    v_s23b = pe.inc(1)

    def vfn_s23c(V):
        V.wait_ge(pe.h, v_s23b)
        V.tensor_copy(s23_vec[:], P.bankc(7, 1))
    v_s23v = P.vbody(vfn_s23c)
    s1_vec = mvec_bc[:, 2:3]

    GW = 3
    sc_ff = ExitStack()
    w1b = [P.sbuf(f"w1b{k}", [128, GW * 128], BF16, sc_ff) for k in range(DT)]
    w3b = [P.sbuf(f"w3b{k}", [128, GW * 128], BF16, sc_ff) for k in range(DT)]
    w2b = [P.sbuf(f"w2b{l}", [128, 2048], BF16, sc_ff) for l in range(GW)]
    g16 = [P.sbuf(f"g16_{l}", [128, 1024], BF16, sc_ff) for l in range(GW)]
    silu_t = P.sbuf("silu_t", [128, 1024], BF16, sc_ff)

    groups = []
    nht = (HID + 127) // 128
    gstart = 0
    while gstart < nht:
        groups.append(list(range(gstart, min(gstart + GW, nht))))
        gstart += GW

    prev_gate = [v_s23v]
    for gi, grp in enumerate(groups):
        h0 = grp[0] * 128
        gw = sum(min(128, HID - ht * 128) for ht in grp)
        lds = []

        @P.block.gpsimd
        def _(g):
            g.wait_ge(ve.h, prev_gate[0])
            if gi == 0:
                for s_, vv in stq_tot_f:
                    g.wait_ge(s_.h, vv)
            for kt in range(DT):
                lds.append(P.mld.issue(g, w1b[kt][0:128, 0:gw],
                                       w1_q[kt * 128:(kt + 1) * 128, h0:h0 + gw], 0))
                lds.append(P.sqd.issue(g, w3b[kt][0:128, 0:gw],
                                       w3_q[kt * 128:(kt + 1) * 128, h0:h0 + gw], 0))

        ld_done = [(s, s.n) for st in (P.mld, P.sqd) for s in st.sems]

        ph_a_pe = []
        ph_a_ve = []
        for li2, ht in enumerate(grp):
            hw = min(128, HID - ht * 128)
            co = sum(min(128, HID - grp[m] * 128) for m in range(li2))
            bp = (li2 % 2) * 4

            @P.block.tensor
            def _(t):
                if li2 == 0:
                    for s_, vv in ld_done:
                        t.wait_ge(s_.h, vv)
                    t.wait_ge(v_xnld[0].h, v_xnld[1])
                if li2 >= 2:
                    t.wait_ge(ve.h, ph_a_ve[li2 - 2])
                mm = None
                for kt in range(DT):
                    for off in (0, 512):
                        t.matmul(P.PSF[0:hw, bp * 512 + off:bp * 512 + off + 512],
                                 w1b[kt][:, co:co + hw],
                                 xn16[kt][:, off:off + 512],
                                 start=(kt == 0), stop=(kt == DT - 1))
                        mm = t.matmul(
                            P.PSF[0:hw, (bp + 2) * 512 + off:(bp + 2) * 512 + off + 512],
                            w3b[kt][:, co:co + hw],
                            xn16[kt][:, off:off + 512],
                            start=(kt == 0), stop=(kt == DT - 1))
                mm.then_inc(pe.h, 1)
            ph_a_pe.append(pe.inc(1))

            def sfn_si(S_, li2=li2, hw=hw, bp=bp):
                S_.wait_ge(pe.h, ph_a_pe[li2])
                if li2 >= 1:
                    S_.wait_ge(ve.h, ph_a_ve[li2 - 1])
                S_.activation(silu_t[0:hw, :], P.PSF[0:hw, bp * 512:bp * 512 + 1024],
                              AF.Silu, scale=mvec_bc[0:hw, 2:3])
            v_si = P.sbody(sfn_si)

            def vfn_g(V, li2=li2, hw=hw, bp=bp, v_si=v_si):
                V.wait_ge(ac.h, v_si)
                V.tensor_mul(g16[li2][0:hw, :], silu_t[0:hw, :],
                             P.PSF[0:hw, (bp + 2) * 512:(bp + 2) * 512 + 1024])
            ph_a_ve.append(P.vbody(vfn_g))

        ldb = []

        @P.block.gpsimd
        def _(g):
            g.wait_ge(ve.h, ph_a_ve[-1])
            for li2, ht in enumerate(grp):
                hw = min(128, HID - ht * 128)
                ldb.append(P.sol.issue(g, w2b[li2][0:hw, :],
                                       w2_q[ht * 128:ht * 128 + hw, :], 0))
        ldb_done = [(s, s.n) for s in P.sol.sems]

        ot_pe, ot_ve = [], []
        for ot in range(DT):
            bp = (ot % 2) * 2

            @P.block.tensor
            def _(t):
                if ot == 0:
                    for s_, vv in ldb_done:
                        t.wait_ge(s_.h, vv)
                    t.wait_ge(ve.h, ph_a_ve[-1])
                if ot >= 2:
                    t.wait_ge(ve.h, ot_ve[ot - 2])
                mm = None
                for li2, ht in enumerate(grp):
                    hw = min(128, HID - ht * 128)
                    for off in (0, 512):
                        mm = t.matmul(P.PSF[:, bp * 512 + off:bp * 512 + off + 512],
                                      w2b[li2][0:hw, ot * 128:(ot + 1) * 128],
                                      g16[li2][0:hw, off:off + 512],
                                      start=(li2 == 0), stop=(li2 == len(grp) - 1))
                mm.then_inc(pe.h, 1)
            ot_pe.append(pe.inc(1))

            def vfn_oa(V, ot=ot, gi=gi, bp=bp):
                V.wait_ge(pe.h, ot_pe[ot])
                if gi == 0:
                    V.tensor_copy(o_acc[ot][:], P.PSF[:, bp * 512:bp * 512 + 1024])
                else:
                    V.tensor_add(o_acc[ot][:], o_acc[ot][:],
                                 P.PSF[:, bp * 512:bp * 512 + 1024])
            ot_ve.append(P.vbody(vfn_oa))
        prev_gate[0] = ot_ve[-1]

    v_ffn = prev_gate[0]
    sc_ff.close()
    sc_mf.close()


    sc_cb = ExitStack()
    xb = [P.sbuf(f"xb{i}", [128, 1024], F32, sc_cb) for i in range(2)]
    gt = P.sbuf("gt", [128, 1024], BF16, sc_cb)
    tmpf = P.sbuf("tmpf", [128, 1024], F32, sc_cb)
    x1st = []
    cb_ve = []
    for i in range(DT):
        bi = i % 2
        xldi = []

        @P.block.gpsimd
        def _(g):
            if i >= 2:
                g.wait_ge(ve.h, cb_ve[i - 2])
                g.wait_ge(x1st[i - 2][0].h, x1st[i - 2][1])
            xldi.append(P.mld.issue(g, xb[bi][:], xT[i * 128:(i + 1) * 128, :], i))

        def sfn_cb(S_, i=i):
            if i == 0:
                S_.wait_ge(ve.h, v_ffn)
            else:
                S_.wait_ge(ve.h, cb_ve[i - 1])
            S_.activation(gt[:], o_acc[i][:], AF.Silu, scale=s23_vec[:, 0:1])
        v_gt = P.sbody(sfn_cb)

        def vfn_cb2(V, i=i, bi=bi, v_gt=v_gt, xldi=xldi):
            V.wait_ge(ac.h, v_gt)
            V.wait_ge(xldi[0][0].h, xldi[0][1])
            V.tensor_mul(tmpf[:], gt[:], sbr[i][:])
            V.tensor_add(xb[bi][:], xb[bi][:], tmpf[:])
        cb_ve.append(P.vbody(vfn_cb2))

        @P.block.gpsimd
        def _(g):
            g.wait_ge(ve.h, cb_ve[i])
            x1st.append(P.stx.issue(g, x1_out[i * 128:(i + 1) * 128, :], xb[bi][:], i))
    x1_totals = [(s, s.n) for s in P.stx.sems]
    sc_cb.close()

    sc_ln = ExitStack()
    alloc_pools(P, sc_ln)
    xn_ready, _hr, _st = emit_ln_h(P, C, sc_ln, x1_out, None, None, "ln1",
                                   xn_dram=xn1_out, entry=x1_totals + [(ve, cb_ve[-1])],
                                   with_h=False)
    sc_ln.close()
    sc_main.close()
    P.default_sc = None


def build_l0b():
    P = LB("l0b").start()
    io = {
        "xT": P.din("xT", [DM, TOK], F32).ap(),
        "xn0": P.din("xnT", [DM, TOK], BF16).ap(),
        "oloc": P.din("oloc_in", [CPC, 128, H * 128], BF16).ap(),
        "qdec": P.din("qdec_in", [CPC, 128, H * 128], BF16).ap(),
        "kvall": P.din("kvpre", [3, 128, H * 128], F32).ap(),
        "wfac_row": P.din("wfac_row", [1, 3], F32).ap(),
        "gap_row": P.din("gap_row", [1, 3], F32).ap(),
        "mq": P.din("mq_in", [1, 1], F32).ap(),
        "s_col": P.din("s_col", [H, 1], F32).ap(),
        "normw_in": P.din("normw_in", [DM, 1], F32).ap(),
        "pnormw_in": P.din("pnormw_in", [DM, 1], F32).ap(),
        "prj_ag": P.din("prj_ag", [DM, DM], BF16).ap(),
        "w1_ag": P.din("w1_ag", [DM, HID], BF16).ap(),
        "w3_ag": P.din("w3_ag", [DM, HID], BF16).ap(),
        "w2_ag": P.din("w2_ag", [8 * 683, DM], BF16).ap(),
        "mrow": P.din("mrow", [1, 8], F32).ap(),
        "x1": P.dout("x1_out", [DM, TOK], F32).ap(),
        "xn1": P.dout("xn1_out", [DM, TOK], BF16).ap(),
    }
    C = emit_consts(P)
    emit_l0b(P, C, io)

    @P.block.gpsimd
    def _(g):
        P.wait_all_streams(g)

    @P.block.sync
    def _(sy):
        P.wait_all_streams(sy)

    return P.finish()



def emit_l1q(P, C, io):
    P.tag = "q"
    sc_main = ExitStack()
    P.default_sc = sc_main
    xnT = io["xn1"]
    wattT = io["watt_ag"]
    pos_row = io["pos_row"]
    qkvpub = io["qkvpub"]

    pe, ac, ve, gp = (P.sem(s) for s in ("pe", "ac", "ve", "gp"))

    xn16 = [P.sbuf(f"xn16_{i}", [128, 1024], BF16) for i in range(DT)]
    Q16 = [P.sbuf(f"Q16_{h}", [128, 1024], BF16) for h in range(H)]
    K16 = [P.sbuf(f"K16_{h}", [128, 1024], BF16) for h in range(H)]
    V16 = [P.sbuf(f"V16_{c}", [128, 2048], BF16) for c in range(CPC)]
    wqt = [P.sbuf(f"wqt{i}", [128, 512], BF16) for i in range(2)]
    cos_t = P.sbuf("cos_t", [128, 1024], F32)
    sin_sgn = P.sbuf("sin_sgn", [128, 1024], F32)
    swapm16 = P.sbuf("swapm16", [128, 128], BF16)
    tmpA = P.sbuf("tmpA", [128, 1024], F32)
    tmpB = P.sbuf("tmpB", [128, 1024], F32)
    posr = P.sbuf("posr", [1, TOK], F32)

    @P.block.gpsimd
    def _(g):
        for i in range(DT):
            P.sxn.issue(g, xn16[i][:], xnT[i * 128:(i + 1) * 128, :])
        P.srp.issue(g, posr[:], pos_row)
    v_xnld = (P.sxn.sems[0], P.sxn.sems[0].n)
    v_pos = (P.srp.sems[0], P.srp.sems[0].n)

    rowscr = P.dram("rowscr", [1, 256], F32).ap()
    sc_r = ExitStack()
    idxrow = P.sbuf("idxrow", [1, 128], F32, sc_r)
    parrow = P.sbuf("parrow", [1, 128], F32, sc_r)
    i2 = P.sbuf("i2", [128, 1], F32, sc_r)
    theta = P.sbuf("theta", [128, 1], F32, sc_r)
    e_col = P.sbuf("e_col", [128, 1], F32, sc_r)
    ev_col = P.sbuf("ev_col", [128, 1], F32, sc_r)
    sgn_col = P.sbuf("sgn_col", [128, 1], F32, sc_r)
    ang = P.sbuf("ang", [128, 1024], F32, sc_r)
    zf = P.sbuf("zf", [128, 1024], F32, sc_r)
    ibig = P.sbuf("ibig", [128, 1024], mybir.dt.int32, sc_r)
    gtp = P.sbuf("gtp", [128, 1024], F32, sc_r)
    a1t = P.sbuf("a1t", [128, 128], F32, sc_r)
    a2t = P.sbuf("a2t", [128, 128], F32, sc_r)

    def gfn(g, G):
        G.iota(idxrow[:], pattern=[[1, 64], [0, 2]], base=0, channel_multiplier=0,
               allow_small_or_imprecise_dtypes=True)
        G.iota(parrow[:], pattern=[[0, 64], [1, 2]], base=0, channel_multiplier=0,
               allow_small_or_imprecise_dtypes=True)
    v_pcol = P.gbody(gfn)

    @P.block.gpsimd
    def _(g):
        g.wait_ge(gp.h, v_pcol)
        P.srr.issue(g, rowscr[0:1, 0:128], idxrow[:])
        P.srr.issue(g, rowscr[0:1, 128:256], parrow[:])
    v_rs1 = (P.srr.sems[0], P.srr.sems[0].n)

    @P.block.gpsimd
    def _(g):
        g.wait_ge(v_rs1[0].h, v_rs1[1])
        P.srr.issue(g, i2[:], rowscr[0:1, 0:128])
        P.srr.issue(g, e_col[:], rowscr[0:1, 128:256])
    v_rs2 = (P.srr.sems[0], P.srr.sems[0].n)

    def vfn_th(V):
        V.wait_ge(v_rs2[0].h, v_rs2[1])
        V.tensor_scalar(sgn_col[:], e_col[:], 2.0, 1.0, ALU.mult, ALU.subtract)
        V.tensor_scalar(ev_col[:], e_col[:], -1.0, 1.0, ALU.mult, ALU.add)
    v_th1 = P.vbody(vfn_th)

    def sfn_th(S_):
        S_.wait_ge(ve.h, v_th1)
        S_.activation(theta[:], i2[:], AF.Exp,
                      scale=-2.0 * math.log(10000.0) / HD)
    v_th2 = P.sbody(sfn_th)

    @P.block.tensor
    def _(t):
        t.wait_ge(v_pos[0].h, v_pos[1])
        mm = None
        for off in (0, 512):
            mm = t.matmul(P.PSF[:, off:off + 512], C["ones1r32"][0:1, :],
                          posr[0:1, off:off + 512], start=True, stop=True)
        mm.then_inc(pe.h, 1)
    v_pb = pe.inc(1)

    TWO_PI = 2.0 * math.pi

    def emit_trig(dst, phase_add, tag, prev_ac=None):
        def vfn(V):
            V.wait_ge(pe.h, v_pb)
            V.wait_ge(ac.h, v_th2)
            if prev_ac is not None:
                V.wait_ge(ac.h, prev_ac)
            V.tensor_single_scalar(ang[:], P.PSF[:, 0:1024], theta[:, 0:1], ALU.mult)
            if phase_add:
                V.tensor_single_scalar(ang[:], ang[:], phase_add, ALU.add)
            V.tensor_single_scalar(zf[:], ang[:], 1.0 / TWO_PI, ALU.mult)
            V.tensor_copy(ibig[:], zf[:])
            V.tensor_copy(zf[:], ibig[:])
            V.scalar_tensor_tensor(ang[:], zf[:], -TWO_PI, ang[:], ALU.mult, ALU.add)
            V.tensor_single_scalar(gtp[:], ang[:], math.pi, ALU.is_gt)
            V.scalar_tensor_tensor(ang[:], gtp[:], -TWO_PI, ang[:], ALU.mult, ALU.add)
        v1 = P.vbody(vfn)

        def sfn(S_):
            S_.wait_ge(ve.h, v1)
            S_.activation(dst[:], ang[:], AF.Sin)
        return P.sbody(sfn)

    v_sin = emit_trig(sin_sgn, 0.0, "s")
    v_cos = emit_trig(cos_t, math.pi / 2.0, "c", prev_ac=v_sin)

    def vfn_sg(V):
        V.wait_ge(ac.h, v_cos)
        V.tensor_single_scalar(sin_sgn[:], sin_sgn[:], sgn_col[:, 0:1], ALU.mult)
        V.tensor_single_scalar(a1t[:], C["iota_pf"][:], -1.0, ALU.is_equal)
        V.tensor_single_scalar(a2t[:], C["iota_pf"][:], 1.0, ALU.is_equal)
        V.tensor_single_scalar(a1t[:], a1t[:], ev_col[:, 0:1], ALU.mult)
        V.tensor_single_scalar(a2t[:], a2t[:], e_col[:, 0:1], ALU.mult)
        V.tensor_add(swapm16[:], a1t[:], a2t[:])
    v_rope_tabs = P.vbody(vfn_sg)
    sc_r.close()

    li = [0]
    ldv = {}
    mm_pe = {}

    def wload(base_f, kt):
        i = li[0]
        bi = i % 2

        @P.block.gpsimd
        def _(g):
            if i == 0:
                g.wait_ge(v_xnld[0].h, v_xnld[1])
                g.wait_ge(ve.h, v_rope_tabs)
            if i >= 2:
                g.wait_ge(pe.h, mm_pe[i - 2])
            ldv[i] = P.wlg.issue(g, wqt[bi][:],
                                 wattT[kt * 128:(kt + 1) * 128, base_f:base_f + 512], i)
        return i, bi

    evac = [0]
    for vg in range(4):
        base_f = 2 * DM + vg * 512
        for kt in range(DT):
            i, bi = wload(base_f, kt)

            @P.block.tensor
            def _(t):
                t.wait_ge(ldv[i][0].h, ldv[i][1])
                if kt == 0 and evac[0]:
                    t.wait_ge(ve.h, evac[0])
                mm = None
                for c in range(CPC):
                    mm = t.matmul(P.bank(c), xn16[kt][:, c * 128:(c + 1) * 128],
                                  wqt[bi][:, 0:512],
                                  start=(kt == 0), stop=(kt == DT - 1))
                mm.then_inc(pe.h, 1)
            mm_pe[i] = pe.inc(1)
            li[0] += 1
        v_vg = pe.n

        def vfn_vev(V, vg=vg, v_vg=v_vg):
            V.wait_ge(pe.h, v_vg)
            for c in range(CPC):
                V.tensor_copy(V16[c][:, vg * 512:(vg + 1) * 512], P.bank(c))
        evac[0] = P.vbody(vfn_vev)
    v_vdone = evac[0]

    @P.block.gpsimd
    def _(g):
        g.wait_ge(ve.h, v_vdone)
        for c in range(CPC):
            for hh in range(H):
                P.svo.issue(g,
                            qkvpub[4096 + hh * 128:4096 + (hh + 1) * 128,
                                   c * 128:(c + 1) * 128],
                            V16[c][:, hh * 128:(hh + 1) * 128], c)

    if "qkvall" in io:
        cc = P.sem("cc")
        _vg = [(s.h, s.n) for s in P.svo.sems if s.n]

        @P.block.gpsimd
        def _(g, _vg=_vg):
            for h_, v_ in _vg:
                g.wait_ge(h_, v_)
            g.collective_compute(
                "AllGather", ALU.bypass, replica_groups=io["cc_groups"],
                ins=[qkvpub[4096:6144, :]],
                outs=[io["qkvall"][0:8192, :]]).then_inc(cc.h, 1)
        cc.inc(1)

    for role in range(2):
        for grp in range(4):
            base_f = role * DM + grp * 512
            for kt in range(DT):
                i, bi = wload(base_f, kt)

                @P.block.tensor
                def _(t):
                    t.wait_ge(ldv[i][0].h, ldv[i][1])
                    if kt == 0 and evac[0]:
                        t.wait_ge(ve.h, evac[0])
                    mm = None
                    for k in range(4):
                        for off in (0, 512):
                            mm = t.matmul(
                                P.PSF[:, k * 1024 + off:k * 1024 + off + 512],
                                wqt[bi][:, k * 128:(k + 1) * 128],
                                xn16[kt][:, off:off + 512],
                                start=(kt == 0), stop=(kt == DT - 1))
                    mm.then_inc(pe.h, 1)
                mm_pe[i] = pe.inc(1)
                li[0] += 1
            v_g = pe.n

            def vfn_ev(V, grp=grp, role=role, v_g=v_g):
                V.wait_ge(pe.h, v_g)
                for k in range(4):
                    hh = grp * 4 + k
                    dst = Q16[hh] if role == 0 else K16[hh]
                    V.tensor_copy(dst[:], P.PSF[:, k * 1024:(k + 1) * 1024])
            evac[0] = P.vbody(vfn_ev)
    v_qkv = evac[0]

    st_prev = {}
    rope_ve = []
    it = [0]
    for grp4 in range(4):
        for role in range(2):
          for k4 in range(4):
            hh = grp4 * 4 + k4
            i = it[0]
            bp = (i % 2) * 2
            src_t = Q16[hh] if role == 0 else K16[hh]
            dst_rows = (0 if role == 0 else 2048) + hh * 128

            @P.block.tensor
            def _(t):
                if i == 0:
                    t.wait_ge(ve.h, v_qkv)
                    t.wait_ge(ve.h, v_rope_tabs)
                if i >= 2:
                    t.wait_ge(ve.h, rope_ve[i - 2])
                mm = None
                for off in (0, 512):
                    mm = t.matmul(P.PSF[:, bp * 512 + off:bp * 512 + off + 512],
                                  swapm16[:], src_t[:, off:off + 512],
                                  start=True, stop=True)
                mm.then_inc(pe.h, 1)
            v_sw = pe.inc(1)

            def vfn_r(V, i=i, bp=bp, src_t=src_t, v_sw=v_sw):
                V.wait_ge(pe.h, v_sw)
                if i >= 2 and (i - 2) in st_prev:
                    V.wait_ge(st_prev[i - 2][0].h, st_prev[i - 2][1])
                V.tensor_mul(tmpA[:], P.PSF[:, bp * 512:bp * 512 + 1024], sin_sgn[:])
                V.tensor_mul(tmpB[:], src_t[:], cos_t[:])
                V.tensor_add(src_t[:], tmpA[:], tmpB[:])
            rope_ve.append(P.vbody(vfn_r))

            @P.block.gpsimd
            def _(g):
                g.wait_ge(ve.h, rope_ve[i])
                st_prev[i] = P.sqo.issue(g, qkvpub[dst_rows:dst_rows + 128, :],
                                         src_t[:], i)
            it[0] += 1

            if k4 == 3 and "qkvall" in io:
                cc = P.sem("cc")
                _pg = [(s.h, s.n) for s in P.sqo.sems if s.n]
                src_r = (0 if role == 0 else 2048) + grp4 * 512
                dst_r = (8192 if role == 0 else 16384) + grp4 * 2048

                @P.block.gpsimd
                def _(g, _pg=_pg, src_r=src_r, dst_r=dst_r):
                    for h_, v_ in _pg:
                        g.wait_ge(h_, v_)
                    g.collective_compute(
                        "AllGather", ALU.bypass, replica_groups=io["cc_groups"],
                        ins=[qkvpub[src_r:src_r + 512, :]],
                        outs=[io["qkvall"][dst_r:dst_r + 2048, :]]).then_inc(
                            cc.h, 1)
                cc.inc(1)

    sc_main.close()
    P.default_sc = None


def build_l1q():
    P = LB("l1q").start()
    io = {
        "xn1": P.din("xnT", [DM, TOK], BF16).ap(),
        "watt_ag": P.din("wattT", [DM, 3 * DM], BF16).ap(),
        "pos_row": P.din("pos_row", [1, TOK], F32).ap(),
        "qkvpub": P.dout("qkvpub", [3 * DM, TOK], BF16).ap(),
    }
    C = emit_consts(P)
    emit_l1q(P, C, io)

    @P.block.gpsimd
    def _(g):
        P.wait_all_streams(g)

    @P.block.sync
    def _(sy):
        P.wait_all_streams(sy)

    return P.finish()



def emit_l1a(P, C, io):
    P.tag = "t"
    sc_main = ExitStack()
    P.default_sc = sc_main
    qkvall = io["qkvall"]
    idx_in = io["idx_l1a"]
    ypub = io["ypub"]

    pe, ac, ve, gp = (P.sem(s) for s in ("pe", "ac", "ve", "gp"))
    idxsb = P.sbuf("idxsb", [128, 48], mybir.dt.int32)

    @P.block.gpsimd
    def _(g):
        P.srr.issue(g, idxsb[:], idx_in)
    v_idx = (P.srr.sems[0], P.srr.sems[0].n)

    qh = [P.sbuf(f"qh{i}", [128, N], BF16) for i in range(2)]
    kh = [P.sbuf(f"kh{i}", [128, N], BF16) for i in range(2)]
    vh = [P.sbuf(f"vh{i}", [128, NCH * 128], BF16) for i in range(2)]
    pstr = [P.sbuf(f"pstr{i}", [128, 1024], BF16) for i in range(2)]
    zc16 = P.sbuf("zc16", [128, 128], BF16)
    rs_row = P.sbuf("rs_row", [1, 1024], F32)
    rb = P.sbuf("rb", [128, 1024], F32)
    ytile = [P.sbuf(f"ytile{i}", [128, 1024], BF16) for i in range(2)]

    def gfn_z(g, G):
        G.memset(zc16[:], 0.0)
    v_z = P.gbody(gfn_z)

    @P.block.tensor
    def _(t):
        t.wait_ge(P.sem("gp").h, v_z)

    bh_done = []
    ld_bh = []
    y_stores = {}

    for bh in range(4):
        p = bh % 2

        @P.block.gpsimd
        def _(g):
            if bh == 0:
                g.wait_ge(v_idx[0].h, v_idx[1])
            if bh >= 2:
                g.wait_ge(ve.h, bh_done[bh - 2])
            l1 = l2 = l3 = None
            for j2 in range(4):
                cidx = bh * 12 + 0 * 4 + j2
                g.indirect_dma_start(
                    qh[p][:, j2 * 1024:(j2 + 1) * 1024], None, qkvall,
                    IndirectOffsetOnAxis(idxsb[:, cidx:cidx + 1], 0)
                ).then_inc(P.lq.sems[bh % 2].h, 16)
                l1 = (P.lq.sems[bh % 2], P.lq.sems[bh % 2].inc(16))
                cidx = bh * 12 + 1 * 4 + j2
                g.indirect_dma_start(
                    kh[p][:, j2 * 1024:(j2 + 1) * 1024], None, qkvall,
                    IndirectOffsetOnAxis(idxsb[:, cidx:cidx + 1], 0)
                ).then_inc(P.lk.sems[bh % 2].h, 16)
                l2 = (P.lk.sems[bh % 2], P.lk.sems[bh % 2].inc(16))
                cidx = bh * 12 + 2 * 4 + j2
                g.indirect_dma_start(
                    vh[p][:, j2 * 1024:(j2 + 1) * 1024], None, qkvall,
                    IndirectOffsetOnAxis(idxsb[:, cidx:cidx + 1], 0)
                ).then_inc(P.lv.sems[bh % 2].h, 16)
                l3 = (P.lv.sems[bh % 2], P.lv.sems[bh % 2].inc(16))
            ld_bh.append((l1, l2, l3))

        for tblk in range(4):
            nkc = 8 * tblk + 8
            kc_pe, kc_ac, kc_ve = [], [], []
            for kc in range(nkc):
                sb = (kc % 2) * 2
                swid = 1024 if kc < 8 * tblk else 1024 - 128 * (kc - 8 * tblk)
                toff = 1024 * tblk + (1024 - swid)
                loff = 1024 - swid
                diag = kc >= 8 * tblk

                @P.block.tensor
                def _(t):
                    if kc == 0:
                        l1, l2, l3 = ld_bh[bh]
                        t.wait_ge(l1[0].h, l1[1])
                        t.wait_ge(l2[0].h, l2[1])
                        t.wait_ge(l3[0].h, l3[1])
                        if tblk == 0 and bh_done:
                            t.wait_ge(ve.h, bh_done[-1])
                        if tblk > 0:
                            t.wait_ge(ve.h, tblk_gate[0])
                    if kc >= 2:
                        t.wait_ge(ac.h, kc_ac[kc - 2])
                        if (kc - 2) < len(kc_ve) and kc_ve[kc - 2]:
                            t.wait_ge(ve.h, kc_ve[kc - 2])
                        else:
                            t.wait_ge(pe.h, kc_pe[kc - 2] + 2)
                    mm = None
                    for off in range(0, swid, 512):
                        w = min(512, swid - off)
                        mm = t.matmul(P.PSF[:, sb * 512 + off:sb * 512 + off + w],
                                      kh[p][:, kc * 128:(kc + 1) * 128],
                                      qh[p][:, toff + off:toff + off + w],
                                      start=True, stop=True)
                    mm.then_inc(pe.h, 1)
                kc_pe.append(pe.inc(1))

                def sfn_e(S_, kc=kc, sb=sb, swid=swid, p=p):
                    S_.wait_ge(pe.h, kc_pe[kc])
                    S_.activation(pstr[kc % 2][:, 0:swid],
                                  P.PSF[:, sb * 512:sb * 512 + swid],
                                  AF.Exp, scale=INV_SQRT_HD)
                kc_ac.append(P.sbody(sfn_e))

                if diag:
                    def vfn_d(V, kc=kc, p=p):
                        V.wait_ge(ac.h, kc_ac[kc])
                        V.tensor_mul(pstr[kc % 2][:, 0:128], pstr[kc % 2][:, 0:128],
                                     C["trilT"][:])
                    kc_ve.append(P.vbody(vfn_d))
                else:
                    kc_ve.append(0)

                @P.block.tensor
                def _(t):
                    if kc_ve[kc]:
                        t.wait_ge(ve.h, kc_ve[kc])
                    else:
                        t.wait_ge(ac.h, kc_ac[kc])
                    mm = None
                    lo = loff
                    while lo < 1024:
                        w = min(512 - (lo % 512), 1024 - lo)
                        t.matmul(P.PSF[0:1, 3072 + lo:3072 + lo + w],
                                 C["ones16"][:, 0:1],
                                 pstr[kc % 2][:, lo - loff:lo - loff + w],
                                 start=(kc == 0), stop=False)
                        mm = t.matmul(
                            P.PSF[:, 2048 + lo:2048 + lo + w],
                            vh[p][:, kc * 128:(kc + 1) * 128],
                            pstr[kc % 2][:, lo - loff:lo - loff + w],
                            start=(kc == 0), stop=False)
                        lo += w
                    mm.then_inc(pe.h, 2)
                pe.inc(2)

            @P.block.tensor
            def _(t):
                mm = None
                for off in (0, 512):
                    t.matmul(P.PSF[0:1, 3072 + off:3072 + off + 512],
                             zc16[:, 0:1], pstr[0][:, off:off + 512],
                             start=False, stop=True)
                    mm = t.matmul(P.PSF[:, 2048 + off:2048 + off + 512],
                                  zc16[:], pstr[0][:, off:off + 512],
                                  start=False, stop=True)
                mm.then_inc(pe.h, 1)
            pe.inc(1)
            v_last = pe.n

            def vfn_rs(V, v_last=v_last):
                V.wait_ge(pe.h, v_last)
                V.tensor_copy(rs_row[:], P.PSF[0:1, 3072:4096])
            v_rs = P.vbody(vfn_rs)

            def sfn_rs(S_):
                S_.wait_ge(ve.h, v_rs)
                S_.activation(rs_row[:], rs_row[:], AF.Ln)
                S_.activation(rs_row[:], rs_row[:], AF.Exp, scale=-1.0)
            v_rc = P.sbody(sfn_rs)

            @P.block.tensor
            def _(t):
                t.wait_ge(ac.h, v_rc)
                mm = None
                for off in (0, 512):
                    mm = t.matmul(P.PSF[:, off:off + 512], C["ones1r32"][0:1, :],
                                  rs_row[0:1, off:off + 512], start=True, stop=True)
                mm.then_inc(pe.h, 1)
            v_rb = pe.inc(1)

            yst = {}

            def vfn_y(V, tblk=tblk, p=p, v_rb=v_rb, bh=bh):
                V.wait_ge(pe.h, v_rb)
                key = tblk % 2
                if key in y_stores:
                    V.wait_ge(y_stores[key][0].h, y_stores[key][1])
                V.tensor_copy(rb[:], P.PSF[:, 0:1024])
                V.tensor_mul(ytile[tblk % 2][:], P.PSF[:, 2048:3072], rb[:])
            v_y = P.vbody(vfn_y)
            tblk_gate = [v_y]

            @P.block.gpsimd
            def _(g):
                g.wait_ge(ve.h, v_y)
                rr = (bh * 4 + tblk) * 128
                y_stores[tblk % 2] = P.sy.issue(
                    g, ypub[rr:rr + 128, :], ytile[tblk % 2][:], tblk)
        bh_done.append(tblk_gate[0])

        if "ypc" in io:
            cc = P.sem("cc")
            st_gates = [(s.h, s.n) for s in P.sy.sems if s.n]

            @P.block.gpsimd
            def _(g, st_gates=st_gates, bh=bh):
                for h_, v_ in st_gates:
                    g.wait_ge(h_, v_)
                g.collective_compute(
                    "AllGather", ALU.bypass, replica_groups=io["cc_groups"],
                    ins=[ypub[bh * 512:(bh + 1) * 512, :]],
                    outs=[io["ypc"][bh]]).then_inc(cc.h, 1)
            cc.inc(1)

    sc_main.close()
    P.default_sc = None


def build_l1a():
    P = LB("l1a").start()
    io = {
        "qkvall": P.din("qkvall", [4 * 3 * DM, TOK], BF16).ap(),
        "idx_l1a": P.din("idx_l1a", [128, 48], mybir.dt.int32).ap(),
        "ypub": P.dout("ypub", [2048, 1024], BF16).ap(),
    }
    C = emit_consts(P)
    emit_l1a(P, C, io)

    @P.block.gpsimd
    def _(g):
        P.wait_all_streams(g)

    @P.block.sync
    def _(sy_):
        P.wait_all_streams(sy_)

    return P.finish()



def emit_l1f(P, C, io):
    P.tag = "f"
    sc_main = ExitStack()
    P.default_sc = sc_main
    xn1T = io["xn1"]
    ypc = io["ypc"]
    idx_y = io["idx_y"]
    x1T = io["x1"]
    wprojT = io["wp1_ag"]
    wffnT = io["wf_ag"]
    bvec = io["bvec"]
    AT = io["AT"]
    BT = io["BT"]
    gwT = io["gwT"]
    gb_row = io["gb_row"]
    x2_out = io["x2"]

    pe, ac, ve, gp = (P.sem(s) for s in ("pe", "ac", "ve", "gp"))
    idxy = P.sbuf("idxy", [128, 16], mybir.dt.int32)

    xn16 = [P.sbuf(f"xn16_{i}", [128, 1024], BF16) for i in range(DT)]
    y16 = [P.sbuf(f"y16_{i}", [128, 1024], BF16) for i in range(DT)]
    sbr = [P.sbuf(f"sbr_{i}", [128, 1024], BF16) for i in range(DT)]
    nb = P.sbuf("nb", [128, DT], F32)
    BT16 = P.sbuf("BT16", [128, DM], BF16)
    xA16 = P.sbuf("xA16", [128, 1024], BF16)
    xAw = P.sbuf("xAw", [128, 1024], BF16)
    wqt = [P.sbuf(f"wqt{i}", [128, 512], BF16) for i in range(2)]

    sc_g = ExitStack()
    gw16 = [P.sbuf(f"gw16_{k}", [128, E], BF16, sc_g) for k in range(DT)]
    at16 = [P.sbuf(f"at16_{k}", [128, E * RLO], BF16, sc_g) for k in range(2)]
    lg = P.sbuf("lg", [128, 64], F32, sc_g)
    gb_b = P.sbuf("gb_b", [128, E], F32, sc_g)
    gbr = P.sbuf("gbr", [1, E], F32, sc_g)
    m1c = P.sbuf("m1c", [128, 8], F32, sc_g)
    m2c = P.sbuf("m2c", [128, 8], F32, sc_g)
    eq1 = P.sbuf("eq1", [128, E], F32, sc_g)
    eq2 = P.sbuf("eq2", [128, E], F32, sc_g)
    mskd = P.sbuf("mskd", [128, E], F32, sc_g)
    g1s = P.sbuf("g1s", [128, 8], F32, sc_g)
    g2s = P.sbuf("g2s", [128, 8], F32, sc_g)
    wd = P.sbuf("wd", [128, 64], F32, sc_g)
    wfm = P.sbuf("wfm", [8, 1024], F32, sc_g)
    ident32 = P.sbuf("ident32", [128, 128], F32, sc_g)
    expd = P.sbuf("expd", [8, 128], F32, sc_g)
    exp2 = P.sbuf("exp2", [8, 128], F32, sc_g)

    @P.block.gpsimd
    def _(g):
        _iv = P.srr.issue(g, idxy[:], idx_y)
        g.wait_ge(_iv[0].h, _iv[1])
        for i in range(DT):
            P.sxn.issue(g, xn16[i][:], xn1T[i * 128:(i + 1) * 128, :])
        for i in range(H):
            g.indirect_dma_start(
                y16[i][:], None, ypc[i % 4],
                IndirectOffsetOnAxis(idxy[:, i:i + 1], 0)
            ).then_inc(P.syl.sems[0].h, 16)
            P.syl.sems[0].inc(16)
        for i in range(DT):
            P.sml.issue(g, nb[:, i:i + 1], bvec[i * 128:(i + 1) * 128, 0:1])
        P.sml.issue(g, BT16[:], BT)
        P.sml.issue(g, gbr[:], gb_row)
        for k in range(DT):
            P.sml.issue(g, gw16[k][:], gwT[k * 128:(k + 1) * 128, :])
    v_xn = (P.sxn.sems[0], P.sxn.sems[0].n)
    v_y = (P.syl.sems[0], P.syl.sems[0].n)
    v_ml = (P.sml.sems[0], P.sml.sems[0].n)
    v_gl = v_ml

    def gfn_e(g, G):
        G.iota(expd[:], pattern=[[1, 128]], base=0, channel_multiplier=-16,
               allow_small_or_imprecise_dtypes=True)
    v_ei = P.gbody(gfn_e)

    def vfn_e(V):
        V.wait_ge(gp.h, v_ei)
        V.tensor_single_scalar(ident32[:], C["iota_pf"][:], 0.0, ALU.is_equal)
        V.tensor_single_scalar(exp2[:], expd[:], 0.0, ALU.is_ge)
        V.tensor_single_scalar(expd[:], expd[:], 16.0, ALU.is_lt)
        V.tensor_mul(expd[:], expd[:], exp2[:])
    v_em = P.vbody(vfn_e)

    @P.block.tensor
    def _(t):
        t.wait_ge(v_gl[0].h, v_gl[1])
        t.wait_ge(v_xn[0].h, v_xn[1])
        mm = None
        for kt in range(DT):
            for tt in range(8):
                mm = t.matmul(P.PSF[:, tt * 512:tt * 512 + E],
                              xn16[kt][:, tt * 128:(tt + 1) * 128], gw16[kt][:],
                              start=(kt == 0), stop=(kt == DT - 1))
        mm.then_inc(pe.h, 1)
    v_lg = pe.inc(1)

    def vfn_lgc(V):
        V.wait_ge(pe.h, v_lg)
        for tt in range(8):
            V.tensor_copy(lg[:, tt * 8:(tt + 1) * 8], P.PSF[:, tt * 512:tt * 512 + E])
    v_lgc = P.vbody(vfn_lgc)

    @P.block.tensor
    def _(t):
        t.wait_ge(ve.h, v_lgc)
        t.matmul(P.bankc(7, E), C["ones1r32"][0:1, :], gbr[0:1, 0:E],
                 start=True, stop=True).then_inc(pe.h, 1)
    v_gbb = pe.inc(1)

    def vfn_lg(V):
        V.wait_ge(pe.h, v_gbb)
        V.tensor_copy(gb_b[:], P.bankc(7, E))
        for tt in range(8):
            sl = slice(tt * 8, (tt + 1) * 8)
            V.tensor_add(lg[:, sl], lg[:, sl], gb_b[:])
            V.tensor_reduce(m1c[:, tt:tt + 1], lg[:, sl], AX, ALU.max)
            V.tensor_single_scalar(eq1[:], lg[:, sl], m1c[:, tt:tt + 1], ALU.is_equal)
            V.scalar_tensor_tensor(mskd[:], eq1[:], -1e30, lg[:, sl],
                                   ALU.mult, ALU.add)
            V.tensor_reduce(m2c[:, tt:tt + 1], mskd[:], AX, ALU.max)
            V.tensor_single_scalar(eq2[:], mskd[:], m2c[:, tt:tt + 1], ALU.is_equal)
            V.tensor_copy(wd[:, sl], eq1[:])
            V.tensor_copy(mskd[:], eq2[:])
            V.tensor_copy(eq2[:], mskd[:])
            V.tensor_copy(lg[:, sl], eq2[:])
        V.tensor_sub(m1c[:], m1c[:], m2c[:])
    v_top = P.vbody(vfn_lg)

    def sfn_sig(S_):
        S_.wait_ge(ve.h, v_top)
        S_.activation(g1s[:], m1c[:], AF.Sigmoid)
    v_sig = P.sbody(sfn_sig)

    def vfn_wd(V):
        V.wait_ge(ac.h, v_sig)
        V.tensor_scalar(g2s[:], g1s[:], -1.0, 1.0, ALU.mult, ALU.add)
        for tt in range(8):
            sl = slice(tt * 8, (tt + 1) * 8)
            V.tensor_single_scalar(wd[:, sl], wd[:, sl], g1s[:, tt:tt + 1], ALU.mult)
            V.tensor_single_scalar(lg[:, sl], lg[:, sl], g2s[:, tt:tt + 1], ALU.mult)
            V.tensor_add(wd[:, sl], wd[:, sl], lg[:, sl])
    v_wd = P.vbody(vfn_wd)

    tr_pe = []
    for tt in range(8):
        @P.block.tensor
        def _(t):
            if tt == 0:
                t.wait_ge(ve.h, v_wd)
                t.wait_ge(ve.h, v_em)
            t.matmul(P.PSF[0:8, 5 * 512 + tt * 128:5 * 512 + (tt + 1) * 128],
                     wd[:, tt * 8:(tt + 1) * 8], ident32[:],
                     start=True, stop=True).then_inc(pe.h, 1)
        tr_pe.append(pe.inc(1))

    def vfn_wf(V):
        V.wait_ge(pe.h, tr_pe[-1])
        for tt in range(8):
            V.tensor_copy(wfm[:, tt * 128:(tt + 1) * 128],
                          P.PSF[0:8, 5 * 512 + tt * 128:5 * 512 + (tt + 1) * 128])
    v_wf = P.vbody(vfn_wf)

    xal = []
    xa_pe = []
    for kt in range(DT):
        @P.block.gpsimd
        def _(g):
            if kt == 0:
                g.wait_ge(v_xn[0].h, v_xn[1])
            if kt >= 2:
                g.wait_ge(pe.h, xa_pe[kt - 2])
            xal.append(P.sxl.issue(g, at16[kt % 2][:],
                                   AT[kt * 128:(kt + 1) * 128, :], kt))

        @P.block.tensor
        def _(t):
            t.wait_ge(xal[kt][0].h, xal[kt][1])
            if kt == 0:
                t.wait_ge(ve.h, v_lgc)
            mm = None
            for off in (0, 512):
                mm = t.matmul(P.PSF[:, off:off + 512],
                              at16[kt % 2][:], xn16[kt][:, off:off + 512],
                              start=(kt == 0), stop=(kt == DT - 1))
            mm.then_inc(pe.h, 1)
        xa_pe.append(pe.inc(1))

    def vfn_xa(V):
        V.wait_ge(pe.h, xa_pe[-1])
        V.tensor_copy(xA16[:], P.PSF[:, 0:1024])
    v_xa = P.vbody(vfn_xa)

    @P.block.tensor
    def _(t):
        t.wait_ge(ve.h, v_wf)
        t.wait_ge(ve.h, v_xa)
        mm = None
        for off in (0, 512):
            mm = t.matmul(P.PSF[:, 1024 + off:1024 + off + 512], expd[0:8, :],
                          wfm[0:8, off:off + 512], start=True, stop=True)
        mm.then_inc(pe.h, 1)
    v_we = pe.inc(1)

    def vfn_xaw(V):
        V.wait_ge(pe.h, v_we)
        V.tensor_mul(xAw[:], xA16[:], P.PSF[:, 1024:2048])
    v_xaw = P.vbody(vfn_xaw)
    sc_g.close()

    li = [0]
    ldv = {}
    mm_pe = {}
    evac = [0]
    for og in range(4):
        for kt in range(DT):
            i = li[0]
            bi = i % 2

            @P.block.gpsimd
            def _(g):
                if i == 0:
                    g.wait_ge(v_y[0].h, v_y[1])
                    g.wait_ge(ve.h, v_xaw)
                if i >= 2:
                    g.wait_ge(pe.h, mm_pe[i - 2])
                ldv[i] = P.wlg.issue(g, wqt[bi][:],
                                     wprojT[kt * 128:(kt + 1) * 128,
                                            og * 512:(og + 1) * 512], i)

            @P.block.tensor
            def _(t):
                t.wait_ge(ldv[i][0].h, ldv[i][1])
                if kt == 0 and evac[0]:
                    t.wait_ge(ve.h, evac[0])
                if kt == 0 and og == 0:
                    t.wait_ge(ve.h, v_xaw)
                mm = None
                for k in range(4):
                    for off in (0, 512):
                        mm = t.matmul(P.PSF[:, k * 1024 + off:k * 1024 + off + 512],
                                      wqt[bi][:, k * 128:(k + 1) * 128],
                                      y16[kt][:, off:off + 512],
                                      start=(kt == 0), stop=(kt == DT - 1))
                mm.then_inc(pe.h, 1)
            mm_pe[i] = pe.inc(1)
            li[0] += 1
        v_g = pe.n

        def vfn_pev(V, og=og, v_g=v_g):
            V.wait_ge(pe.h, v_g)
            for k in range(4):
                V.tensor_copy(sbr[og * 4 + k][:], P.PSF[:, k * 1024:(k + 1) * 1024])
        evac[0] = P.vbody(vfn_pev)
    v_sbr = evac[0]

    sc_cb = ExitStack()
    xb = [P.sbuf(f"xb{i}", [128, 1024], F32, sc_cb) for i in range(2)]
    gt = [P.sbuf(f"gt{i}", [128, 1024], BF16, sc_cb) for i in range(2)]
    tmpf = P.sbuf("tmpf", [128, 1024], F32, sc_cb)
    x2st = []
    cb_ve = []
    ffn_pe = []
    li2 = [0]
    for og in range(4):
        for kt in range(DT):
            i = li2[0]
            bi = i % 2

            @P.block.gpsimd
            def _(g):
                if i == 0:
                    g.wait_ge(ve.h, v_sbr)
                if i >= 2:
                    g.wait_ge(pe.h, mm_pe[(64, i - 2)])
                ldv[(64, i)] = P.wlg.issue(g, wqt[bi][:],
                                           wffnT[kt * 128:(kt + 1) * 128,
                                                 og * 512:(og + 1) * 512], i)

            @P.block.tensor
            def _(t):
                t.wait_ge(ldv[(64, i)][0].h, ldv[(64, i)][1])
                if kt == 0 and og > 0:
                    t.wait_ge(ve.h, cb_ve[-1])
                mm = None
                for k in range(4):
                    for off in (0, 512):
                        mm = t.matmul(P.PSF[:, k * 1024 + off:k * 1024 + off + 512],
                                      wqt[bi][:, k * 128:(k + 1) * 128],
                                      xn16[kt][:, off:off + 512],
                                      start=(kt == 0), stop=False)
                mm.then_inc(pe.h, 1)
            mm_pe[(64, i)] = pe.inc(1)
            li2[0] += 1

        @P.block.tensor
        def _(t):
            mm = None
            for k in range(4):
                ot = og * 4 + k
                for off in (0, 512):
                    mm = t.matmul(P.PSF[:, k * 1024 + off:k * 1024 + off + 512],
                                  BT16[:, ot * 128:(ot + 1) * 128],
                                  xAw[:, off:off + 512],
                                  start=False, stop=True)
            mm.then_inc(pe.h, 1)
        v_lora = pe.inc(1)

        for k in range(4):
            ot = og * 4 + k
            bi = ot % 2

            def sfn_g(S_, ot=ot, k=k, bi=bi, v_lora=v_lora):
                S_.wait_ge(pe.h, v_lora)
                if ot >= 2:
                    S_.wait_ge(ve.h, cb_ve[ot - 2])
                S_.wait_ge(v_ml[0].h, v_ml[1])
                S_.activation(gt[bi][:], P.PSF[:, k * 1024:(k + 1) * 1024], AF.Silu,
                              bias=nb[:, ot:ot + 1], scale=1.0)
            v_gt = P.sbody(sfn_g)

            xldi = []

            @P.block.gpsimd
            def _(g):
                if ot >= 2:
                    g.wait_ge(ve.h, cb_ve[ot - 2])
                    g.wait_ge(x2st[ot - 2][0].h, x2st[ot - 2][1])
                xldi.append(P.sxl.issue(g, xb[bi][:],
                                        x1T[ot * 128:(ot + 1) * 128, :], ot))

            def vfn_c(V, ot=ot, bi=bi, v_gt=v_gt, xldi=xldi):
                V.wait_ge(ac.h, v_gt)
                V.wait_ge(xldi[0][0].h, xldi[0][1])
                V.tensor_mul(tmpf[:], gt[bi][:], sbr[ot][:])
                V.tensor_add(xb[bi][:], xb[bi][:], tmpf[:])
            cb_ve.append(P.vbody(vfn_c))

            @P.block.gpsimd
            def _(g):
                g.wait_ge(ve.h, cb_ve[ot])
                x2st.append(P.sxs.issue(g, x2_out[ot * 128:(ot + 1) * 128, :],
                                        xb[bi][:], ot))
    sc_cb.close()
    sc_main.close()
    P.default_sc = None


def build_l1f():
    P = LB("l1f").start()
    io = {
        "xn1": P.din("xn1T", [DM, TOK], BF16).ap(),
        "ypc": [P.din(f"ypc{b}", [4 * 512, 1024], BF16).ap() for b in range(4)],
        "idx_y": P.din("idx_y", [128, H], mybir.dt.int32).ap(),
        "x1": P.din("x1T", [DM, TOK], F32).ap(),
        "wp1_ag": P.din("wprojT", [DM, DM], BF16).ap(),
        "wf_ag": P.din("wffnT", [DM, DM], BF16).ap(),
        "bvec": P.din("bvec", [DM, 1], F32).ap(),
        "AT": P.din("AT", [DM, E * RLO], F32).ap(),
        "BT": P.din("BT", [E * RLO, DM], F32).ap(),
        "gwT": P.din("gwT", [DM, E], F32).ap(),
        "gb_row": P.din("gb_row", [1, E], F32).ap(),
        "x2": P.dout("x2_out", [DM, TOK], F32).ap(),
    }
    C = emit_consts(P)
    emit_l1f(P, C, io)

    @P.block.gpsimd
    def _(g):
        P.wait_all_streams(g)

    @P.block.sync
    def _(sy_):
        P.wait_all_streams(sy_)

    return P.finish()



def build_fused():
    P = LB("fused").start()
    io = {
        "xT": P.din("xT", [DM, TOK], F32).ap(),
        "qkv_sl": P.din("qkv_sl", [256, 6144], F32).ap(),
        "prj_sl": P.din("prj_sl", [256, 2048], F32).ap(),
        "w1_sl": P.din("w1_sl", [256, HID], F32).ap(),
        "w3_sl": P.din("w3_sl", [256, HID], F32).ap(),
        "w2_sl": P.din("w2_sl", [683, 2048], F32).ap(),
        "qkvnrm": P.din("qkvnrm", [DM, 1], F32).ap(),
        "s_col": P.din("s_col", [H, 1], F32).ap(),
        "s_row": P.din("s_row", [1, H], F32).ap(),
        "normw_in": P.din("normw_in", [DM, 1], F32).ap(),
        "pnormw_in": P.din("pnormw_in", [DM, 1], F32).ap(),
        "watt_sl": P.din("watt_sl", [256, 3 * DM], F32).ap(),
        "pos_row": P.din("pos_row", [1, TOK], F32).ap(),
        "wfac_row": P.din("wfac_row", [1, 3], F32).ap(),
        "gap_row": P.din("gap_row", [1, 3], F32).ap(),
        "wp1_sl": P.din("wp1_sl", [256, DM], F32).ap(),
        "wf_sl": P.din("wf_sl", [256, DM], F32).ap(),
        "bvec": P.din("bvec", [DM, 1], F32).ap(),
        "AT": P.din("AT", [DM, E * RLO], F32).ap(),
        "BT": P.din("BT", [E * RLO, DM], F32).ap(),
        "gwT": P.din("gwT", [DM, E], F32).ap(),
        "gb_row": P.din("gb_row", [1, E], F32).ap(),
        "idx_l1a": P.din("idx_l1a", [128, 48], mybir.dt.int32).ap(),
        "idx_y": P.din("idx_y", [128, H], mybir.dt.int32).ap(),
        "x2": P.dout("x2_out", [DM, TOK], F32).ap(),
    }
    P.tag = "z"
    for nm, R, Cc, _, _c in WPREP:
        io[nm + "_pub"] = P.dram(nm + "_pub", [R, Cc], BF16).ap()
        io[nm + "_ag"] = P.dram(nm + "_ag", [8 * R, Cc], BF16).ap()
    for nm, R, Cc in WCAST:
        io[nm + "_pub"] = P.dram(nm + "_pub", [R, Cc], BF16).ap()
        io[nm + "_ag"] = P.dram(nm + "_ag", [8 * R, Cc], BF16).ap()
    io["mrow"] = P.dram("mrow", [1, 8], F32).ap()
    io["xn0"] = P.dram("xn0", [DM, TOK], BF16).ap()
    io["oloc"] = P.dram("oloc", [CPC, 128, H * 128], BF16).ap()
    io["qdec"] = P.dram("qdec", [CPC, 128, H * 128], BF16).ap()
    io["kvagg"] = P.dram("kvagg", [128, H * 128], F32).ap()
    kvall_h = P.dram("kvall", [4, 128, H * 128], F32)
    io["kvall"] = kvall_h.ap()
    io["x1"] = P.dram("x1", [DM, TOK], F32).ap()
    io["xn1"] = P.dram("xn1", [DM, TOK], BF16).ap()
    io["qkvpub"] = P.dram("qkvpub", [3 * DM, TOK], BF16).ap()
    io["qkvall"] = P.dram("qkvall", [4 * 3 * DM, TOK], BF16).ap()
    io["ypub"] = P.dram("ypub", [2048, 1024], BF16).ap()
    io["ypc"] = [P.dram(f"ypc{b}", [4 * 512, 1024], BF16).ap() for b in range(4)]

    cc = P.sem("cc")
    GROUPS = [[0, 1, 2, 3], [4, 5, 6, 7]]

    C = emit_consts(P)
    wstate = {}

    def walloc_cb():
        wstate["w"] = alloc_wprep(P, P.default_sc)

    def wprep_cb():
        w = wstate["w"]
        tag0 = P.tag
        P.tag = "w"
        emit_wprep_cast(P, C, io, w, ["watt", "wp1", "wf"], stage=0)
        emit_wprep_means(P, C, io, w)
        g1 = emit_wprep_quant(P, C, io, w, ["qkv"])
        io["cc_qkv"] = g1["qkv"]
        emit_wprep_quant(P, C, io, w, ["prj", "w1", "w3", "w2"])
        emit_wprep_cast(P, C, io, w, ["watt", "wp1", "wf"], stage=1)
        P.tag = tag0
        return []

    emit_l0a(P, C, io, walloc_cb=walloc_cb, wprep_cb=wprep_cb)

    @P.block.gpsimd
    def _(g):
        g.wait_ge(P.mvec.sems[0].h, P.mvec.sems[0].n)
        g.collective_compute(
            "AllGather", ALU.bypass, replica_groups=GROUPS,
            ins=[io["kvagg"]],
            outs=[io["kvall"].rearrange("a b c -> (a b) c")]).then_inc(cc.h, 1)
    cc.inc(1)
    P.phase_barrier()

    emit_l0b(P, C, io)
    P.phase_barrier()

    io["cc_groups"] = GROUPS
    emit_l1q(P, C, io)
    P.phase_barrier()

    emit_l1a(P, C, io)
    P.phase_barrier()

    emit_l1f(P, C, io)

    @P.block.gpsimd
    def _(g):
        P.wait_all_streams(g)

    @P.block.sync
    def _(sy_):
        P.wait_all_streams(sy_)

    @P.block.vector
    def _(v):
        P.wait_all_streams(v)

    return P.finish()


def host_inputs(inp):
    x = _f32c(inp["L0_attn_x"])
    s_np = _f32c(inp["L0_attn_s"]).reshape(H, 1)
    lA = _f32c(inp["L1_ffn_lora_A"])
    lB = _f32c(inp["L1_ffn_lora_B"])
    wqkvT = np.ascontiguousarray(_f32c(inp["L0_attn_qkv_w"]).T)
    wprojT = np.ascontiguousarray(_f32c(inp["L0_attn_proj_w"]).T)
    w1T = np.ascontiguousarray(_f32c(inp["L0_ffn_w1"]).T)
    w3T = np.ascontiguousarray(_f32c(inp["L0_ffn_w3"]).T)
    w2T = np.ascontiguousarray(_f32c(inp["L0_ffn_w2"]).T)
    w2Tp = np.zeros((8 * 683, DM), np.float32)
    w2Tp[:HID] = w2T
    wattT = np.ascontiguousarray(_f32c(inp["L1_attn_w_atten"]).T)
    wp1T = np.ascontiguousarray(_f32c(inp["L1_attn_w_proj"]).T)
    wfT = np.ascontiguousarray(_f32c(inp["L1_ffn_W"]).T)
    shared = {
        "qkvnrm": _f32c(inp["L0_attn_qkv_norm_w"]).reshape(DM, 1),
        "s_col": s_np,
        "s_row": np.ascontiguousarray(s_np.reshape(1, H)),
        "normw_in": _f32c(inp["L0_attn_norm_w"]).reshape(DM, 1),
        "pnormw_in": _f32c(inp["L0_attn_proj_norm_w"]).reshape(DM, 1),

        "bvec": _f32c(inp["L1_ffn_b"]).reshape(DM, 1),
        "AT": np.ascontiguousarray(lA.reshape(E * RLO, DM).T),
        "BT": np.ascontiguousarray(np.transpose(lB, (0, 2, 1)).reshape(E * RLO, DM)),
        "gwT": np.ascontiguousarray(_f32c(inp["L1_ffn_gate_w"]).T),
        "gb_row": _f32c(inp["L1_ffn_gate_b"]).reshape(1, E),
    }
    maps = []
    pp = np.arange(128, dtype=np.int64)
    for c in range(NCORES):
        b, j = c // 4, c % 4
        m = dict(shared)
        m["xT"] = np.ascontiguousarray(x[b, j * TOK:(j + 1) * TOK, :].T)
        m["qkv_sl"] = np.ascontiguousarray(wqkvT[c * 256:(c + 1) * 256])
        m["prj_sl"] = np.ascontiguousarray(wprojT[c * 256:(c + 1) * 256])
        m["w1_sl"] = np.ascontiguousarray(w1T[c * 256:(c + 1) * 256])
        m["w3_sl"] = np.ascontiguousarray(w3T[c * 256:(c + 1) * 256])
        m["w2_sl"] = np.ascontiguousarray(w2Tp[c * 683:(c + 1) * 683])
        m["watt_sl"] = np.ascontiguousarray(wattT[c * 256:(c + 1) * 256])
        m["wp1_sl"] = np.ascontiguousarray(wp1T[c * 256:(c + 1) * 256])
        m["wf_sl"] = np.ascontiguousarray(wfT[c * 256:(c + 1) * 256])
        m["pos_row"] = np.arange(j * TOK, (j + 1) * TOK,
                                 dtype=np.float32).reshape(1, TOK)
        wf = np.zeros((1, 3), np.float32)
        gp_ = np.zeros((1, 3), np.float32)
        for p in range(j):
            wf[0, p] = 1.0
            gp_[0, p] = 8.0 * (j - p - 1)
        m["wfac_row"] = wf
        m["gap_row"] = gp_
        idx_a = np.zeros((128, 48), np.int32)
        for bh in range(4):
            hh = 4 * j + bh
            g_ = hh // 4
            for role in range(3):
                for j2 in range(4):
                    if role == 2:
                        base = j2 * 2048 + hh * 128
                    else:
                        base = (8192 if role == 0 else 16384) + \
                            g_ * 2048 + j2 * 512 + (hh % 4) * 128
                    idx_a[:, bh * 12 + role * 4 + j2] = base + pp
        m["idx_l1a"] = idx_a
        idx_y = np.zeros((128, H), np.int32)
        for hh in range(H):
            base = (hh // 4) * 512 + j * 128
            idx_y[:, hh] = base + pp
        m["idx_y"] = idx_y
        maps.append(m)
    return maps


def _f32c(a):
    return np.ascontiguousarray(np.asarray(a, dtype=np.float32))


_BUILD_CACHE = {}


def _get(name, fn):
    if name not in _BUILD_CACHE:
        _BUILD_CACHE[name] = fn()
    return _BUILD_CACHE[name]


def kernel(**inp):
    maps = host_inputs(inp)
    nc = _get("fused", build_fused)
    res = run_bass_kernel_spmd(nc, maps, core_ids=list(range(NCORES))).results
    out = np.zeros((B, N, DM), np.float32)
    for c in range(NCORES):
        b, j = c // 4, c % 4
        out[b, j * TOK:(j + 1) * TOK, :] = np.asarray(
            res[c]["x2_out"], np.float32).reshape(DM, TOK).T
    return out


# version 18
# speedup vs baseline: 1.0987x; 1.0546x over previous
import math
import numpy as np
import ml_dtypes
from contextlib import ExitStack

import concourse.bass as bass
import concourse.mybir as mybir
from concourse.bass_utils import run_bass_kernel_spmd
from concourse.bass_core_rust import IndirectOffsetOnAxis

F32 = mybir.dt.float32
BF16 = mybir.dt.bfloat16
AF = mybir.ActivationFunctionType
ALU = mybir.AluOpType
AX = mybir.AxisListType.X
BFNP = ml_dtypes.bfloat16

B, N, DM, H, HD, T = 2, 4096, 2048, 16, 128, 128
NCH = N // T
CPC = 8
TOK = CPC * T
DT = DM // 128
HID = 5461
E, RLO = 8, 16
EPS32 = float(np.finfo(np.float32).eps)
INV_SQRT_HD = 1.0 / math.sqrt(HD)
NCORES = 8


class SemC:
    def __init__(self, h):
        self.h = h
        self.n = 0

    def inc(self, k=1):
        self.n += k
        return self.n


class SEng:

    PASS = ("wait_ge",)

    def __init__(self, e, sem):
        self._e = e
        self._s = sem

    def __getattr__(self, name):
        tgt = getattr(self._e, name)
        if name in SEng.PASS or not callable(tgt):
            return tgt

        def call(*a, **k):
            if self._s.n:
                self._e.wait_ge(self._s.h, self._s.n)
            ins = tgt(*a, **k)
            ins.then_inc(self._s.h, 1)
            self._s.inc(1)
            return ins
        return call


class Stream:

    def __init__(self, P, name, nslots=2):
        self.sems = [SemC(P.ctx.enter_context(P.nc.semaphore(f"{name}{i}")))
                     for i in range(nslots)]
        self.k = nslots

    def issue(self, g, out, in_, i=0):
        s = self.sems[i % self.k]
        g.dma_start(out=out, in_=in_).then_inc(s.h, 16)
        return (s, s.inc(16))


class LB:
    def __init__(self, name):
        self.nc = bass.Bass("TRN2", target_bir_lowering=False)
        self.name = name
        self.ctx = ExitStack()
        self.sems = {}
        self.block = None

    def start(self):
        for s in ("pe", "ac", "ve", "gp", "cc"):
            self.sems[s] = SemC(self.ctx.enter_context(self.nc.semaphore(s)))
        self.block = self.ctx.enter_context(self.nc.Block())
        self.PSF = self.ctx.enter_context(self.nc.psum_tensor("PSF", [128, 4096], F32))
        self.streams = []
        self.tag = "x"
        self.default_sc = None
        for nm in ("mvec", "mld", "stq", "stx", "sqd", "sol", "sxn",
                   "sgw", "srp", "sqo", "svo", "lq", "lk", "lv", "sy",
                   "syl", "sxl", "sxs", "sml", "srr"):
            setattr(self, nm, self.stream(nm))
        self.mld4 = self.stream("mld4", 8)
        self.wl = self.stream("wl", 4)
        self.wlg = self.stream("wlg", 4)
        return self

    def phase_barrier(self, include_cc=True):
        engs = ("pe", "ac", "ve", "gp", "cc") if include_cc else ("pe", "ac", "ve", "gp")
        vals = [(self.sems[s].h, self.sems[s].n)
                for s in engs if self.sems[s].n]
        for st in self.streams:
            for s in st.sems:
                if s.n:
                    vals.append((s.h, s.n))

        for eng in ("tensor", "scalar", "sync", "gpsimd", "vector"):
            def fgate(e, vals=vals):
                for h, v in vals:
                    e.wait_ge(h, v)
            getattr(self.block, eng)(fgate)

    def stream(self, name, k=2):
        st = Stream(self, name, k)
        self.streams.append(st)
        return st

    def wait_all_streams(self, e):
        for st in self.streams:
            for s in st.sems:
                if s.n:
                    e.wait_ge(s.h, s.n)

    def bank(self, i, w=1, rows=128):
        return self.PSF[0:rows, i * 512:(i + w) * 512]

    def bankc(self, i, cols, rows=128):
        return self.PSF[0:rows, i * 512:i * 512 + cols]

    def sem(self, name):
        return self.sems[name]

    def sbuf(self, name, shape, dtype, sc=None):
        return (sc or self.default_sc or self.ctx).enter_context(
            self.nc.sbuf_tensor(f"{self.tag}_{name}", shape, dtype))

    def dram(self, name, shape, dtype):
        return self.nc.dram_tensor(f"{self.tag}_{name}", shape, dtype, kind="Internal")

    def din(self, name, shape, dtype):
        return self.nc.dram_tensor(name, shape, dtype, kind="ExternalInput")

    def dout(self, name, shape, dtype):
        return self.nc.dram_tensor(name, shape, dtype, kind="ExternalOutput")

    def vbody(self, fn):
        ve = self.sem("ve")

        def wrap(v):
            fn(SEng(v, ve))
        self.block.vector(wrap)
        return ve.n

    def sbody(self, fn):
        ac = self.sem("ac")

        def wrap(s):
            fn(SEng(s, ac))
        self.block.scalar(wrap)
        return ac.n

    def gbody(self, fn):
        gp = self.sem("gp")

        def wrap(g):
            fn(g, SEng(g, gp))
        self.block.gpsimd(wrap)
        return gp.n

    def finish(self):
        self.ctx.close()
        return self.nc


def tile_list(rows, cols, tr=128, tc=1024):
    out = []
    r = 0
    while r < rows:
        pr = min(tr, rows - r)
        c = 0
        while c < cols:
            pc = min(tc, cols - c)
            out.append((r, pr, c, pc))
            c += pc
        r += pr
    return out


def emit_consts(P):
    C = {}
    C["ones16"] = P.sbuf("c_ones16", [128, 1], BF16)
    C["ones1r32"] = P.sbuf("c_ones1r32", [1, 128], F32)
    C["ones1r16"] = P.sbuf("c_ones1r16", [1, 128], BF16)
    C["onescol32"] = P.sbuf("c_onescol32", [128, 1], F32)
    C["iota_pf"] = P.sbuf("c_iotapf", [128, 128], F32)
    C["trilT"] = P.sbuf("c_trilT", [128, 128], F32)
    C["ident16"] = P.sbuf("c_ident16", [128, 128], BF16)
    gp, ve = P.sem("gp"), P.sem("ve")

    def gfn(g, G):
        G.memset(C["ones16"][:], 1.0)
        G.memset(C["ones1r32"][:], 1.0)
        G.memset(C["ones1r16"][:], 1.0)
        G.memset(C["onescol32"][:], 1.0)
        G.iota(C["iota_pf"][:], pattern=[[-1, 128]], base=0, channel_multiplier=1,
               allow_small_or_imprecise_dtypes=True)
    v_gp = P.gbody(gfn)

    def vfn(V):
        V.wait_ge(gp.h, v_gp)
        V.tensor_single_scalar(C["trilT"][:], C["iota_pf"][:], 0.0, ALU.is_le)
        V.tensor_single_scalar(C["ident16"][:], C["iota_pf"][:], 0.0, ALU.is_equal)
    v_ve = P.vbody(vfn)

    for eng in ("tensor", "scalar", "sync", "gpsimd", "vector"):
        def fgate(e, v_gp=v_gp, v_ve=v_ve):
            e.wait_ge(gp.h, v_gp)
            e.wait_ge(ve.h, v_ve)
        getattr(P.block, eng)(fgate)
    return C


_POOL_N = [0]


def alloc_pools(P, sc):
    k = _POOL_N[0]
    _POOL_N[0] += 1
    P.wb32 = [P.sbuf(f"wb32_{k}_{i}", [128, 1024], F32, sc) for i in range(2)]
    P.ba16 = [P.sbuf(f"ba16_{k}_{i}", [128, 1024], BF16, sc) for i in range(2)]
    P.bb16 = [P.sbuf(f"bb16_{k}_{i}", [128, 1024], BF16, sc) for i in range(2)]



def emit_mean_pass(P, C, sc, wsrc, rows, cols, clip_lo, tag, mq_store=None,
                   entry=()):
    pe, ve = P.sem("pe"), P.sem("ve")
    acc = P.sbuf(f"{tag}_acc", [128, 1], F32, sc)
    red = P.sbuf(f"{tag}_red", [128, 1], F32, sc)
    m_sb = P.sbuf(f"{tag}_m", [1, 1], F32, sc)
    th_pos = P.sbuf(f"{tag}_thp", [128, 1], F32, sc)
    th_neg = P.sbuf(f"{tag}_thn", [128, 1], F32, sc)
    tiles = tile_list(rows, cols)
    nelem = rows * cols

    def vfn0(V):
        for es, evv in entry:
            V.wait_ge(es.h, evv)
        V.memset(acc[:], 0.0)
    P.vbody(vfn0)

    ld_v, ve_done = [], []
    for i, (r, pr, c, pc) in enumerate(tiles):
        bi = i % 2

        @P.block.gpsimd
        def _(g):
            if i >= 2:
                g.wait_ge(ve.h, ve_done[i - 2])
            else:
                for es, evv in entry:
                    g.wait_ge(es.h, evv)
            ld_v.append(P.mld.issue(g, P.wb32[bi][0:pr, 0:pc],
                                    wsrc[r:r + pr, c:c + pc], i))

        def vfn(V, i=i, bi=bi, pr=pr, pc=pc):
            V.wait_ge(ld_v[i][0].h, ld_v[i][1])
            V.tensor_reduce(red[0:pr, 0:1], P.wb32[bi][0:pr, 0:pc], AX, ALU.add,
                            apply_absolute_value=True)
            V.tensor_add(acc[0:pr, 0:1], acc[0:pr, 0:1], red[0:pr, 0:1])
        ve_done.append(P.vbody(vfn))

    @P.block.tensor
    def _(t):
        t.wait_ge(ve.h, ve_done[-1])
        t.matmul(P.bankc(7, 1, rows=1), acc[:, 0:1], C["onescol32"][:, 0:1],
                 start=True, stop=True).then_inc(pe.h, 1)
    v_pe = pe.inc(1)

    def vfn2(V):
        V.wait_ge(pe.h, v_pe)
        if clip_lo is None:
            V.tensor_single_scalar(m_sb[:], P.bankc(7, 1, rows=1), 1.0 / nelem, ALU.mult)
        else:
            V.tensor_scalar(m_sb[:], P.bankc(7, 1, rows=1), 1.0 / nelem, clip_lo,
                            ALU.mult, ALU.max)
    v_m = P.vbody(vfn2)

    @P.block.tensor
    def _(t):
        t.wait_ge(ve.h, v_m)
        t.matmul(P.bankc(7, 1), C["ones1r32"][0:1, :], m_sb[0:1, 0:1],
                 start=True, stop=True).then_inc(pe.h, 1)
    v_pe2 = pe.inc(1)

    def vfn3(V):
        V.wait_ge(pe.h, v_pe2)
        V.tensor_single_scalar(th_pos[:], P.bankc(7, 1), 0.5, ALU.mult)
        V.tensor_single_scalar(th_neg[:], th_pos[:], -1.0, ALU.mult)
    done = P.vbody(vfn3)

    if mq_store is not None:
        @P.block.gpsimd
        def _(g):
            g.wait_ge(ve.h, done)
            P.mvec.issue(g, mq_store, m_sb[:])
    return m_sb, th_pos, th_neg, done


def emit_quantize_pass(P, wsrc, wdst, rows, cols, th_pos, th_neg, th_ready, tag):
    ac, ve = P.sem("ac"), P.sem("ve")
    tiles = tile_list(rows, cols)
    ld_v, ac_done, ve_done, st_v = [], [], [], []
    pre_stq = [(s, s.n) for s in P.stq.sems]
    for i, (r, pr, c, pc) in enumerate(tiles):
        bi = i % 2

        @P.block.gpsimd
        def _(g):
            if i >= 2:
                g.wait_ge(ac.h, ac_done[i - 2])
            else:
                g.wait_ge(ve.h, th_ready)
            ld_v.append(P.mld.issue(g, P.wb32[bi][0:pr, 0:pc],
                                    wsrc[r:r + pr, c:c + pc], i))

        def sfn(S_, i=i, bi=bi, pr=pr, pc=pc):
            S_.wait_ge(ld_v[i][0].h, ld_v[i][1])
            if i == 0:
                S_.wait_ge(ve.h, th_ready)
            if i < 2:
                for s_, vv in pre_stq:
                    if vv:
                        S_.wait_ge(s_.h, vv)
            if i >= 2:
                S_.wait_ge(st_v[i - 2][0].h, st_v[i - 2][1])
            S_.activation(P.ba16[bi][0:pr, 0:pc], P.wb32[bi][0:pr, 0:pc], AF.Sign,
                          bias=th_neg[0:pr, 0:1])
            S_.activation(P.bb16[bi][0:pr, 0:pc], P.wb32[bi][0:pr, 0:pc], AF.Sign,
                          bias=th_pos[0:pr, 0:1])
        ac_done.append(P.sbody(sfn))

        def vfn(V, i=i, bi=bi, pr=pr, pc=pc):
            V.wait_ge(ac.h, ac_done[i])
            if i >= 2:
                V.wait_ge(st_v[i - 2][0].h, st_v[i - 2][1])
            V.tensor_add(P.bb16[bi][0:pr, 0:pc], P.bb16[bi][0:pr, 0:pc],
                         P.ba16[bi][0:pr, 0:pc])
        ve_done.append(P.vbody(vfn))

        @P.block.gpsimd
        def _(g):
            g.wait_ge(ve.h, ve_done[i])
            st_v.append(P.stq.issue(g, wdst[r:r + pr, c:c + pc],
                                    P.bb16[bi][0:pr, 0:pc], i))
    return st_v, ve_done[-1]


def emit_ln_h(P, C, sc, xsrc, h16, nrm_sb, tag, xn_dram=None, entry=(),
              eps_ln=1e-5, eps_h=EPS32, with_h=True, xn_tiles=None):
    pe, ac, ve, gp = (P.sem(s) for s in ("pe", "ac", "ve", "gp"))
    ps_sum = P.PSF[0:1, 0:1024]
    ps_sq = P.PSF[0:1, 1024:2048]
    A = P.sbuf(f"{tag}_A", [1, 1024], F32, sc)
    Bv = P.sbuf(f"{tag}_B", [1, 1024], F32, sc)
    Cv = P.sbuf(f"{tag}_C", [1, 1024], F32, sc)
    Dv = P.sbuf(f"{tag}_D", [1, 1024], F32, sc)
    ep1 = P.sbuf(f"{tag}_e1", [1, 1], F32, sc)
    ep2 = P.sbuf(f"{tag}_e2", [1, 1], F32, sc)
    bm = P.sbuf(f"{tag}_bm", [128, 1024], F32, sc)
    br = P.sbuf(f"{tag}_br", [128, 1024], F32, sc)
    brh = P.sbuf(f"{tag}_brh", [128, 1024], F32, sc) if with_h else None

    def gfn(g, G):
        G.memset(ep1[:], eps_ln)
        G.memset(ep2[:], eps_h)
    v_eps = P.gbody(gfn)

    ld_v, ac_done, ve_done, pe_done = [], [], [], []
    for i in range(DT):
        bi = i % 2

        @P.block.gpsimd
        def _(g):
            if i >= 2:
                g.wait_ge(ac.h, ac_done[i - 2])
            else:
                for es, evv in entry:
                    g.wait_ge(es.h, evv)
            ld_v.append(P.mld.issue(g, P.wb32[bi][:],
                                    xsrc[i * 128:(i + 1) * 128, :], i))

        def vfn(V, i=i, bi=bi):
            V.wait_ge(ld_v[i][0].h, ld_v[i][1])
            if i >= 2:
                V.wait_ge(pe.h, pe_done[i - 2])
            V.tensor_copy(P.ba16[bi][:], P.wb32[bi][:])
        ve_done.append(P.vbody(vfn))

        def sfn(S_, i=i, bi=bi):
            S_.wait_ge(ve.h, ve_done[i])
            S_.activation(P.bb16[bi][:], P.wb32[bi][:], AF.Square)
        ac_done.append(P.sbody(sfn))

        @P.block.tensor
        def _(t):
            t.wait_ge(ac.h, ac_done[i])
            for off in (0, 512):
                t.matmul(ps_sum[0:1, off:off + 512], C["ones16"][:, 0:1],
                         P.ba16[bi][:, off:off + 512], start=(i == 0), stop=(i == DT - 1))
                mm = t.matmul(ps_sq[0:1, off:off + 512], C["ones16"][:, 0:1],
                              P.bb16[bi][:, off:off + 512], start=(i == 0),
                              stop=(i == DT - 1))
            mm.then_inc(pe.h, 1)
        pe_done.append(pe.inc(1))

    def vfn1(V):
        V.wait_ge(pe.h, pe_done[-1])
        V.tensor_single_scalar(A[:], ps_sum, 1.0 / DM, ALU.mult)
        V.tensor_single_scalar(Bv[:], ps_sq, 1.0 / DM, ALU.mult)
    v1 = P.vbody(vfn1)

    def sfn1(S_):
        S_.wait_ge(ve.h, v1)
        S_.activation(Dv[:], A[:], AF.Square)
    v2 = P.sbody(sfn1)

    def vfn2(V):
        V.wait_ge(ac.h, v2)
        V.tensor_sub(Bv[:], Bv[:], Dv[:])
    v3 = P.vbody(vfn2)

    def sfn2(S_):
        S_.wait_ge(ve.h, v3)
        S_.wait_ge(gp.h, v_eps)
        S_.activation(Cv[:], Bv[:], AF.Ln, bias=ep1[0:1, 0:1])
        if with_h:
            S_.activation(Dv[:], Cv[:], AF.Exp, scale=-1.0)
        S_.activation(Cv[:], Cv[:], AF.Exp, scale=-0.5)
    v4 = P.sbody(sfn2)

    def vfn3(V):
        V.wait_ge(ac.h, v4)
        if with_h:
            V.tensor_mul(Bv[:], Bv[:], Dv[:])
        V.tensor_mul(A[:], A[:], Cv[:])
    v5 = P.vbody(vfn3)

    if with_h:
        def sfn3(S_):
            S_.wait_ge(ve.h, v5)
            S_.activation(Bv[:], Bv[:], AF.Ln, bias=ep2[0:1, 0:1])
            S_.activation(Bv[:], Bv[:], AF.Exp, scale=-0.5)
        v6 = P.sbody(sfn3)
    else:
        v6 = None

    @P.block.tensor
    def _(t):
        t.wait_ge(ve.h, v5)
        if v6 is not None:
            t.wait_ge(ac.h, v6)
        for off in (0, 512):
            t.matmul(P.PSF[:, 2048 + off:2048 + off + 512], C["ones1r32"][0:1, :],
                     A[0:1, off:off + 512], start=True, stop=True)
            mm = t.matmul(P.PSF[:, 3072 + off:3072 + off + 512], C["ones1r32"][0:1, :],
                          Cv[0:1, off:off + 512], start=True, stop=True)
            if with_h:
                mm = t.matmul(P.PSF[:, 1024 + off:1024 + off + 512],
                              C["ones1r32"][0:1, :],
                              Bv[0:1, off:off + 512], start=True, stop=True)
        mm.then_inc(pe.h, 1)
    v7 = pe.inc(1)

    def vfn4(V):
        V.wait_ge(pe.h, v7)
        V.tensor_copy(bm[:], P.PSF[:, 2048:3072])
        V.tensor_copy(br[:], P.PSF[:, 3072:4096])
        if with_h:
            V.tensor_copy(brh[:], P.PSF[:, 1024:2048])
    v8 = P.vbody(vfn4)

    ld2, xn_ready, h_ready, xn_st = [], [], [], []
    for i in range(DT):
        bi = i % 2

        @P.block.gpsimd
        def _(g):
            if i < 2:
                g.wait_ge(ve.h, v8)
            else:
                g.wait_ge(ve.h, (h_ready if with_h else xn_ready)[i - 2])
                if xn_st:
                    g.wait_ge(xn_st[i - 2][0].h, xn_st[i - 2][1])
            ld2.append(P.mld.issue(g, P.wb32[bi][:],
                                   xsrc[i * 128:(i + 1) * 128, :], i))

        dst_t = xn_tiles[i] if xn_tiles is not None else P.ba16[bi]

        def vfn5(V, i=i, bi=bi, dst_t=dst_t):
            V.wait_ge(ld2[i][0].h, ld2[i][1])
            V.tensor_mul(P.wb32[bi][:], P.wb32[bi][:], br[:])
            V.tensor_sub(dst_t[:], P.wb32[bi][:], bm[:])
        xn_ready.append(P.vbody(vfn5))

        if xn_dram is not None:
            @P.block.gpsimd
            def _(g):
                g.wait_ge(ve.h, xn_ready[i])
                xn_st.append(P.stx.issue(g, xn_dram[i * 128:(i + 1) * 128, :],
                                         dst_t[:], i))

        if with_h:
            def vfn6(V, i=i, dst_t=dst_t):
                V.scalar_tensor_tensor(h16[i][:], dst_t[:], nrm_sb[:, i:i + 1],
                                       brh[:], ALU.mult, ALU.mult)
            h_ready.append(P.vbody(vfn6))
    return xn_ready, h_ready, xn_st


WPREP = [
    ("qkv", 256, 6144, 6144 * 2048, 1e-5),
    ("prj", 256, 2048, 2048 * 2048, 1e-5),
    ("w1", 256, HID, 2048 * HID, None),
    ("w3", 256, HID, 2048 * HID, None),
    ("w2", 683, 2048, HID * 2048, None),
]


def alloc_wprep(P, sc):
    w = {}
    w["acc"] = P.sbuf("w_acc", [128, 1], F32, sc)
    w["red"] = P.sbuf("w_red", [128, 1], F32, sc)
    w["prow"] = P.sbuf("w_prow", [1, 8], F32, sc)
    w["mrow_sb"] = P.sbuf("w_mrow", [1, 8], F32, sc)
    w["throw"] = P.sbuf("w_throw", [1, 8], F32, sc)
    w["th_bc"] = P.sbuf("w_thbc", [128, 16], F32, sc)
    w["wb"] = [P.sbuf(f"w_wb{i}", [128, 1024], F32, sc) for i in range(2)]
    w["q16"] = [P.sbuf(f"w_q16{i}", [128, 1024], BF16, sc) for i in range(2)]
    w["gtt"] = P.sbuf("w_gtt", [128, 1024], BF16, sc)
    w["prev_st"] = []
    w["pr_d"] = P.dram("w_pr_d", [1, 8], F32).ap()
    w["sums_d"] = P.dram("w_sums_d", [1, 8], F32).ap()
    w["prev_q_ve"] = 0
    return w


def emit_wprep_means(P, C, io, w):
    pe, ve, cc = P.sem("pe"), P.sem("ve"), P.sem("cc")
    GROUPS8 = [list(range(8))]
    acc, red, prow = w["acc"], w["red"], w["prow"]
    mrow_sb, throw, th_bc = w["mrow_sb"], w["throw"], w["th_bc"]
    wb = w["wb"]
    nb = len(wb)
    pr_d, sums_d = w["pr_d"], w["sums_d"]

    part_ve = []

    def vfn_z(V):
        V.memset(prow[:], 0.0)
        V.memset(mrow_sb[:], 0.0)
        V.memset(throw[:], 0.0)
    P.vbody(vfn_z)
    gi = [0]
    ve_done = {}
    for wi, (nm, R, Cc, nelem, clip) in enumerate(WPREP):
        wsl = io[nm + "_sl"]
        tiles = tile_list(R, Cc)

        def vfn0(V, wi=wi):
            V.memset(acc[:], 0.0)
        P.vbody(vfn0)
        for ti, (r, pr_, c, pc) in enumerate(tiles):
            i = gi[0]
            bi = i % nb

            @P.block.gpsimd
            def _(g):
                if i >= nb:
                    g.wait_ge(ve.h, ve_done[i - nb])
                w.setdefault("ld_v", {})[i] = P.mld4.issue(
                    g, wb[bi][0:pr_, 0:pc], wsl[r:r + pr_, c:c + pc], i % nb)

            def vfn(V, i=i, bi=bi, pr_=pr_, pc=pc):
                lv = w["ld_v"][i]
                V.wait_ge(lv[0].h, lv[1])
                V.tensor_reduce(red[0:pr_, 0:1], wb[bi][0:pr_, 0:pc], AX, ALU.add,
                                apply_absolute_value=True)
                V.tensor_add(acc[0:pr_, 0:1], acc[0:pr_, 0:1], red[0:pr_, 0:1])
            ve_done[i] = P.vbody(vfn)
            gi[0] += 1

        @P.block.tensor
        def _(t):
            t.wait_ge(ve.h, ve_done[gi[0] - 1])
            t.matmul(P.PSF[0:1, 7 * 512 + wi:7 * 512 + wi + 1], acc[:, 0:1],
                     C["onescol32"][:, 0:1], start=True, stop=True).then_inc(pe.h, 1)
        v_pe = pe.inc(1)

        def vfn2(V, wi=wi, v_pe=v_pe):
            V.wait_ge(pe.h, v_pe)
            V.tensor_copy(prow[0:1, wi:wi + 1], P.PSF[0:1, 7 * 512 + wi:7 * 512 + wi + 1])
        part_ve.append(P.vbody(vfn2))

    @P.block.gpsimd
    def _(g):
        g.wait_ge(ve.h, part_ve[-1])
        P.sgw.issue(g, pr_d, prow[:])
        g.wait_ge(P.sgw.sems[0].h, P.sgw.sems[0].n)
        g.collective_compute("AllReduce", ALU.add, replica_groups=GROUPS8,
                             ins=[pr_d], outs=[sums_d]).then_inc(cc.h, 1)
    v_cc_ar = cc.inc(1)

    @P.block.gpsimd
    def _(g):
        g.wait_ge(cc.h, v_cc_ar)
        P.sgw.issue(g, mrow_sb[:], sums_d)
    v_sums = (P.sgw.sems[0], P.sgw.sems[0].n)

    def vfn_m(V):
        V.wait_ge(v_sums[0].h, v_sums[1])
        for wi, (nm, R, Cc, nelem, clip) in enumerate(WPREP):
            sl = slice(wi, wi + 1)
            if clip is None:
                V.tensor_single_scalar(mrow_sb[0:1, sl], mrow_sb[0:1, sl],
                                       1.0 / nelem, ALU.mult)
            else:
                V.tensor_scalar(mrow_sb[0:1, sl], mrow_sb[0:1, sl], 1.0 / nelem,
                                clip, ALU.mult, ALU.max)
        V.tensor_single_scalar(throw[:], mrow_sb[:], 0.5, ALU.mult)
    v_m = P.vbody(vfn_m)

    @P.block.gpsimd
    def _(g):
        g.wait_ge(ve.h, v_m)
        P.sgw.issue(g, io["mrow"], mrow_sb[:])

    @P.block.tensor
    def _(t):
        t.wait_ge(ve.h, v_m)
        t.matmul(P.PSF[:, 7 * 512:7 * 512 + 8], C["ones1r32"][0:1, :],
                 throw[0:1, 0:8], start=True, stop=True).then_inc(pe.h, 1)
    v_thb = pe.inc(1)

    def vfn_tb(V):
        V.wait_ge(pe.h, v_thb)
        V.tensor_copy(th_bc[:, 0:8], P.PSF[:, 7 * 512:7 * 512 + 8])
        V.tensor_single_scalar(th_bc[:, 8:16], th_bc[:, 0:8], -1.0, ALU.mult)
    w["v_tb"] = P.vbody(vfn_tb)


def emit_wprep_quant(P, C, io, w, names):
    ve, cc = P.sem("ve"), P.sem("cc")
    GROUPS8 = [list(range(8))]
    wb, q16, gtt, th_bc = w["wb"], w["q16"], w["gtt"], w["th_bc"]
    cc_gates = {}
    widx = {nm: i for i, (nm, _, _, _, _) in enumerate(WPREP)}
    ag_jobs = []
    for nm in names:
        wi = widx[nm]
        R, Cc = WPREP[wi][1], WPREP[wi][2]
        wsl = io[nm + "_sl"]
        pub = io[nm + "_pub"]
        tiles = tile_list(R, Cc)
        ld_v, ve_done, st_v = [], [], []
        prev_st = w["prev_st"]
        for i, (r, pr_, c, pc) in enumerate(tiles):
            bi = i % 2

            @P.block.gpsimd
            def _(g):
                if i >= 2:
                    g.wait_ge(ve.h, ve_done[i - 2])
                else:
                    g.wait_ge(ve.h, w["prev_q_ve"] if w["prev_q_ve"] else w["v_tb"])
                ld_v.append(P.sqd.issue(g, wb[bi][0:pr_, 0:pc],
                                        wsl[r:r + pr_, c:c + pc], i))

            def vfn(V, i=i, bi=bi, pr_=pr_, pc=pc, wi=wi, prev_st=prev_st):
                V.wait_ge(ld_v[i][0].h, ld_v[i][1])
                if i == 0:
                    V.wait_ge(ve.h, w["v_tb"])
                if i < 2:
                    for s_, v_ in prev_st[-2:]:
                        V.wait_ge(s_.h, v_)
                if i >= 2:
                    V.wait_ge(st_v[i - 2][0].h, st_v[i - 2][1])
                V.tensor_single_scalar(q16[bi][0:pr_, 0:pc], wb[bi][0:pr_, 0:pc],
                                       th_bc[0:pr_, wi:wi + 1], ALU.is_gt)
                V.tensor_single_scalar(gtt[0:pr_, 0:pc], wb[bi][0:pr_, 0:pc],
                                       th_bc[0:pr_, 8 + wi:9 + wi], ALU.is_lt)
                V.tensor_sub(q16[bi][0:pr_, 0:pc], q16[bi][0:pr_, 0:pc],
                             gtt[0:pr_, 0:pc])
            ve_done.append(P.vbody(vfn))

            @P.block.gpsimd
            def _(g):
                g.wait_ge(ve.h, ve_done[i])
                st_v.append(P.sol.issue(g, pub[r:r + pr_, c:c + pc],
                                        q16[bi][0:pr_, 0:pc], i))

        w["prev_q_ve"] = ve_done[-1]
        w["prev_st"] = st_v[-2:]
        ag_jobs.append((nm, pub, [(s.h, s.n) for s in P.sol.sems if s.n]))

    for nm, pub, gates in ag_jobs:
        @P.block.gpsimd
        def _(g, pub=pub, gates=gates):
            for h_, v_ in gates:
                g.wait_ge(h_, v_)
            g.collective_compute("AllGather", ALU.bypass, replica_groups=GROUPS8,
                                 ins=[pub], outs=[io[nm + "_ag"]]).then_inc(cc.h, 1)
        cc_gates[nm] = cc.inc(1)
    return cc_gates


WCAST = [("watt", 256, 6144), ("wp1", 256, 2048), ("wf", 256, 2048)]


def emit_wprep_cast(P, C, io, w, names, stage):
    cc = P.sem("cc")
    GROUPS8 = [list(range(8))]
    cc_gates = {}
    if stage == 0:
        w["cast_st"] = {}

        @P.block.gpsimd
        def _(g):
            for nm in names:
                w["cast_st"][nm] = P.srr.issue(g, io[nm + "_pub"], io[nm + "_sl"], 1)
        return cc_gates

    for nm in names:
        st = w["cast_st"][nm]

        @P.block.gpsimd
        def _(g, st=st, nm=nm):
            g.wait_ge(st[0].h, st[1])
            g.collective_compute("AllGather", ALU.bypass, replica_groups=GROUPS8,
                                 ins=[io[nm + "_pub"]],
                                 outs=[io[nm + "_ag"]]).then_inc(cc.h, 1)
        cc_gates[nm] = cc.inc(1)
    return cc_gates


def _finish_early(P):
    @P.block.gpsimd
    def _(g):
        P.wait_all_streams(g)

    @P.block.sync
    def _(sy):
        P.wait_all_streams(sy)
    return P.finish()

def emit_l0a(P, C, io, upto=99, walloc_cb=None, wprep_cb=None):
    P.tag = "a"
    sc_main = ExitStack()
    P.default_sc = sc_main
    xT = io["xT"]
    qkvnrm = io["qkvnrm"]
    s_col = io["s_col"]
    s_row = io["s_row"]
    xn_out = io["xn0"]
    oloc_out = io["oloc"]
    qdec_out = io["qdec"]
    kvagg_out = io["kvagg"]

    wq_q = io["qkv_ag"]
    cc_qkv = io.get("cc_qkv", 0)
    rowscr = P.dram("rowscr", [1, 4096], F32).ap()
    rowscr2 = P.dram("rowscr2", [1, 4096], F32).ap()

    pe, ac, ve, gp, cc = (P.sem(s) for s in ("pe", "ac", "ve", "gp", "cc"))

    s_sb = P.sbuf("s_sb", [H, 1], F32)
    srow_sb = P.sbuf("srow_sb", [1, H], F32)
    nrm_sb = P.sbuf("nrm_sb", [128, DT], F32)
    neg_s = P.sbuf("neg_s", [H, 1], F32)
    sT_bias = P.sbuf("sT_bias", [H, 1], F32)
    lam_t16 = P.sbuf("lam_t16", [H, 128], F32)
    lam_rev16 = P.sbuf("lam_rev16", [H, 128], F32)
    lamT16 = P.sbuf("lamT16", [H, 1], F32)
    powj16 = P.sbuf("powj16", [H, 8], F32)
    iota_t = P.sbuf("iota_t", [H, 128], F32)
    iota_j = P.sbuf("iota_j", [H, 8], F32)
    lam_btall = P.sbuf("lam_btall", [128, H * 128], BF16)
    lamrev_btall = P.sbuf("lamrev_btall", [128, H * 128], BF16)
    lam_bt = [lam_btall[:, h * 128:(h + 1) * 128] for h in range(H)]
    lamrev_bt = [lamrev_btall[:, h * 128:(h + 1) * 128] for h in range(H)]
    powj_bc = P.sbuf("powj_bc", [128, 128], F32)
    lamT_bc = P.sbuf("lamT_bc", [128, 16], F32)
    lamT_ball = P.sbuf("lamT_ball", [128, H * 128], F32)
    negs_bc = P.sbuf("negs_bc", [128, 16], F32)
    diffpos = P.sbuf("diffpos", [128, 128], F32)
    maskTall = P.sbuf("maskTall", [128, H * 128], BF16)
    maskT = [maskTall[:, h * 128:(h + 1) * 128] for h in range(H)]
    h16 = [P.sbuf(f"h16_{i}", [128, 1024], BF16) for i in range(DT)]
    V16 = [P.sbuf(f"V16_{c}", [128, 2048], BF16) for c in range(CPC)]
    S = P.sbuf("S", [128, 2048], F32)
    S16 = P.sbuf("S16", [128, 2048], BF16)

    @P.block.gpsimd
    def _(g):
        P.mvec.issue(g, s_sb[:], s_col)
        P.mvec.issue(g, srow_sb[:], s_row)
        for i in range(DT):
            P.mvec.issue(g, nrm_sb[:, i:i + 1], qkvnrm[i * 128:(i + 1) * 128, 0:1])
    v_vecs = (P.mvec.sems[0], P.mvec.sems[0].n)

    def gfn_iota(g, G):
        G.iota(iota_t[:], pattern=[[1, 128]], base=0, channel_multiplier=0,
               allow_small_or_imprecise_dtypes=True)
        G.iota(iota_j[:], pattern=[[128, 8]], base=0, channel_multiplier=0,
               allow_small_or_imprecise_dtypes=True)
    v_iot = P.gbody(gfn_iota)

    def vfn_negs(V):
        V.wait_ge(v_vecs[0].h, v_vecs[1])
        V.tensor_single_scalar(neg_s[:], s_sb[:], -1.0, ALU.mult)
        V.tensor_single_scalar(sT_bias[:], s_sb[:], -float(T), ALU.mult)
    v_negs = P.vbody(vfn_negs)

    def sfn_tabs(S_):
        S_.wait_ge(gp.h, v_iot)
        S_.wait_ge(ve.h, v_negs)
        S_.activation(lam_t16[:], iota_t[:], AF.Exp, scale=neg_s[:, 0:1])
        S_.activation(lam_rev16[:], iota_t[:], AF.Exp, scale=s_sb[:, 0:1],
                      bias=sT_bias[:, 0:1])
        S_.activation(lamT16[:], s_sb[:], AF.Exp, scale=-float(T))
        S_.activation(powj16[:], iota_j[:], AF.Exp, scale=neg_s[:, 0:1])
    v_tabs = P.sbody(sfn_tabs)

    @P.block.gpsimd
    def _(g):
        g.wait_ge(ac.h, v_tabs)
        for hh in range(H):
            P.mvec.issue(g, rowscr[0:1, hh * 128:(hh + 1) * 128],
                         lam_t16[hh:hh + 1, :])
            P.mvec.issue(g, rowscr2[0:1, hh * 128:(hh + 1) * 128],
                         lam_rev16[hh:hh + 1, :])
            P.mvec.issue(g, rowscr[0:1, 2048 + hh * 8:2048 + (hh + 1) * 8],
                         powj16[hh:hh + 1, :])
            P.mvec.issue(g, rowscr2[0:1, 2048 + hh:2049 + hh],
                         lamT16[hh:hh + 1, 0:1])
    v_spread = (P.mvec.sems[0], P.mvec.sems[0].n)

    sc_rows = ExitStack()
    lamrow = P.sbuf("lamrow", [1, 2048], BF16, sc_rows)
    lamrevrow = P.sbuf("lamrevrow", [1, 2048], BF16, sc_rows)
    powrow = P.sbuf("powrow", [1, 128], F32, sc_rows)
    lamTrow = P.sbuf("lamTrow", [1, 16], F32, sc_rows)

    @P.block.gpsimd
    def _(g):
        g.wait_ge(v_spread[0].h, v_spread[1])
        P.mvec.issue(g, lamrow[:], rowscr[0:1, 0:2048])
        P.mvec.issue(g, lamrevrow[:], rowscr2[0:1, 0:2048])
        P.mvec.issue(g, powrow[:], rowscr[0:1, 2048:2176])
        P.mvec.issue(g, lamTrow[:], rowscr2[0:1, 2048:2064])
    v_rowsld = (P.mvec.sems[0], P.mvec.sems[0].n)

    last_copy = [0]
    for grp in range(4):
        for which, (src, dstl) in enumerate(((lamrow, lam_bt), (lamrevrow, lamrev_bt))):
            @P.block.tensor
            def _(t):
                if grp == 0 and which == 0:
                    t.wait_ge(v_rowsld[0].h, v_rowsld[1])
                else:
                    t.wait_ge(ve.h, last_copy[0])
                t.matmul(P.bank(6), C["ones1r16"][0:1, :],
                         src[0:1, grp * 512:(grp + 1) * 512],
                         start=True, stop=True).then_inc(pe.h, 1)
            v1 = pe.inc(1)

            def vfn(V, grp=grp, dstl=dstl, v1=v1):
                V.wait_ge(pe.h, v1)
                for k in range(4):
                    hh = grp * 4 + k
                    V.tensor_copy(dstl[hh][:],
                                  P.PSF[:, 6 * 512 + k * 128:6 * 512 + (k + 1) * 128])
            last_copy[0] = P.vbody(vfn)

    @P.block.tensor
    def _(t):
        t.wait_ge(ve.h, last_copy[0])
        t.matmul(P.bankc(6, 128), C["ones1r32"][0:1, :], powrow[0:1, 0:128],
                 start=True, stop=True)
        t.matmul(P.PSF[:, 6 * 512 + 128:6 * 512 + 144], C["ones1r32"][0:1, :],
                 lamTrow[0:1, 0:16], start=True, stop=True)
        t.matmul(P.PSF[:, 6 * 512 + 144:6 * 512 + 160], C["ones1r32"][0:1, :],
                 srow_sb[0:1, 0:16], start=True, stop=True).then_inc(pe.h, 1)
    v_bt = pe.inc(1)

    def vfn_bc(V):
        V.wait_ge(pe.h, v_bt)
        V.tensor_copy(powj_bc[:], P.bankc(6, 128))
        V.tensor_copy(lamT_bc[:], P.PSF[:, 6 * 512 + 128:6 * 512 + 144])
        V.tensor_single_scalar(negs_bc[:], P.PSF[:, 6 * 512 + 144:6 * 512 + 160],
                               -1.0, ALU.mult)
        V.memset(lamT_ball[:], 1.0)
        for hh in range(H):
            sl = slice(hh * 128, (hh + 1) * 128)
            V.tensor_single_scalar(lamT_ball[:, sl], lamT_ball[:, sl],
                                   lamT_bc[:, hh:hh + 1], ALU.mult)
    v_bc_done = P.vbody(vfn_bc)
    sc_rows.close()

    def vfn_dp(V):
        V.tensor_single_scalar(diffpos[:], C["iota_pf"][:], -1.0, ALU.mult)
        V.tensor_scalar_max(diffpos[:], diffpos[:], 0.0)
    v_dp = P.vbody(vfn_dp)

    mask_ve = []
    for hh in range(H):
        def sfn_m(S_, hh=hh):
            if hh == 0:
                S_.wait_ge(ve.h, v_bc_done)
                S_.wait_ge(ve.h, v_dp)
            S_.activation(maskT[hh][:], diffpos[:], AF.Exp,
                          scale=negs_bc[:, hh:hh + 1])
        v_m = P.sbody(sfn_m)

        def vfn_m(V, hh=hh, v_m=v_m):
            V.wait_ge(ac.h, v_m)
            V.tensor_mul(maskT[hh][:], maskT[hh][:], C["trilT"][:])
        mask_ve.append(P.vbody(vfn_m))
    v_masks = mask_ve[-1]

    if walloc_cb is not None:
        walloc_cb()

    sc_pool = ExitStack()
    alloc_pools(P, sc_pool)

    sc_ln = ExitStack()
    xn_ready, h_ready, xn_st = emit_ln_h(
        P, C, sc_ln, xT, h16, nrm_sb, "ln0", xn_dram=xn_out,
        entry=[(ve, v_bc_done)])
    v_h = h_ready[-1]
    sc_ln.close()
    sc_pool.close()
    stx_fin = [(s.h, s.n) for s in P.stx.sems if s.n]

    if wprep_cb is not None:
        stx_fin = stx_fin + wprep_cb()
        P.tag = "a"

    QW = P.sbuf("QW", [128, H * 1024], BF16)
    KW = P.sbuf("KW", [128, H * 1024], BF16)
    Q16 = [QW[:, h * 1024:(h + 1) * 1024] for h in range(H)]
    K16 = [KW[:, h * 1024:(h + 1) * 1024] for h in range(H)]
    wqt = [P.sbuf(f"wqt{i}", [128, 512], BF16) for i in range(4)]
    qtall = P.sbuf("qtall", [128, H * 128], BF16)
    qdall = P.sbuf("qdall", [128, H * 128], BF16)
    ktall = P.sbuf("ktall", [128, H * 128], BF16)
    ktmall = P.sbuf("ktmall", [128, H * 128], BF16)
    smaskall = P.sbuf("smaskall", [128, H * 128], BF16)
    otall = P.sbuf("otall", [128, H * 128], BF16)

    ld_i = [0]
    ldv = {}
    mm_pe = {}

    def wload(base_f, kt):
        i = ld_i[0]
        bi = i % 4

        @P.block.sync
        def _(sy):
            if i == 0:
                ccg = io.get("cc_qkv", 0)
                if ccg:
                    sy.wait_ge(cc.h, ccg)
                sy.wait_ge(ve.h, v_h)
                for h_, v_ in stx_fin:
                    sy.wait_ge(h_, v_)
            if i >= 4:
                sy.wait_ge(pe.h, mm_pe[i - 4])
            ldv[i] = P.wl.issue(sy, wqt[bi][:],
                                wq_q[kt * 128:(kt + 1) * 128, base_f:base_f + 512], i)
        return i, bi

    evac_gate = [0]
    for role in range(2):
        for grp in range(4):
            base_f = role * DM + grp * 512
            for kt in range(DT):
                i, bi = wload(base_f, kt)

                @P.block.tensor
                def _(t):
                    t.wait_ge(ldv[i][0].h, ldv[i][1])
                    if kt == 0 and evac_gate[0]:
                        t.wait_ge(ve.h, evac_gate[0])
                    mm = None
                    for k in range(4):
                        for off in (0, 512):
                            mm = t.matmul(
                                P.PSF[:, k * 1024 + off:k * 1024 + off + 512],
                                wqt[bi][:, k * 128:(k + 1) * 128],
                                h16[kt][:, off:off + 512],
                                start=(kt == 0), stop=(kt == DT - 1))
                    mm.then_inc(pe.h, 1)
                mm_pe[i] = pe.inc(1)
                ld_i[0] += 1
            v_grp = pe.n

            def vfn_ev(V, grp=grp, role=role, v_grp=v_grp):
                V.wait_ge(pe.h, v_grp)
                if role == 0 and grp == 0:
                    for h_, v_ in stx_fin:
                        V.wait_ge(h_, v_)
                for k in range(4):
                    hh = grp * 4 + k
                    dst = Q16[hh] if role == 0 else K16[hh]
                    V.tensor_copy(dst[:], P.PSF[:, k * 1024:(k + 1) * 1024])
            evac_gate[0] = P.vbody(vfn_ev)

    for vg in range(4):
        base_f = 2 * DM + vg * 512
        for kt in range(DT):
            i, bi = wload(base_f, kt)

            @P.block.tensor
            def _(t):
                t.wait_ge(ldv[i][0].h, ldv[i][1])
                if kt == 0:
                    t.wait_ge(ve.h, evac_gate[0])
                mm = None
                for c in range(CPC):
                    mm = t.matmul(P.bank(c), h16[kt][:, c * 128:(c + 1) * 128],
                                  wqt[bi][:, 0:512],
                                  start=(kt == 0), stop=(kt == DT - 1))
                mm.then_inc(pe.h, 1)
            mm_pe[i] = pe.inc(1)
            ld_i[0] += 1
        v_vg = pe.n

        def vfn_vev(V, vg=vg, v_vg=v_vg):
            V.wait_ge(pe.h, v_vg)
            for c in range(CPC):
                V.tensor_copy(V16[c][:, vg * 512:(vg + 1) * 512], P.bank(c))
        evac_gate[0] = P.vbody(vfn_vev)

    v_qkv = evac_gate[0]

    def vfn_s0(V):
        V.wait_ge(ve.h, v_qkv)
        V.memset(S[:], 0.0)
        V.memset(S16[:], 0.0)
    v_s0 = P.vbody(vfn_s0)

    @P.block.tensor
    def _(t):
        t.wait_ge(ve.h, v_masks)
        t.wait_ge(ve.h, v_s0)

    prev = {"qdec_gq": None, "oloc_gq": None}

    def v3(ap2d):
        return ap2d.rearrange("p (h j) -> p h j", h=H)

    for c in range(CPC):
        qap = v3(QW[:])[:, :, c * 128:(c + 1) * 128]
        kap = v3(KW[:])[:, :, c * 128:(c + 1) * 128]

        @P.block.tensor
        def _(t):
            mm = None
            for hh in range(H):
                cs = slice(c * 128, (c + 1) * 128)
                mm = t.matmul(P.PSF[:, hh * 128:(hh + 1) * 128],
                              K16[hh][:, cs], Q16[hh][:, cs],
                              start=True, stop=True)
            mm.then_inc(pe.h, 1)
        v_sc = pe.inc(1)

        qd_prev = prev["qdec_gq"]

        def vfn_a(V, c=c, v_sc=v_sc, qd_prev=qd_prev, qap=qap, kap=kap):
            V.wait_ge(pe.h, v_sc)
            if qd_prev is not None:
                V.wait_ge(qd_prev[0].h, qd_prev[1])
            V.tensor_mul(smaskall[:], P.PSF[:, 0:2048], maskTall[:])
            V.tensor_mul(v3(qtall[:]), qap, v3(lam_btall[:]))
            V.tensor_mul(v3(ktall[:]), kap, v3(lamrev_btall[:]))
            for hh in range(H):
                sl = slice(hh * 128, (hh + 1) * 128)
                V.tensor_single_scalar(qdall[:, sl], qtall[:, sl],
                                       powj_bc[:, hh * 8 + c:hh * 8 + c + 1],
                                       ALU.mult)
        v_sm = P.vbody(vfn_a)

        @P.block.gpsimd
        def _(g):
            g.wait_ge(ve.h, v_sm)
            prev["qdec_gq"] = P.sqd.issue(g, qdec_out[c], qdall[:], c)

        @P.block.tensor
        def _(t):
            t.wait_ge(ve.h, v_sm)
            mm = None
            for hh in range(H):
                hs = slice(hh * 128, (hh + 1) * 128)
                t.matmul(P.PSF[:, 2048 + hh * 128:2048 + (hh + 1) * 128],
                         V16[c][:, hs], smaskall[:, hs],
                         start=True, stop=(c == 0))
                if c > 0:
                    t.matmul(P.PSF[:, 2048 + hh * 128:2048 + (hh + 1) * 128],
                             S16[:, hs], qtall[:, hs], start=False, stop=True)
                mm = t.matmul(P.PSF[:, hh * 128:(hh + 1) * 128], ktall[:, hs],
                              C["ident16"][:], start=True, stop=True)
            mm.then_inc(pe.h, 1)
        v_oc = pe.inc(1)

        ol_prev = prev["oloc_gq"]

        def vfn_b(V, v_oc=v_oc, ol_prev=ol_prev):
            V.wait_ge(pe.h, v_oc)
            if ol_prev is not None:
                V.wait_ge(ol_prev[0].h, ol_prev[1])
            V.tensor_copy(otall[:], P.PSF[:, 2048:4096])
            V.tensor_copy(ktmall[:], P.PSF[:, 0:2048])
        v_ev = P.vbody(vfn_b)

        @P.block.gpsimd
        def _(g):
            g.wait_ge(ve.h, v_ev)
            prev["oloc_gq"] = P.sol.issue(g, oloc_out[c], otall[:], c)

        @P.block.tensor
        def _(t):
            t.wait_ge(ve.h, v_ev)
            mm = None
            for hh in range(H):
                hs = slice(hh * 128, (hh + 1) * 128)
                mm = t.matmul(P.PSF[:, 2048 + hh * 128:2048 + (hh + 1) * 128],
                              ktmall[:, hs], V16[c][:, hs], start=True, stop=True)
            mm.then_inc(pe.h, 1)
        v_kv = pe.inc(1)

        def vfn_c(V, c=c, v_kv=v_kv):
            V.wait_ge(pe.h, v_kv)
            V.tensor_mul(S[:], S[:], lamT_ball[:])
            V.tensor_add(S[:], S[:], P.PSF[:, 2048:4096])
            if c < CPC - 1:
                V.tensor_copy(S16[:], S[:])
        P.vbody(vfn_c)

    v_scan = ve.n

    @P.block.gpsimd
    def _(g):
        g.wait_ge(ve.h, v_scan)
        P.mvec.issue(g, kvagg_out, S[:])

    sc_main.close()
    P.default_sc = None


def build_l0a():
    P = LB("l0a").start()
    io = {
        "xT": P.din("xT", [DM, TOK], F32).ap(),
        "qkv_ag": P.din("qkv_ag", [DM, 3 * DM], BF16).ap(),
        "qkvnrm": P.din("qkvnrm", [DM, 1], F32).ap(),
        "s_col": P.din("s_col", [H, 1], F32).ap(),
        "s_row": P.din("s_row", [1, H], F32).ap(),
        "xn0": P.dout("xn_out", [DM, TOK], BF16).ap(),
        "oloc": P.dout("oloc_out", [CPC, 128, H * 128], BF16).ap(),
        "qdec": P.dout("qdec_out", [CPC, 128, H * 128], BF16).ap(),
        "kvagg": P.dout("kvagg_out", [128, H * 128], F32).ap(),
        "mq": P.dout("mq_out", [1, 1], F32).ap(),
    }
    C = emit_consts(P)
    emit_l0a(P, C, io)

    @P.block.gpsimd
    def _(g):
        P.wait_all_streams(g)

    @P.block.sync
    def _(sy):
        P.wait_all_streams(sy)

    return P.finish()


def emit_rms_apply(P, C, sc, tiles_in, out_tiles, normw_sb, eps, tag,
                   entry=(), bank_row=0, bank_bc=2):
    pe, ac, ve, gp = (P.sem(s) for s in ("pe", "ac", "ve", "gp"))
    ps_row = P.PSF[0:1, bank_row * 512:bank_row * 512 + 1024]
    ps_bc = P.PSF[:, bank_bc * 512:bank_bc * 512 + 1024]
    sq = [P.sbuf(f"{tag}_sq{i}", [128, 1024], BF16, sc) for i in range(2)]
    row = P.sbuf(f"{tag}_row", [1, 1024], F32, sc)
    epst = P.sbuf(f"{tag}_eps", [1, 1], F32, sc)
    rb = P.sbuf(f"{tag}_rb", [128, 1024], F32, sc)

    def gfn(g, G):
        G.memset(epst[:], eps)
    v_eps = P.gbody(gfn)

    ac_done, pe_done = [], []
    for i in range(DT):
        bi = i % 2

        def sfn(S_, i=i, bi=bi):
            if i == 0:
                for es, evv in entry:
                    S_.wait_ge(es.h, evv)
            if i >= 2:
                S_.wait_ge(pe.h, pe_done[i - 2])
            S_.activation(sq[bi][:], tiles_in[i][:], AF.Square)
        ac_done.append(P.sbody(sfn))

        @P.block.tensor
        def _(t):
            t.wait_ge(ac.h, ac_done[i])
            mm = None
            for off in (0, 512):
                mm = t.matmul(ps_row[0:1, off:off + 512], C["ones16"][:, 0:1],
                              sq[bi][:, off:off + 512], start=(i == 0),
                              stop=(i == DT - 1))
            mm.then_inc(pe.h, 1)
        pe_done.append(pe.inc(1))

    def vfn1(V):
        V.wait_ge(pe.h, pe_done[-1])
        V.tensor_single_scalar(row[:], ps_row, 1.0 / DM, ALU.mult)
    v1 = P.vbody(vfn1)

    def sfn1(S_):
        S_.wait_ge(ve.h, v1)
        S_.wait_ge(gp.h, v_eps)
        S_.activation(row[:], row[:], AF.Ln, bias=epst[0:1, 0:1])
        S_.activation(row[:], row[:], AF.Exp, scale=-0.5)
    v2 = P.sbody(sfn1)

    @P.block.tensor
    def _(t):
        t.wait_ge(ac.h, v2)
        mm = None
        for off in (0, 512):
            mm = t.matmul(ps_bc[:, off:off + 512], C["ones1r32"][0:1, :],
                          row[0:1, off:off + 512], start=True, stop=True)
        mm.then_inc(pe.h, 1)
    v3 = pe.inc(1)

    def vfn2(V):
        V.wait_ge(pe.h, v3)
        V.tensor_copy(rb[:], ps_bc)
    v4 = P.vbody(vfn2)

    out_ready = []
    for i in range(DT):
        def vfn3(V, i=i):
            V.scalar_tensor_tensor(out_tiles[i][:], tiles_in[i][:],
                                   normw_sb[:, i:i + 1], rb[:], ALU.mult, ALU.mult)
        out_ready.append(P.vbody(vfn3))
    return out_ready



def emit_l0b(P, C, io):
    P.tag = "b"
    sc_main = ExitStack()
    P.default_sc = sc_main
    xT = io["xT"]
    xnT = io["xn0"]
    oloc_in = io["oloc"]
    qdec_in = io["qdec"]
    kvpre = io["kvall"]
    wfac_row = io["wfac_row"]
    gap_row = io["gap_row"]
    s_col = io["s_col"]
    normw_in = io["normw_in"]
    pnormw_in = io["pnormw_in"]
    x1_out = io["x1"]
    xn1_out = io["xn1"]

    wp_q = io["prj_ag"]
    w1_q = io["w1_ag"]
    w3_q = io["w3_ag"]
    w2_q = io["w2_ag"]
    mrow_in = io["mrow"]
    rowscr = P.dram("rowscr", [1, 4096], F32).ap()

    pe, ac, ve, gp = (P.sem(s) for s in ("pe", "ac", "ve", "gp"))

    s_sb = P.sbuf("s_sb", [H, 1], F32)
    mq_sb = P.sbuf("mq_sb", [1, 1], F32)
    nrmw = P.sbuf("nrmw", [128, DT], F32)
    pnrmw = P.sbuf("pnrmw", [128, DT], F32)
    facs16 = P.sbuf("facs16", [H, 3], F32)
    facs_bc = P.sbuf("facs_bc", [128, 48], F32)
    alpha_vec = P.sbuf("alpha_vec", [128, 1], F32)
    s23_vec = P.sbuf("s23_vec", [128, 1], F32)
    S_in = P.sbuf("S_in", [128, H * 128], F32)
    S_in16 = P.sbuf("S_in16", [128, H * 128], BF16)
    xn16 = [P.sbuf(f"xn16_{i}", [128, 1024], BF16) for i in range(DT)]
    sbr = [P.sbuf(f"sbr_{i}", [128, 1024], BF16) for i in range(DT)]
    mrow_sb = P.sbuf("mrow_sb", [1, 8], F32)
    mvec_bc = P.sbuf("mvec_bc", [128, 8], F32)
    OA = [P.sbuf(f"OA{i}", [128, 1024], F32) for i in range(DT)]
    o_acc = OA
    sc_o2 = ExitStack()
    o2t = [P.sbuf(f"o2t_{i}", [128, 1024], BF16, sc_o2) for i in range(DT)]


    @P.block.gpsimd
    def _(g):
        P.mvec.issue(g, s_sb[:], s_col)
        P.mvec.issue(g, mq_sb[:], mrow_in[0:1, 0:1])
        P.mvec.issue(g, mrow_sb[:], mrow_in)
        for i in range(DT):
            P.mvec.issue(g, nrmw[:, i:i + 1], normw_in[i * 128:(i + 1) * 128, 0:1])
            P.mvec.issue(g, pnrmw[:, i:i + 1], pnormw_in[i * 128:(i + 1) * 128, 0:1])
        for i in range(DT):
            P.sxn.issue(g, xn16[i][:], xnT[i * 128:(i + 1) * 128, :])
    v_vl = (P.mvec.sems[0], P.mvec.sems[0].n)
    v_xnld = (P.sxn.sems[0], P.sxn.sems[0].n)

    sc_fac = ExitStack()
    s128 = P.sbuf("s128", [H, 1], F32, sc_fac)
    gap_bc = P.sbuf("gap_bc", [H, 3], F32, sc_fac)
    w_bc = P.sbuf("w_bc", [H, 3], F32, sc_fac)
    arow = P.sbuf("arow", [1, 1], F32, sc_fac)
    facrow = P.sbuf("facrow", [1, 48], F32, sc_fac)
    gapw = P.sbuf("gapw", [1, 6], F32, sc_fac)

    @P.block.gpsimd
    def _(g):
        P.sgw.issue(g, gapw[0:1, 0:3], gap_row)
        P.sgw.issue(g, gapw[0:1, 3:6], wfac_row)
    v_gw = (P.sgw.sems[0], P.sgw.sems[0].n)

    @P.block.tensor
    def _(t):
        t.wait_ge(v_gw[0].h, v_gw[1])
        t.matmul(P.bankc(6, 6, rows=H), C["ones1r32"][0:1, 0:H], gapw[0:1, 0:6],
                 start=True, stop=True).then_inc(pe.h, 1)
    v_gb = pe.inc(1)

    def vfn_f1(V):
        V.wait_ge(pe.h, v_gb)
        V.wait_ge(v_vl[0].h, v_vl[1])
        V.tensor_copy(gap_bc[:], P.PSF[0:H, 6 * 512:6 * 512 + 3])
        V.tensor_copy(w_bc[:], P.PSF[0:H, 6 * 512 + 3:6 * 512 + 6])
        V.tensor_single_scalar(s128[:], s_sb[:], -float(T), ALU.mult)
    v_f1 = P.vbody(vfn_f1)

    def sfn_f(S_):
        S_.wait_ge(ve.h, v_f1)
        S_.activation(facs16[:], gap_bc[:], AF.Exp, scale=s128[:, 0:1])
    v_f2 = P.sbody(sfn_f)

    def vfn_f2(V):
        V.wait_ge(ac.h, v_f2)
        V.tensor_mul(facs16[:], facs16[:], w_bc[:])
        V.tensor_copy(arow[:], mq_sb[:])
        V.tensor_mul(gapw[0:1, 0:1], arow[:], arow[:])
        V.tensor_mul(arow[:], gapw[0:1, 0:1], arow[:])
        V.tensor_single_scalar(arow[:], arow[:], INV_SQRT_HD, ALU.mult)
    v_f3 = P.vbody(vfn_f2)

    @P.block.gpsimd
    def _(g):
        g.wait_ge(ve.h, v_f3)
        for hh in range(H):
            P.mvec.issue(g, rowscr[0:1, hh * 3:(hh + 1) * 3], facs16[hh:hh + 1, :])
    v_sp = (P.mvec.sems[0], P.mvec.sems[0].n)

    @P.block.gpsimd
    def _(g):
        g.wait_ge(v_sp[0].h, v_sp[1])
        P.mvec.issue(g, facrow[:], rowscr[0:1, 0:48])
    v_fr = (P.mvec.sems[0], P.mvec.sems[0].n)

    @P.block.tensor
    def _(t):
        t.wait_ge(v_fr[0].h, v_fr[1])
        t.matmul(P.bankc(6, 48), C["ones1r32"][0:1, :], facrow[0:1, 0:48],
                 start=True, stop=True)
        t.matmul(P.bankc(7, 1), C["ones1r32"][0:1, :], arow[0:1, 0:1],
                 start=True, stop=True)
        t.matmul(P.PSF[:, 7 * 512 + 8:7 * 512 + 16], C["ones1r32"][0:1, :],
                 mrow_sb[0:1, 0:8], start=True, stop=True).then_inc(pe.h, 1)
    v_fb = pe.inc(1)

    def vfn_f4(V):
        V.wait_ge(pe.h, v_fb)
        V.tensor_copy(facs_bc[:, 0:48], P.bankc(6, 48))
        V.tensor_copy(alpha_vec[:], P.bankc(7, 1))
        V.tensor_copy(mvec_bc[:], P.PSF[:, 7 * 512 + 8:7 * 512 + 16])
    v_facs = P.vbody(vfn_f4)
    sc_fac.close()

    sc_kv = ExitStack()
    kvb = [P.sbuf(f"kvb{i}", [128, H * 128], F32, sc_kv) for i in range(3)]
    kv_ld = []

    @P.block.gpsimd
    def _(g):
        g.wait_ge(ve.h, v_facs)
        kv_ld.append(P.mld.issue(g, kvb[0][:], kvpre[0], 0))
        kv_ld.append(P.mld.issue(g, kvb[1][:], kvpre[1], 1))
        kv_ld.append(P.sqd.issue(g, kvb[2][:], kvpre[2], 0))

    def vfn_s(V):
        V.wait_ge(ve.h, v_facs)
        V.memset(S_in[:], 0.0)
        for p in range(3):
            V.wait_ge(kv_ld[p][0].h, kv_ld[p][1])
            for hh in range(H):
                hs = slice(hh * 128, (hh + 1) * 128)
                V.scalar_tensor_tensor(S_in[:, hs], kvb[p][:, hs],
                                       facs_bc[:, hh * 3 + p:hh * 3 + p + 1],
                                       S_in[:, hs], ALU.mult, ALU.add)
        V.tensor_copy(S_in16[:], S_in[:])
    v_sin = P.vbody(vfn_s)
    sc_kv.close()

    sc_o = ExitStack()
    o_sb = OA
    qdb = [P.sbuf(f"qdb{i}", [128, 1024], BF16, sc_o) for i in range(2)]
    olb = [P.sbuf(f"olb{i}", [128, 1024], BF16, sc_o) for i in range(2)]
    corr_pe, comb_ve, qld, old = [], [], [], []
    for hh in range(H):
        bi = hh % 2

        @P.block.gpsimd
        def _(g):
            if hh >= 2:
                g.wait_ge(ve.h, comb_ve[hh - 2])
            else:
                g.wait_ge(ve.h, v_sin)
            qsrc = qdec_in[:, :, hh * 128:(hh + 1) * 128].rearrange(
                "c p j -> p c j")
            osrc = oloc_in[:, :, hh * 128:(hh + 1) * 128].rearrange(
                "c p j -> p c j")
            qld.append(P.sqd.issue(
                g, qdb[bi][:].rearrange("p (c j) -> p c j", c=CPC), qsrc, hh))
            old.append(P.sol.issue(
                g, olb[bi][:].rearrange("p (c j) -> p c j", c=CPC), osrc, hh))

        @P.block.tensor
        def _(t):
            t.wait_ge(qld[hh][0].h, qld[hh][1])
            if hh == 0:
                t.wait_ge(ve.h, v_sin)
            if hh >= 2:
                t.wait_ge(ve.h, comb_ve[hh - 2])
            bp = (hh % 2) * 2
            mm = None
            for off in (0, 512):
                mm = t.matmul(P.PSF[:, bp * 512 + off:bp * 512 + off + 512],
                              S_in16[:, hh * 128:(hh + 1) * 128],
                              qdb[bi][:, off:off + 512], start=True, stop=True)
            mm.then_inc(pe.h, 1)
        corr_pe.append(pe.inc(1))

        def vfn_cb(V, hh=hh, bi=bi):
            V.wait_ge(pe.h, corr_pe[hh])
            V.wait_ge(old[hh][0].h, old[hh][1])
            bp = (hh % 2) * 2
            V.tensor_add(o_sb[hh][:], P.PSF[:, bp * 512:bp * 512 + 1024], olb[bi][:])
            V.tensor_single_scalar(o_sb[hh][:], o_sb[hh][:], alpha_vec[:, 0:1],
                                   ALU.mult)
        comb_ve.append(P.vbody(vfn_cb))

    sc_rms = ExitStack()
    r1 = emit_rms_apply(P, C, sc_rms, o_sb, o_sb, nrmw, 1e-6, "rm1",
                        entry=[(ve, comb_ve[-1])], bank_row=4, bank_bc=6)
    r2 = emit_rms_apply(P, C, sc_rms, o_sb, o2t, pnrmw, EPS32, "rm2",
                        entry=[(ve, r1[-1])], bank_row=4, bank_bc=6)
    sc_rms.close()
    sc_o.close()
    v_o2 = r2[-1]

    sc_mp = ExitStack()
    wpt = [P.sbuf(f"wpt{i}", [128, 512], BF16, sc_mp) for i in range(2)]
    ldv = {}
    mm_pe = {}
    li = [0]

    def pload(g_or_sy_unused):
        pass

    evac = [0]
    for og in range(4):
        for kt in range(DT):
            i = li[0]
            bi = i % 2

            @P.block.sync
            def _(sy):
                if i == 0:
                    sy.wait_ge(ve.h, v_o2)
                if i >= 2:
                    sy.wait_ge(pe.h, mm_pe[i - 2])
                ldv[i] = P.wl.issue(sy, wpt[bi][:],
                                    wp_q[kt * 128:(kt + 1) * 128,
                                         og * 512:(og + 1) * 512], i)

            @P.block.tensor
            def _(t):
                t.wait_ge(ldv[i][0].h, ldv[i][1])
                if kt == 0 and evac[0]:
                    t.wait_ge(ve.h, evac[0])
                mm = None
                for k in range(4):
                    for off in (0, 512):
                        mm = t.matmul(P.PSF[:, k * 1024 + off:k * 1024 + off + 512],
                                      wpt[bi][:, k * 128:(k + 1) * 128],
                                      o2t[kt][:, off:off + 512],
                                      start=(kt == 0), stop=(kt == DT - 1))
                mm.then_inc(pe.h, 1)
            mm_pe[i] = pe.inc(1)
            li[0] += 1
        v_g = pe.n

        def vfn_pe(V, og=og, v_g=v_g):
            V.wait_ge(pe.h, v_g)
            for k in range(4):
                ot = og * 4 + k
                V.tensor_single_scalar(sbr[ot][:], P.PSF[:, k * 1024:(k + 1) * 1024],
                                       mvec_bc[:, 1:2], ALU.mult)
        evac[0] = P.vbody(vfn_pe)
    v_sbr = evac[0]
    sc_mp.close()
    sc_o2.close()

    sc_mf = ExitStack()
    stq_tot_f = []
    s23row = P.sbuf("s23row", [1, 1], F32, sc_mf)

    def vfn_s23(V):
        V.wait_ge(ve.h, v_sbr)
        V.tensor_mul(s23row[:], mrow_sb[0:1, 3:4], mrow_sb[0:1, 4:5])
    v_s23 = P.vbody(vfn_s23)

    @P.block.tensor
    def _(t):
        t.wait_ge(ve.h, v_s23)
        t.matmul(P.bankc(7, 1), C["ones1r32"][0:1, :], s23row[0:1, 0:1],
                 start=True, stop=True).then_inc(pe.h, 1)
    v_s23b = pe.inc(1)

    def vfn_s23c(V):
        V.wait_ge(pe.h, v_s23b)
        V.tensor_copy(s23_vec[:], P.bankc(7, 1))
    v_s23v = P.vbody(vfn_s23c)
    s1_vec = mvec_bc[:, 2:3]

    GW = 3
    sc_ff = ExitStack()
    w1b = [P.sbuf(f"w1b{k}", [128, GW * 128], BF16, sc_ff) for k in range(DT)]
    w3b = [P.sbuf(f"w3b{k}", [128, GW * 128], BF16, sc_ff) for k in range(DT)]
    w2b = [P.sbuf(f"w2b{l}", [128, 2048], BF16, sc_ff) for l in range(GW)]
    g16 = [P.sbuf(f"g16_{l}", [128, 1024], BF16, sc_ff) for l in range(GW)]
    silu_t = P.sbuf("silu_t", [128, 1024], BF16, sc_ff)

    groups = []
    nht = (HID + 127) // 128
    gstart = 0
    while gstart < nht:
        groups.append(list(range(gstart, min(gstart + GW, nht))))
        gstart += GW

    prev_gate = [v_s23v]
    pha_last = {}
    phb_last = {}
    for gi, grp in enumerate(groups):
        pg_ve = prev_gate[0]
        h0 = grp[0] * 128
        gw = sum(min(128, HID - ht * 128) for ht in grp)
        lds = []

        @P.block.gpsimd
        def _(g):
            if gi == 0:
                g.wait_ge(ve.h, v_s23v)
                for s_, vv in stq_tot_f:
                    g.wait_ge(s_.h, vv)
            else:
                g.wait_ge(pe.h, pha_last[gi - 1])
            for kt in range(DT):
                lds.append(P.mld.issue(g, w1b[kt][0:128, 0:gw],
                                       w1_q[kt * 128:(kt + 1) * 128, h0:h0 + gw], 0))
                lds.append(P.sqd.issue(g, w3b[kt][0:128, 0:gw],
                                       w3_q[kt * 128:(kt + 1) * 128, h0:h0 + gw], 0))

        @P.block.gpsimd
        def _(g):
            if gi >= 1:
                g.wait_ge(pe.h, phb_last[gi - 1])
            for li2, ht in enumerate(grp):
                hw = min(128, HID - ht * 128)
                P.sol.issue(g, w2b[li2][0:hw, :],
                            w2_q[ht * 128:ht * 128 + hw, :], 0)

        ld_done = [(s, s.n) for st in (P.mld, P.sqd) for s in st.sems]

        ph_a_pe = []
        ph_a_ve = []
        for li2, ht in enumerate(grp):
            hw = min(128, HID - ht * 128)
            co = sum(min(128, HID - grp[m] * 128) for m in range(li2))
            bp = (li2 % 2) * 4

            @P.block.tensor
            def _(t, pg_ve=pg_ve):
                if li2 == 0:
                    for s_, vv in ld_done:
                        t.wait_ge(s_.h, vv)
                    t.wait_ge(v_xnld[0].h, v_xnld[1])
                    if pg_ve:
                        t.wait_ge(ve.h, pg_ve)
                if li2 >= 2:
                    t.wait_ge(ve.h, ph_a_ve[li2 - 2])
                mm = None
                for kt in range(DT):
                    for off in (0, 512):
                        t.matmul(P.PSF[0:hw, bp * 512 + off:bp * 512 + off + 512],
                                 w1b[kt][:, co:co + hw],
                                 xn16[kt][:, off:off + 512],
                                 start=(kt == 0), stop=(kt == DT - 1))
                        mm = t.matmul(
                            P.PSF[0:hw, (bp + 2) * 512 + off:(bp + 2) * 512 + off + 512],
                            w3b[kt][:, co:co + hw],
                            xn16[kt][:, off:off + 512],
                            start=(kt == 0), stop=(kt == DT - 1))
                mm.then_inc(pe.h, 1)
            ph_a_pe.append(pe.inc(1))

            def sfn_si(S_, li2=li2, hw=hw, bp=bp):
                S_.wait_ge(pe.h, ph_a_pe[li2])
                if li2 >= 1:
                    S_.wait_ge(ve.h, ph_a_ve[li2 - 1])
                S_.activation(silu_t[0:hw, :], P.PSF[0:hw, bp * 512:bp * 512 + 1024],
                              AF.Silu, scale=mvec_bc[0:hw, 2:3])
            v_si = P.sbody(sfn_si)

            def vfn_g(V, li2=li2, hw=hw, bp=bp, v_si=v_si):
                V.wait_ge(ac.h, v_si)
                V.tensor_mul(g16[li2][0:hw, :], silu_t[0:hw, :],
                             P.PSF[0:hw, (bp + 2) * 512:(bp + 2) * 512 + 1024])
            ph_a_ve.append(P.vbody(vfn_g))

        ldb_done = [(s, s.n) for s in P.sol.sems]

        ot_pe, ot_ve = [], []
        for ot in range(DT):
            bp = (ot % 2) * 2

            @P.block.tensor
            def _(t):
                if ot == 0:
                    for s_, vv in ldb_done:
                        t.wait_ge(s_.h, vv)
                    t.wait_ge(ve.h, ph_a_ve[-1])
                if ot >= 2:
                    t.wait_ge(ve.h, ot_ve[ot - 2])
                mm = None
                for li2, ht in enumerate(grp):
                    hw = min(128, HID - ht * 128)
                    for off in (0, 512):
                        mm = t.matmul(P.PSF[:, bp * 512 + off:bp * 512 + off + 512],
                                      w2b[li2][0:hw, ot * 128:(ot + 1) * 128],
                                      g16[li2][0:hw, off:off + 512],
                                      start=(li2 == 0), stop=(li2 == len(grp) - 1))
                mm.then_inc(pe.h, 1)
            ot_pe.append(pe.inc(1))

            def vfn_oa(V, ot=ot, gi=gi, bp=bp):
                V.wait_ge(pe.h, ot_pe[ot])
                if gi == 0:
                    V.tensor_copy(o_acc[ot][:], P.PSF[:, bp * 512:bp * 512 + 1024])
                else:
                    V.tensor_add(o_acc[ot][:], o_acc[ot][:],
                                 P.PSF[:, bp * 512:bp * 512 + 1024])
            ot_ve.append(P.vbody(vfn_oa))
        prev_gate[0] = ot_ve[-1]
        pha_last[gi] = ph_a_pe[-1]
        phb_last[gi] = ot_pe[-1]

    v_ffn = prev_gate[0]
    sc_ff.close()
    sc_mf.close()


    sc_cb = ExitStack()
    xb = [P.sbuf(f"xb{i}", [128, 1024], F32, sc_cb) for i in range(2)]
    gt = P.sbuf("gt", [128, 1024], BF16, sc_cb)
    tmpf = P.sbuf("tmpf", [128, 1024], F32, sc_cb)
    x1st = []
    cb_ve = []
    for i in range(DT):
        bi = i % 2
        xldi = []

        @P.block.gpsimd
        def _(g):
            if i < 2:
                g.wait_ge(pe.h, phb_last[len(groups) - 1])
            if i >= 2:
                g.wait_ge(ve.h, cb_ve[i - 2])
                g.wait_ge(x1st[i - 2][0].h, x1st[i - 2][1])
            xldi.append(P.mld.issue(g, xb[bi][:], xT[i * 128:(i + 1) * 128, :], i))

        def sfn_cb(S_, i=i):
            if i == 0:
                S_.wait_ge(ve.h, v_ffn)
            else:
                S_.wait_ge(ve.h, cb_ve[i - 1])
            S_.activation(gt[:], o_acc[i][:], AF.Silu, scale=s23_vec[:, 0:1])
        v_gt = P.sbody(sfn_cb)

        def vfn_cb2(V, i=i, bi=bi, v_gt=v_gt, xldi=xldi):
            V.wait_ge(ac.h, v_gt)
            V.wait_ge(xldi[0][0].h, xldi[0][1])
            V.tensor_mul(tmpf[:], gt[:], sbr[i][:])
            V.tensor_add(xb[bi][:], xb[bi][:], tmpf[:])
        cb_ve.append(P.vbody(vfn_cb2))

        @P.block.gpsimd
        def _(g):
            g.wait_ge(ve.h, cb_ve[i])
            x1st.append(P.stx.issue(g, x1_out[i * 128:(i + 1) * 128, :], xb[bi][:], i))
    x1_totals = [(s, s.n) for s in P.stx.sems]
    sc_cb.close()

    sc_ln = ExitStack()
    alloc_pools(P, sc_ln)
    xn_ready, _hr, _st = emit_ln_h(P, C, sc_ln, x1_out, None, None, "ln1",
                                   xn_dram=xn1_out, entry=x1_totals + [(ve, cb_ve[-1])],
                                   with_h=False)
    sc_ln.close()
    sc_main.close()
    P.default_sc = None


def build_l0b():
    P = LB("l0b").start()
    io = {
        "xT": P.din("xT", [DM, TOK], F32).ap(),
        "xn0": P.din("xnT", [DM, TOK], BF16).ap(),
        "oloc": P.din("oloc_in", [CPC, 128, H * 128], BF16).ap(),
        "qdec": P.din("qdec_in", [CPC, 128, H * 128], BF16).ap(),
        "kvall": P.din("kvpre", [3, 128, H * 128], F32).ap(),
        "wfac_row": P.din("wfac_row", [1, 3], F32).ap(),
        "gap_row": P.din("gap_row", [1, 3], F32).ap(),
        "mq": P.din("mq_in", [1, 1], F32).ap(),
        "s_col": P.din("s_col", [H, 1], F32).ap(),
        "normw_in": P.din("normw_in", [DM, 1], F32).ap(),
        "pnormw_in": P.din("pnormw_in", [DM, 1], F32).ap(),
        "prj_ag": P.din("prj_ag", [DM, DM], BF16).ap(),
        "w1_ag": P.din("w1_ag", [DM, HID], BF16).ap(),
        "w3_ag": P.din("w3_ag", [DM, HID], BF16).ap(),
        "w2_ag": P.din("w2_ag", [8 * 683, DM], BF16).ap(),
        "mrow": P.din("mrow", [1, 8], F32).ap(),
        "x1": P.dout("x1_out", [DM, TOK], F32).ap(),
        "xn1": P.dout("xn1_out", [DM, TOK], BF16).ap(),
    }
    C = emit_consts(P)
    emit_l0b(P, C, io)

    @P.block.gpsimd
    def _(g):
        P.wait_all_streams(g)

    @P.block.sync
    def _(sy):
        P.wait_all_streams(sy)

    return P.finish()



def emit_l1q(P, C, io):
    P.tag = "q"
    sc_main = ExitStack()
    P.default_sc = sc_main
    xnT = io["xn1"]
    wattT = io["watt_ag"]
    pos_row = io["pos_row"]
    qkvpub = io["qkvpub"]

    pe, ac, ve, gp = (P.sem(s) for s in ("pe", "ac", "ve", "gp"))

    xn16 = [P.sbuf(f"xn16_{i}", [128, 1024], BF16) for i in range(DT)]
    Q16 = [P.sbuf(f"Q16_{h}", [128, 1024], BF16) for h in range(H)]
    K16 = [P.sbuf(f"K16_{h}", [128, 1024], BF16) for h in range(H)]
    V16 = [P.sbuf(f"V16_{c}", [128, 2048], BF16) for c in range(CPC)]
    wqt = [P.sbuf(f"wqt{i}", [128, 512], BF16) for i in range(2)]
    cos_t = P.sbuf("cos_t", [128, 1024], F32)
    sin_sgn = P.sbuf("sin_sgn", [128, 1024], F32)
    swapm16 = P.sbuf("swapm16", [128, 128], BF16)
    tmpA = P.sbuf("tmpA", [128, 1024], F32)
    tmpB = P.sbuf("tmpB", [128, 1024], F32)
    posr = P.sbuf("posr", [1, TOK], F32)

    @P.block.gpsimd
    def _(g):
        for i in range(DT):
            P.sxn.issue(g, xn16[i][:], xnT[i * 128:(i + 1) * 128, :])
        P.srp.issue(g, posr[:], pos_row)
    v_xnld = (P.sxn.sems[0], P.sxn.sems[0].n)
    v_pos = (P.srp.sems[0], P.srp.sems[0].n)

    rowscr = P.dram("rowscr", [1, 256], F32).ap()
    sc_r = ExitStack()
    idxrow = P.sbuf("idxrow", [1, 128], F32, sc_r)
    parrow = P.sbuf("parrow", [1, 128], F32, sc_r)
    i2 = P.sbuf("i2", [128, 1], F32, sc_r)
    theta = P.sbuf("theta", [128, 1], F32, sc_r)
    e_col = P.sbuf("e_col", [128, 1], F32, sc_r)
    ev_col = P.sbuf("ev_col", [128, 1], F32, sc_r)
    sgn_col = P.sbuf("sgn_col", [128, 1], F32, sc_r)
    ang = P.sbuf("ang", [128, 1024], F32, sc_r)
    zf = P.sbuf("zf", [128, 1024], F32, sc_r)
    ibig = P.sbuf("ibig", [128, 1024], mybir.dt.int32, sc_r)
    gtp = P.sbuf("gtp", [128, 1024], F32, sc_r)
    a1t = P.sbuf("a1t", [128, 128], F32, sc_r)
    a2t = P.sbuf("a2t", [128, 128], F32, sc_r)

    def gfn(g, G):
        G.iota(idxrow[:], pattern=[[1, 64], [0, 2]], base=0, channel_multiplier=0,
               allow_small_or_imprecise_dtypes=True)
        G.iota(parrow[:], pattern=[[0, 64], [1, 2]], base=0, channel_multiplier=0,
               allow_small_or_imprecise_dtypes=True)
    v_pcol = P.gbody(gfn)

    @P.block.gpsimd
    def _(g):
        g.wait_ge(gp.h, v_pcol)
        P.srr.issue(g, rowscr[0:1, 0:128], idxrow[:])
        P.srr.issue(g, rowscr[0:1, 128:256], parrow[:])
    v_rs1 = (P.srr.sems[0], P.srr.sems[0].n)

    @P.block.gpsimd
    def _(g):
        g.wait_ge(v_rs1[0].h, v_rs1[1])
        P.srr.issue(g, i2[:], rowscr[0:1, 0:128])
        P.srr.issue(g, e_col[:], rowscr[0:1, 128:256])
    v_rs2 = (P.srr.sems[0], P.srr.sems[0].n)

    def vfn_th(V):
        V.wait_ge(v_rs2[0].h, v_rs2[1])
        V.tensor_scalar(sgn_col[:], e_col[:], 2.0, 1.0, ALU.mult, ALU.subtract)
        V.tensor_scalar(ev_col[:], e_col[:], -1.0, 1.0, ALU.mult, ALU.add)
    v_th1 = P.vbody(vfn_th)

    def sfn_th(S_):
        S_.wait_ge(ve.h, v_th1)
        S_.activation(theta[:], i2[:], AF.Exp,
                      scale=-2.0 * math.log(10000.0) / HD)
    v_th2 = P.sbody(sfn_th)

    @P.block.tensor
    def _(t):
        t.wait_ge(v_pos[0].h, v_pos[1])
        mm = None
        for off in (0, 512):
            mm = t.matmul(P.PSF[:, off:off + 512], C["ones1r32"][0:1, :],
                          posr[0:1, off:off + 512], start=True, stop=True)
        mm.then_inc(pe.h, 1)
    v_pb = pe.inc(1)

    TWO_PI = 2.0 * math.pi

    def emit_trig(dst, phase_add, tag, prev_ac=None):
        def vfn(V):
            V.wait_ge(pe.h, v_pb)
            V.wait_ge(ac.h, v_th2)
            if prev_ac is not None:
                V.wait_ge(ac.h, prev_ac)
            V.tensor_single_scalar(ang[:], P.PSF[:, 0:1024], theta[:, 0:1], ALU.mult)
            if phase_add:
                V.tensor_single_scalar(ang[:], ang[:], phase_add, ALU.add)
            V.tensor_single_scalar(zf[:], ang[:], 1.0 / TWO_PI, ALU.mult)
            V.tensor_copy(ibig[:], zf[:])
            V.tensor_copy(zf[:], ibig[:])
            V.scalar_tensor_tensor(ang[:], zf[:], -TWO_PI, ang[:], ALU.mult, ALU.add)
            V.tensor_single_scalar(gtp[:], ang[:], math.pi, ALU.is_gt)
            V.scalar_tensor_tensor(ang[:], gtp[:], -TWO_PI, ang[:], ALU.mult, ALU.add)
        v1 = P.vbody(vfn)

        def sfn(S_):
            S_.wait_ge(ve.h, v1)
            S_.activation(dst[:], ang[:], AF.Sin)
        return P.sbody(sfn)

    v_sin = emit_trig(sin_sgn, 0.0, "s")
    v_cos = emit_trig(cos_t, math.pi / 2.0, "c", prev_ac=v_sin)

    def vfn_sg(V):
        V.wait_ge(ac.h, v_cos)
        V.tensor_single_scalar(sin_sgn[:], sin_sgn[:], sgn_col[:, 0:1], ALU.mult)
        V.tensor_single_scalar(a1t[:], C["iota_pf"][:], -1.0, ALU.is_equal)
        V.tensor_single_scalar(a2t[:], C["iota_pf"][:], 1.0, ALU.is_equal)
        V.tensor_single_scalar(a1t[:], a1t[:], ev_col[:, 0:1], ALU.mult)
        V.tensor_single_scalar(a2t[:], a2t[:], e_col[:, 0:1], ALU.mult)
        V.tensor_add(swapm16[:], a1t[:], a2t[:])
    v_rope_tabs = P.vbody(vfn_sg)
    sc_r.close()

    li = [0]
    ldv = {}
    mm_pe = {}

    def wload(base_f, kt):
        i = li[0]
        bi = i % 2

        @P.block.gpsimd
        def _(g):
            if i == 0:
                g.wait_ge(v_xnld[0].h, v_xnld[1])
                g.wait_ge(ve.h, v_rope_tabs)
            if i >= 2:
                g.wait_ge(pe.h, mm_pe[i - 2])
            ldv[i] = P.wlg.issue(g, wqt[bi][:],
                                 wattT[kt * 128:(kt + 1) * 128, base_f:base_f + 512], i)
        return i, bi

    evac = [0]
    for vg in range(4):
        base_f = 2 * DM + vg * 512
        for kt in range(DT):
            i, bi = wload(base_f, kt)

            @P.block.tensor
            def _(t):
                t.wait_ge(ldv[i][0].h, ldv[i][1])
                if kt == 0 and evac[0]:
                    t.wait_ge(ve.h, evac[0])
                mm = None
                for c in range(CPC):
                    mm = t.matmul(P.bank(c), xn16[kt][:, c * 128:(c + 1) * 128],
                                  wqt[bi][:, 0:512],
                                  start=(kt == 0), stop=(kt == DT - 1))
                mm.then_inc(pe.h, 1)
            mm_pe[i] = pe.inc(1)
            li[0] += 1
        v_vg = pe.n

        def vfn_vev(V, vg=vg, v_vg=v_vg):
            V.wait_ge(pe.h, v_vg)
            for c in range(CPC):
                V.tensor_copy(V16[c][:, vg * 512:(vg + 1) * 512], P.bank(c))
        evac[0] = P.vbody(vfn_vev)
    v_vdone = evac[0]

    @P.block.gpsimd
    def _(g):
        g.wait_ge(ve.h, v_vdone)
        for c in range(CPC):
            for hh in range(H):
                P.svo.issue(g,
                            qkvpub[4096 + hh * 128:4096 + (hh + 1) * 128,
                                   c * 128:(c + 1) * 128],
                            V16[c][:, hh * 128:(hh + 1) * 128], c)

    if "qkvall" in io:
        cc = P.sem("cc")
        _vg = [(s.h, s.n) for s in P.svo.sems if s.n]

        @P.block.gpsimd
        def _(g, _vg=_vg):
            for h_, v_ in _vg:
                g.wait_ge(h_, v_)
            g.collective_compute(
                "AllGather", ALU.bypass, replica_groups=io["cc_groups"],
                ins=[qkvpub[4096:6144, :]],
                outs=[io["qkvall"][0:8192, :]]).then_inc(cc.h, 1)
        cc.inc(1)

    for role in range(2):
        for grp in range(4):
            base_f = role * DM + grp * 512
            for kt in range(DT):
                i, bi = wload(base_f, kt)

                @P.block.tensor
                def _(t):
                    t.wait_ge(ldv[i][0].h, ldv[i][1])
                    if kt == 0 and evac[0]:
                        t.wait_ge(ve.h, evac[0])
                    mm = None
                    for k in range(4):
                        for off in (0, 512):
                            mm = t.matmul(
                                P.PSF[:, k * 1024 + off:k * 1024 + off + 512],
                                wqt[bi][:, k * 128:(k + 1) * 128],
                                xn16[kt][:, off:off + 512],
                                start=(kt == 0), stop=(kt == DT - 1))
                    mm.then_inc(pe.h, 1)
                mm_pe[i] = pe.inc(1)
                li[0] += 1
            v_g = pe.n

            def vfn_ev(V, grp=grp, role=role, v_g=v_g):
                V.wait_ge(pe.h, v_g)
                for k in range(4):
                    hh = grp * 4 + k
                    dst = Q16[hh] if role == 0 else K16[hh]
                    V.tensor_copy(dst[:], P.PSF[:, k * 1024:(k + 1) * 1024])
            evac[0] = P.vbody(vfn_ev)
    v_qkv = evac[0]

    st_prev = {}
    rope_ve = []
    it = [0]
    for grp4 in range(4):
        for role in range(2):
          for k4 in range(4):
            hh = grp4 * 4 + k4
            i = it[0]
            bp = (i % 2) * 2
            src_t = Q16[hh] if role == 0 else K16[hh]
            dst_rows = (0 if role == 0 else 2048) + hh * 128

            @P.block.tensor
            def _(t):
                if i == 0:
                    t.wait_ge(ve.h, v_qkv)
                    t.wait_ge(ve.h, v_rope_tabs)
                if i >= 2:
                    t.wait_ge(ve.h, rope_ve[i - 2])
                mm = None
                for off in (0, 512):
                    mm = t.matmul(P.PSF[:, bp * 512 + off:bp * 512 + off + 512],
                                  swapm16[:], src_t[:, off:off + 512],
                                  start=True, stop=True)
                mm.then_inc(pe.h, 1)
            v_sw = pe.inc(1)

            def vfn_r(V, i=i, bp=bp, src_t=src_t, v_sw=v_sw):
                V.wait_ge(pe.h, v_sw)
                if i >= 2 and (i - 2) in st_prev:
                    V.wait_ge(st_prev[i - 2][0].h, st_prev[i - 2][1])
                V.tensor_mul(tmpA[:], P.PSF[:, bp * 512:bp * 512 + 1024], sin_sgn[:])
                V.tensor_mul(tmpB[:], src_t[:], cos_t[:])
                V.tensor_add(src_t[:], tmpA[:], tmpB[:])
            rope_ve.append(P.vbody(vfn_r))

            @P.block.gpsimd
            def _(g):
                g.wait_ge(ve.h, rope_ve[i])
                st_prev[i] = P.sqo.issue(g, qkvpub[dst_rows:dst_rows + 128, :],
                                         src_t[:], i)
            it[0] += 1

            if k4 == 3 and "qkvall" in io:
                cc = P.sem("cc")
                _pg = [(s.h, s.n) for s in P.sqo.sems if s.n]
                src_r = (0 if role == 0 else 2048) + grp4 * 512
                dst_r = (8192 if role == 0 else 16384) + grp4 * 2048

                @P.block.gpsimd
                def _(g, _pg=_pg, src_r=src_r, dst_r=dst_r):
                    for h_, v_ in _pg:
                        g.wait_ge(h_, v_)
                    g.collective_compute(
                        "AllGather", ALU.bypass, replica_groups=io["cc_groups"],
                        ins=[qkvpub[src_r:src_r + 512, :]],
                        outs=[io["qkvall"][dst_r:dst_r + 2048, :]]).then_inc(
                            cc.h, 1)
                cc.inc(1)

    sc_main.close()
    P.default_sc = None


def build_l1q():
    P = LB("l1q").start()
    io = {
        "xn1": P.din("xnT", [DM, TOK], BF16).ap(),
        "watt_ag": P.din("wattT", [DM, 3 * DM], BF16).ap(),
        "pos_row": P.din("pos_row", [1, TOK], F32).ap(),
        "qkvpub": P.dout("qkvpub", [3 * DM, TOK], BF16).ap(),
    }
    C = emit_consts(P)
    emit_l1q(P, C, io)

    @P.block.gpsimd
    def _(g):
        P.wait_all_streams(g)

    @P.block.sync
    def _(sy):
        P.wait_all_streams(sy)

    return P.finish()



def emit_l1a(P, C, io):
    P.tag = "t"
    sc_main = ExitStack()
    P.default_sc = sc_main
    qkvall = io["qkvall"]
    idx_in = io["idx_l1a"]
    ypub = io["ypub"]

    pe, ac, ve, gp = (P.sem(s) for s in ("pe", "ac", "ve", "gp"))
    idxsb = P.sbuf("idxsb", [128, 48], mybir.dt.int32)

    @P.block.gpsimd
    def _(g):
        P.srr.issue(g, idxsb[:], idx_in)
    v_idx = (P.srr.sems[0], P.srr.sems[0].n)

    qh = [P.sbuf(f"qh{i}", [128, N], BF16) for i in range(2)]
    kh = [P.sbuf(f"kh{i}", [128, N], BF16) for i in range(2)]
    vh = [P.sbuf(f"vh{i}", [128, NCH * 128], BF16) for i in range(2)]
    pstr = [P.sbuf(f"pstr{i}", [128, 1024], BF16) for i in range(2)]
    zc16 = P.sbuf("zc16", [128, 128], BF16)
    rs_row = P.sbuf("rs_row", [1, 1024], F32)
    rb = P.sbuf("rb", [128, 1024], F32)
    ytile = [P.sbuf(f"ytile{i}", [128, 1024], BF16) for i in range(2)]

    def gfn_z(g, G):
        G.memset(zc16[:], 0.0)
    v_z = P.gbody(gfn_z)

    @P.block.tensor
    def _(t):
        t.wait_ge(P.sem("gp").h, v_z)

    bh_done = []
    ld_bh = []
    y_stores = {}

    for bh in range(4):
        p = bh % 2

        @P.block.gpsimd
        def _(g):
            if bh == 0:
                g.wait_ge(v_idx[0].h, v_idx[1])
            if bh >= 2:
                g.wait_ge(ve.h, bh_done[bh - 2])
            l1 = l2 = l3 = None
            for j2 in range(4):
                cidx = bh * 12 + 0 * 4 + j2
                g.indirect_dma_start(
                    qh[p][:, j2 * 1024:(j2 + 1) * 1024], None, qkvall,
                    IndirectOffsetOnAxis(idxsb[:, cidx:cidx + 1], 0)
                ).then_inc(P.lq.sems[bh % 2].h, 16)
                l1 = (P.lq.sems[bh % 2], P.lq.sems[bh % 2].inc(16))
                cidx = bh * 12 + 1 * 4 + j2
                g.indirect_dma_start(
                    kh[p][:, j2 * 1024:(j2 + 1) * 1024], None, qkvall,
                    IndirectOffsetOnAxis(idxsb[:, cidx:cidx + 1], 0)
                ).then_inc(P.lk.sems[bh % 2].h, 16)
                l2 = (P.lk.sems[bh % 2], P.lk.sems[bh % 2].inc(16))
                cidx = bh * 12 + 2 * 4 + j2
                g.indirect_dma_start(
                    vh[p][:, j2 * 1024:(j2 + 1) * 1024], None, qkvall,
                    IndirectOffsetOnAxis(idxsb[:, cidx:cidx + 1], 0)
                ).then_inc(P.lv.sems[bh % 2].h, 16)
                l3 = (P.lv.sems[bh % 2], P.lv.sems[bh % 2].inc(16))
            ld_bh.append((l1, l2, l3))

        for tblk in range(4):
            nkc = 8 * tblk + 8
            kc_pe, kc_ac, kc_ve = [], [], []
            for kc in range(nkc):
                sb = (kc % 2) * 2
                swid = 1024 if kc < 8 * tblk else 1024 - 128 * (kc - 8 * tblk)
                toff = 1024 * tblk + (1024 - swid)
                loff = 1024 - swid
                diag = kc >= 8 * tblk

                @P.block.tensor
                def _(t):
                    if kc == 0:
                        l1, l2, l3 = ld_bh[bh]
                        t.wait_ge(l1[0].h, l1[1])
                        t.wait_ge(l2[0].h, l2[1])
                        t.wait_ge(l3[0].h, l3[1])
                        if tblk == 0 and bh_done:
                            t.wait_ge(ve.h, bh_done[-1])
                        if tblk > 0:
                            t.wait_ge(ve.h, tblk_gate[0])
                    if kc >= 2:
                        t.wait_ge(ac.h, kc_ac[kc - 2])
                        if (kc - 2) < len(kc_ve) and kc_ve[kc - 2]:
                            t.wait_ge(ve.h, kc_ve[kc - 2])
                        else:
                            t.wait_ge(pe.h, kc_pe[kc - 2] + 2)
                    mm = None
                    for off in range(0, swid, 512):
                        w = min(512, swid - off)
                        mm = t.matmul(P.PSF[:, sb * 512 + off:sb * 512 + off + w],
                                      kh[p][:, kc * 128:(kc + 1) * 128],
                                      qh[p][:, toff + off:toff + off + w],
                                      start=True, stop=True)
                    mm.then_inc(pe.h, 1)
                kc_pe.append(pe.inc(1))

                def sfn_e(S_, kc=kc, sb=sb, swid=swid, p=p):
                    S_.wait_ge(pe.h, kc_pe[kc])
                    S_.activation(pstr[kc % 2][:, 0:swid],
                                  P.PSF[:, sb * 512:sb * 512 + swid],
                                  AF.Exp, scale=INV_SQRT_HD)
                kc_ac.append(P.sbody(sfn_e))

                if diag:
                    def vfn_d(V, kc=kc, p=p):
                        V.wait_ge(ac.h, kc_ac[kc])
                        V.tensor_mul(pstr[kc % 2][:, 0:128], pstr[kc % 2][:, 0:128],
                                     C["trilT"][:])
                    kc_ve.append(P.vbody(vfn_d))
                else:
                    kc_ve.append(0)

                @P.block.tensor
                def _(t):
                    if kc_ve[kc]:
                        t.wait_ge(ve.h, kc_ve[kc])
                    else:
                        t.wait_ge(ac.h, kc_ac[kc])
                    mm = None
                    lo = loff
                    while lo < 1024:
                        w = min(512 - (lo % 512), 1024 - lo)
                        t.matmul(P.PSF[0:1, 3072 + lo:3072 + lo + w],
                                 C["ones16"][:, 0:1],
                                 pstr[kc % 2][:, lo - loff:lo - loff + w],
                                 start=(kc == 0), stop=False)
                        mm = t.matmul(
                            P.PSF[:, 2048 + lo:2048 + lo + w],
                            vh[p][:, kc * 128:(kc + 1) * 128],
                            pstr[kc % 2][:, lo - loff:lo - loff + w],
                            start=(kc == 0), stop=False)
                        lo += w
                    mm.then_inc(pe.h, 2)
                pe.inc(2)

            @P.block.tensor
            def _(t):
                mm = None
                for off in (0, 512):
                    t.matmul(P.PSF[0:1, 3072 + off:3072 + off + 512],
                             zc16[:, 0:1], pstr[0][:, off:off + 512],
                             start=False, stop=True)
                    mm = t.matmul(P.PSF[:, 2048 + off:2048 + off + 512],
                                  zc16[:], pstr[0][:, off:off + 512],
                                  start=False, stop=True)
                mm.then_inc(pe.h, 1)
            pe.inc(1)
            v_last = pe.n

            def vfn_rs(V, v_last=v_last):
                V.wait_ge(pe.h, v_last)
                V.tensor_copy(rs_row[:], P.PSF[0:1, 3072:4096])
            v_rs = P.vbody(vfn_rs)

            def sfn_rs(S_):
                S_.wait_ge(ve.h, v_rs)
                S_.activation(rs_row[:], rs_row[:], AF.Ln)
                S_.activation(rs_row[:], rs_row[:], AF.Exp, scale=-1.0)
            v_rc = P.sbody(sfn_rs)

            @P.block.tensor
            def _(t):
                t.wait_ge(ac.h, v_rc)
                mm = None
                for off in (0, 512):
                    mm = t.matmul(P.PSF[:, off:off + 512], C["ones1r32"][0:1, :],
                                  rs_row[0:1, off:off + 512], start=True, stop=True)
                mm.then_inc(pe.h, 1)
            v_rb = pe.inc(1)

            yst = {}

            def vfn_y(V, tblk=tblk, p=p, v_rb=v_rb, bh=bh):
                V.wait_ge(pe.h, v_rb)
                key = tblk % 2
                if key in y_stores:
                    V.wait_ge(y_stores[key][0].h, y_stores[key][1])
                V.tensor_copy(rb[:], P.PSF[:, 0:1024])
                V.tensor_mul(ytile[tblk % 2][:], P.PSF[:, 2048:3072], rb[:])
            v_y = P.vbody(vfn_y)
            tblk_gate = [v_y]

            @P.block.gpsimd
            def _(g):
                g.wait_ge(ve.h, v_y)
                rr = (bh * 4 + tblk) * 128
                y_stores[tblk % 2] = P.sy.issue(
                    g, ypub[rr:rr + 128, :], ytile[tblk % 2][:], tblk)
        bh_done.append(tblk_gate[0])

        if "ypc" in io:
            cc = P.sem("cc")
            st_gates = [(s.h, s.n) for s in P.sy.sems if s.n]

            @P.block.gpsimd
            def _(g, st_gates=st_gates, bh=bh):
                for h_, v_ in st_gates:
                    g.wait_ge(h_, v_)
                g.collective_compute(
                    "AllGather", ALU.bypass, replica_groups=io["cc_groups"],
                    ins=[ypub[bh * 512:(bh + 1) * 512, :]],
                    outs=[io["ypc"][bh]]).then_inc(cc.h, 1)
            cc.inc(1)

    sc_main.close()
    P.default_sc = None


def build_l1a():
    P = LB("l1a").start()
    io = {
        "qkvall": P.din("qkvall", [4 * 3 * DM, TOK], BF16).ap(),
        "idx_l1a": P.din("idx_l1a", [128, 48], mybir.dt.int32).ap(),
        "ypub": P.dout("ypub", [2048, 1024], BF16).ap(),
    }
    C = emit_consts(P)
    emit_l1a(P, C, io)

    @P.block.gpsimd
    def _(g):
        P.wait_all_streams(g)

    @P.block.sync
    def _(sy_):
        P.wait_all_streams(sy_)

    return P.finish()



def emit_l1f(P, C, io):
    P.tag = "f"
    sc_main = ExitStack()
    P.default_sc = sc_main
    xn1T = io["xn1"]
    ypc = io["ypc"]
    idx_y = io["idx_y"]
    x1T = io["x1"]
    wprojT = io["wp1_ag"]
    wffnT = io["wf_ag"]
    bvec = io["bvec"]
    AT = io["AT"]
    BT = io["BT"]
    gwT = io["gwT"]
    gb_row = io["gb_row"]
    x2_out = io["x2"]

    pe, ac, ve, gp = (P.sem(s) for s in ("pe", "ac", "ve", "gp"))
    idxy = P.sbuf("idxy", [128, 16], mybir.dt.int32)

    xn16 = [P.sbuf(f"xn16_{i}", [128, 1024], BF16) for i in range(DT)]
    y16 = [P.sbuf(f"y16_{i}", [128, 1024], BF16) for i in range(DT)]
    sbr = [P.sbuf(f"sbr_{i}", [128, 1024], BF16) for i in range(DT)]
    nb = P.sbuf("nb", [128, DT], F32)
    BT16 = P.sbuf("BT16", [128, DM], BF16)
    xA16 = P.sbuf("xA16", [128, 1024], BF16)
    xAw = P.sbuf("xAw", [128, 1024], BF16)
    wqt = [P.sbuf(f"wqt{i}", [128, 512], BF16) for i in range(2)]

    sc_g = ExitStack()
    gw16 = [P.sbuf(f"gw16_{k}", [128, E], BF16, sc_g) for k in range(DT)]
    at16 = [P.sbuf(f"at16_{k}", [128, E * RLO], BF16, sc_g) for k in range(2)]
    lg = P.sbuf("lg", [128, 64], F32, sc_g)
    gb_b = P.sbuf("gb_b", [128, E], F32, sc_g)
    gbr = P.sbuf("gbr", [1, E], F32, sc_g)
    m1c = P.sbuf("m1c", [128, 8], F32, sc_g)
    m2c = P.sbuf("m2c", [128, 8], F32, sc_g)
    eq1 = P.sbuf("eq1", [128, E], F32, sc_g)
    eq2 = P.sbuf("eq2", [128, E], F32, sc_g)
    mskd = P.sbuf("mskd", [128, E], F32, sc_g)
    g1s = P.sbuf("g1s", [128, 8], F32, sc_g)
    g2s = P.sbuf("g2s", [128, 8], F32, sc_g)
    wd = P.sbuf("wd", [128, 64], F32, sc_g)
    wfm = P.sbuf("wfm", [8, 1024], F32, sc_g)
    ident32 = P.sbuf("ident32", [128, 128], F32, sc_g)
    expd = P.sbuf("expd", [8, 128], F32, sc_g)
    exp2 = P.sbuf("exp2", [8, 128], F32, sc_g)

    @P.block.gpsimd
    def _(g):
        _iv = P.srr.issue(g, idxy[:], idx_y)
        g.wait_ge(_iv[0].h, _iv[1])
        for i in range(DT):
            P.sxn.issue(g, xn16[i][:], xn1T[i * 128:(i + 1) * 128, :])
        for i in range(H):
            g.indirect_dma_start(
                y16[i][:], None, ypc[i % 4],
                IndirectOffsetOnAxis(idxy[:, i:i + 1], 0)
            ).then_inc(P.syl.sems[0].h, 16)
            P.syl.sems[0].inc(16)
        for i in range(DT):
            P.sml.issue(g, nb[:, i:i + 1], bvec[i * 128:(i + 1) * 128, 0:1])
        P.sml.issue(g, BT16[:], BT)
        P.sml.issue(g, gbr[:], gb_row)
        for k in range(DT):
            P.sml.issue(g, gw16[k][:], gwT[k * 128:(k + 1) * 128, :])
    v_xn = (P.sxn.sems[0], P.sxn.sems[0].n)
    v_y = (P.syl.sems[0], P.syl.sems[0].n)
    v_ml = (P.sml.sems[0], P.sml.sems[0].n)
    v_gl = v_ml

    def gfn_e(g, G):
        G.iota(expd[:], pattern=[[1, 128]], base=0, channel_multiplier=-16,
               allow_small_or_imprecise_dtypes=True)
    v_ei = P.gbody(gfn_e)

    def vfn_e(V):
        V.wait_ge(gp.h, v_ei)
        V.tensor_single_scalar(ident32[:], C["iota_pf"][:], 0.0, ALU.is_equal)
        V.tensor_single_scalar(exp2[:], expd[:], 0.0, ALU.is_ge)
        V.tensor_single_scalar(expd[:], expd[:], 16.0, ALU.is_lt)
        V.tensor_mul(expd[:], expd[:], exp2[:])
    v_em = P.vbody(vfn_e)

    @P.block.tensor
    def _(t):
        t.wait_ge(v_gl[0].h, v_gl[1])
        t.wait_ge(v_xn[0].h, v_xn[1])
        mm = None
        for kt in range(DT):
            for tt in range(8):
                mm = t.matmul(P.PSF[:, tt * 512:tt * 512 + E],
                              xn16[kt][:, tt * 128:(tt + 1) * 128], gw16[kt][:],
                              start=(kt == 0), stop=(kt == DT - 1))
        mm.then_inc(pe.h, 1)
    v_lg = pe.inc(1)

    def vfn_lgc(V):
        V.wait_ge(pe.h, v_lg)
        for tt in range(8):
            V.tensor_copy(lg[:, tt * 8:(tt + 1) * 8], P.PSF[:, tt * 512:tt * 512 + E])
    v_lgc = P.vbody(vfn_lgc)

    @P.block.tensor
    def _(t):
        t.wait_ge(ve.h, v_lgc)
        t.matmul(P.bankc(7, E), C["ones1r32"][0:1, :], gbr[0:1, 0:E],
                 start=True, stop=True).then_inc(pe.h, 1)
    v_gbb = pe.inc(1)

    def vfn_lg(V):
        V.wait_ge(pe.h, v_gbb)
        V.tensor_copy(gb_b[:], P.bankc(7, E))
        for tt in range(8):
            sl = slice(tt * 8, (tt + 1) * 8)
            V.tensor_add(lg[:, sl], lg[:, sl], gb_b[:])
            V.tensor_reduce(m1c[:, tt:tt + 1], lg[:, sl], AX, ALU.max)
            V.tensor_single_scalar(eq1[:], lg[:, sl], m1c[:, tt:tt + 1], ALU.is_equal)
            V.scalar_tensor_tensor(mskd[:], eq1[:], -1e30, lg[:, sl],
                                   ALU.mult, ALU.add)
            V.tensor_reduce(m2c[:, tt:tt + 1], mskd[:], AX, ALU.max)
            V.tensor_single_scalar(eq2[:], mskd[:], m2c[:, tt:tt + 1], ALU.is_equal)
            V.tensor_copy(wd[:, sl], eq1[:])
            V.tensor_copy(mskd[:], eq2[:])
            V.tensor_copy(eq2[:], mskd[:])
            V.tensor_copy(lg[:, sl], eq2[:])
        V.tensor_sub(m1c[:], m1c[:], m2c[:])
    v_top = P.vbody(vfn_lg)

    def sfn_sig(S_):
        S_.wait_ge(ve.h, v_top)
        S_.activation(g1s[:], m1c[:], AF.Sigmoid)
    v_sig = P.sbody(sfn_sig)

    def vfn_wd(V):
        V.wait_ge(ac.h, v_sig)
        V.tensor_scalar(g2s[:], g1s[:], -1.0, 1.0, ALU.mult, ALU.add)
        for tt in range(8):
            sl = slice(tt * 8, (tt + 1) * 8)
            V.tensor_single_scalar(wd[:, sl], wd[:, sl], g1s[:, tt:tt + 1], ALU.mult)
            V.tensor_single_scalar(lg[:, sl], lg[:, sl], g2s[:, tt:tt + 1], ALU.mult)
            V.tensor_add(wd[:, sl], wd[:, sl], lg[:, sl])
    v_wd = P.vbody(vfn_wd)

    tr_pe = []
    for tt in range(8):
        @P.block.tensor
        def _(t):
            if tt == 0:
                t.wait_ge(ve.h, v_wd)
                t.wait_ge(ve.h, v_em)
            t.matmul(P.PSF[0:8, 5 * 512 + tt * 128:5 * 512 + (tt + 1) * 128],
                     wd[:, tt * 8:(tt + 1) * 8], ident32[:],
                     start=True, stop=True).then_inc(pe.h, 1)
        tr_pe.append(pe.inc(1))

    def vfn_wf(V):
        V.wait_ge(pe.h, tr_pe[-1])
        for tt in range(8):
            V.tensor_copy(wfm[:, tt * 128:(tt + 1) * 128],
                          P.PSF[0:8, 5 * 512 + tt * 128:5 * 512 + (tt + 1) * 128])
    v_wf = P.vbody(vfn_wf)

    xal = []
    xa_pe = []
    for kt in range(DT):
        @P.block.gpsimd
        def _(g):
            if kt == 0:
                g.wait_ge(v_xn[0].h, v_xn[1])
            if kt >= 2:
                g.wait_ge(pe.h, xa_pe[kt - 2])
            xal.append(P.sxl.issue(g, at16[kt % 2][:],
                                   AT[kt * 128:(kt + 1) * 128, :], kt))

        @P.block.tensor
        def _(t):
            t.wait_ge(xal[kt][0].h, xal[kt][1])
            if kt == 0:
                t.wait_ge(ve.h, v_lgc)
            mm = None
            for off in (0, 512):
                mm = t.matmul(P.PSF[:, off:off + 512],
                              at16[kt % 2][:], xn16[kt][:, off:off + 512],
                              start=(kt == 0), stop=(kt == DT - 1))
            mm.then_inc(pe.h, 1)
        xa_pe.append(pe.inc(1))

    def vfn_xa(V):
        V.wait_ge(pe.h, xa_pe[-1])
        V.tensor_copy(xA16[:], P.PSF[:, 0:1024])
    v_xa = P.vbody(vfn_xa)

    @P.block.tensor
    def _(t):
        t.wait_ge(ve.h, v_wf)
        t.wait_ge(ve.h, v_xa)
        mm = None
        for off in (0, 512):
            mm = t.matmul(P.PSF[:, 1024 + off:1024 + off + 512], expd[0:8, :],
                          wfm[0:8, off:off + 512], start=True, stop=True)
        mm.then_inc(pe.h, 1)
    v_we = pe.inc(1)

    def vfn_xaw(V):
        V.wait_ge(pe.h, v_we)
        V.tensor_mul(xAw[:], xA16[:], P.PSF[:, 1024:2048])
    v_xaw = P.vbody(vfn_xaw)
    sc_g.close()

    li = [0]
    ldv = {}
    mm_pe = {}
    evac = [0]
    for og in range(4):
        for kt in range(DT):
            i = li[0]
            bi = i % 2

            @P.block.gpsimd
            def _(g):
                if i == 0:
                    g.wait_ge(v_y[0].h, v_y[1])
                    g.wait_ge(ve.h, v_xaw)
                if i >= 2:
                    g.wait_ge(pe.h, mm_pe[i - 2])
                ldv[i] = P.wlg.issue(g, wqt[bi][:],
                                     wprojT[kt * 128:(kt + 1) * 128,
                                            og * 512:(og + 1) * 512], i)

            @P.block.tensor
            def _(t):
                t.wait_ge(ldv[i][0].h, ldv[i][1])
                if kt == 0 and evac[0]:
                    t.wait_ge(ve.h, evac[0])
                if kt == 0 and og == 0:
                    t.wait_ge(ve.h, v_xaw)
                mm = None
                for k in range(4):
                    for off in (0, 512):
                        mm = t.matmul(P.PSF[:, k * 1024 + off:k * 1024 + off + 512],
                                      wqt[bi][:, k * 128:(k + 1) * 128],
                                      y16[kt][:, off:off + 512],
                                      start=(kt == 0), stop=(kt == DT - 1))
                mm.then_inc(pe.h, 1)
            mm_pe[i] = pe.inc(1)
            li[0] += 1
        v_g = pe.n

        def vfn_pev(V, og=og, v_g=v_g):
            V.wait_ge(pe.h, v_g)
            for k in range(4):
                V.tensor_copy(sbr[og * 4 + k][:], P.PSF[:, k * 1024:(k + 1) * 1024])
        evac[0] = P.vbody(vfn_pev)
    v_sbr = evac[0]

    sc_cb = ExitStack()
    xb = [P.sbuf(f"xb{i}", [128, 1024], F32, sc_cb) for i in range(2)]
    gt = [P.sbuf(f"gt{i}", [128, 1024], BF16, sc_cb) for i in range(2)]
    tmpf = P.sbuf("tmpf", [128, 1024], F32, sc_cb)
    x2st = []
    cb_ve = []
    ffn_pe = []
    li2 = [0]
    for og in range(4):
        for kt in range(DT):
            i = li2[0]
            bi = i % 2

            @P.block.gpsimd
            def _(g):
                if i == 0:
                    g.wait_ge(ve.h, v_sbr)
                if i >= 2:
                    g.wait_ge(pe.h, mm_pe[(64, i - 2)])
                ldv[(64, i)] = P.wlg.issue(g, wqt[bi][:],
                                           wffnT[kt * 128:(kt + 1) * 128,
                                                 og * 512:(og + 1) * 512], i)

            @P.block.tensor
            def _(t):
                t.wait_ge(ldv[(64, i)][0].h, ldv[(64, i)][1])
                if kt == 0 and og > 0:
                    t.wait_ge(ve.h, cb_ve[-1])
                mm = None
                for k in range(4):
                    for off in (0, 512):
                        mm = t.matmul(P.PSF[:, k * 1024 + off:k * 1024 + off + 512],
                                      wqt[bi][:, k * 128:(k + 1) * 128],
                                      xn16[kt][:, off:off + 512],
                                      start=(kt == 0), stop=False)
                mm.then_inc(pe.h, 1)
            mm_pe[(64, i)] = pe.inc(1)
            li2[0] += 1

        @P.block.tensor
        def _(t):
            mm = None
            for k in range(4):
                ot = og * 4 + k
                for off in (0, 512):
                    mm = t.matmul(P.PSF[:, k * 1024 + off:k * 1024 + off + 512],
                                  BT16[:, ot * 128:(ot + 1) * 128],
                                  xAw[:, off:off + 512],
                                  start=False, stop=True)
            mm.then_inc(pe.h, 1)
        v_lora = pe.inc(1)

        for k in range(4):
            ot = og * 4 + k
            bi = ot % 2

            def sfn_g(S_, ot=ot, k=k, bi=bi, v_lora=v_lora):
                S_.wait_ge(pe.h, v_lora)
                if ot >= 2:
                    S_.wait_ge(ve.h, cb_ve[ot - 2])
                S_.wait_ge(v_ml[0].h, v_ml[1])
                S_.activation(gt[bi][:], P.PSF[:, k * 1024:(k + 1) * 1024], AF.Silu,
                              bias=nb[:, ot:ot + 1], scale=1.0)
            v_gt = P.sbody(sfn_g)

            xldi = []

            @P.block.gpsimd
            def _(g):
                if ot >= 2:
                    g.wait_ge(ve.h, cb_ve[ot - 2])
                    g.wait_ge(x2st[ot - 2][0].h, x2st[ot - 2][1])
                xldi.append(P.sxl.issue(g, xb[bi][:],
                                        x1T[ot * 128:(ot + 1) * 128, :], ot))

            def vfn_c(V, ot=ot, bi=bi, v_gt=v_gt, xldi=xldi):
                V.wait_ge(ac.h, v_gt)
                V.wait_ge(xldi[0][0].h, xldi[0][1])
                V.tensor_mul(tmpf[:], gt[bi][:], sbr[ot][:])
                V.tensor_add(xb[bi][:], xb[bi][:], tmpf[:])
            cb_ve.append(P.vbody(vfn_c))

            @P.block.gpsimd
            def _(g):
                g.wait_ge(ve.h, cb_ve[ot])
                x2st.append(P.sxs.issue(g, x2_out[ot * 128:(ot + 1) * 128, :],
                                        xb[bi][:], ot))
    sc_cb.close()
    sc_main.close()
    P.default_sc = None


def build_l1f():
    P = LB("l1f").start()
    io = {
        "xn1": P.din("xn1T", [DM, TOK], BF16).ap(),
        "ypc": [P.din(f"ypc{b}", [4 * 512, 1024], BF16).ap() for b in range(4)],
        "idx_y": P.din("idx_y", [128, H], mybir.dt.int32).ap(),
        "x1": P.din("x1T", [DM, TOK], F32).ap(),
        "wp1_ag": P.din("wprojT", [DM, DM], BF16).ap(),
        "wf_ag": P.din("wffnT", [DM, DM], BF16).ap(),
        "bvec": P.din("bvec", [DM, 1], F32).ap(),
        "AT": P.din("AT", [DM, E * RLO], F32).ap(),
        "BT": P.din("BT", [E * RLO, DM], F32).ap(),
        "gwT": P.din("gwT", [DM, E], F32).ap(),
        "gb_row": P.din("gb_row", [1, E], F32).ap(),
        "x2": P.dout("x2_out", [DM, TOK], F32).ap(),
    }
    C = emit_consts(P)
    emit_l1f(P, C, io)

    @P.block.gpsimd
    def _(g):
        P.wait_all_streams(g)

    @P.block.sync
    def _(sy_):
        P.wait_all_streams(sy_)

    return P.finish()



def build_fused():
    P = LB("fused").start()
    io = {
        "xT": P.din("xT", [DM, TOK], F32).ap(),
        "qkv_sl": P.din("qkv_sl", [256, 6144], F32).ap(),
        "prj_sl": P.din("prj_sl", [256, 2048], F32).ap(),
        "w1_sl": P.din("w1_sl", [256, HID], F32).ap(),
        "w3_sl": P.din("w3_sl", [256, HID], F32).ap(),
        "w2_sl": P.din("w2_sl", [683, 2048], F32).ap(),
        "qkvnrm": P.din("qkvnrm", [DM, 1], F32).ap(),
        "s_col": P.din("s_col", [H, 1], F32).ap(),
        "s_row": P.din("s_row", [1, H], F32).ap(),
        "normw_in": P.din("normw_in", [DM, 1], F32).ap(),
        "pnormw_in": P.din("pnormw_in", [DM, 1], F32).ap(),
        "watt_sl": P.din("watt_sl", [256, 3 * DM], F32).ap(),
        "pos_row": P.din("pos_row", [1, TOK], F32).ap(),
        "wfac_row": P.din("wfac_row", [1, 3], F32).ap(),
        "gap_row": P.din("gap_row", [1, 3], F32).ap(),
        "wp1_sl": P.din("wp1_sl", [256, DM], F32).ap(),
        "wf_sl": P.din("wf_sl", [256, DM], F32).ap(),
        "bvec": P.din("bvec", [DM, 1], F32).ap(),
        "AT": P.din("AT", [DM, E * RLO], F32).ap(),
        "BT": P.din("BT", [E * RLO, DM], F32).ap(),
        "gwT": P.din("gwT", [DM, E], F32).ap(),
        "gb_row": P.din("gb_row", [1, E], F32).ap(),
        "idx_l1a": P.din("idx_l1a", [128, 48], mybir.dt.int32).ap(),
        "idx_y": P.din("idx_y", [128, H], mybir.dt.int32).ap(),
        "x2": P.dout("x2_out", [DM, TOK], F32).ap(),
    }
    P.tag = "z"
    for nm, R, Cc, _, _c in WPREP:
        io[nm + "_pub"] = P.dram(nm + "_pub", [R, Cc], BF16).ap()
        io[nm + "_ag"] = P.dram(nm + "_ag", [8 * R, Cc], BF16).ap()
    for nm, R, Cc in WCAST:
        io[nm + "_pub"] = P.dram(nm + "_pub", [R, Cc], BF16).ap()
        io[nm + "_ag"] = P.dram(nm + "_ag", [8 * R, Cc], BF16).ap()
    io["mrow"] = P.dram("mrow", [1, 8], F32).ap()
    io["xn0"] = P.dram("xn0", [DM, TOK], BF16).ap()
    io["oloc"] = P.dram("oloc", [CPC, 128, H * 128], BF16).ap()
    io["qdec"] = P.dram("qdec", [CPC, 128, H * 128], BF16).ap()
    io["kvagg"] = P.dram("kvagg", [128, H * 128], F32).ap()
    kvall_h = P.dram("kvall", [4, 128, H * 128], F32)
    io["kvall"] = kvall_h.ap()
    io["x1"] = P.dram("x1", [DM, TOK], F32).ap()
    io["xn1"] = P.dram("xn1", [DM, TOK], BF16).ap()
    io["qkvpub"] = P.dram("qkvpub", [3 * DM, TOK], BF16).ap()
    io["qkvall"] = P.dram("qkvall", [4 * 3 * DM, TOK], BF16).ap()
    io["ypub"] = P.dram("ypub", [2048, 1024], BF16).ap()
    io["ypc"] = [P.dram(f"ypc{b}", [4 * 512, 1024], BF16).ap() for b in range(4)]

    cc = P.sem("cc")
    GROUPS = [[0, 1, 2, 3], [4, 5, 6, 7]]

    C = emit_consts(P)
    wstate = {}

    def walloc_cb():
        wstate["w"] = alloc_wprep(P, P.default_sc)

    def wprep_cb():
        w = wstate["w"]
        tag0 = P.tag
        P.tag = "w"
        emit_wprep_cast(P, C, io, w, ["watt", "wp1", "wf"], stage=0)
        emit_wprep_means(P, C, io, w)
        g1 = emit_wprep_quant(P, C, io, w, ["qkv"])
        io["cc_qkv"] = g1["qkv"]
        emit_wprep_quant(P, C, io, w, ["prj", "w1", "w3", "w2"])
        emit_wprep_cast(P, C, io, w, ["watt", "wp1", "wf"], stage=1)
        P.tag = tag0
        return []

    emit_l0a(P, C, io, walloc_cb=walloc_cb, wprep_cb=wprep_cb)

    @P.block.gpsimd
    def _(g):
        g.wait_ge(P.mvec.sems[0].h, P.mvec.sems[0].n)
        g.collective_compute(
            "AllGather", ALU.bypass, replica_groups=GROUPS,
            ins=[io["kvagg"]],
            outs=[io["kvall"].rearrange("a b c -> (a b) c")]).then_inc(cc.h, 1)
    cc.inc(1)
    P.phase_barrier()

    emit_l0b(P, C, io)
    P.phase_barrier()

    io["cc_groups"] = GROUPS
    emit_l1q(P, C, io)
    P.phase_barrier()

    emit_l1a(P, C, io)
    P.phase_barrier()

    emit_l1f(P, C, io)

    @P.block.gpsimd
    def _(g):
        P.wait_all_streams(g)

    @P.block.sync
    def _(sy_):
        P.wait_all_streams(sy_)

    @P.block.vector
    def _(v):
        P.wait_all_streams(v)

    return P.finish()


def host_inputs(inp):
    x = _f32c(inp["L0_attn_x"])
    s_np = _f32c(inp["L0_attn_s"]).reshape(H, 1)
    lA = _f32c(inp["L1_ffn_lora_A"])
    lB = _f32c(inp["L1_ffn_lora_B"])
    wqkvT = np.ascontiguousarray(_f32c(inp["L0_attn_qkv_w"]).T)
    wprojT = np.ascontiguousarray(_f32c(inp["L0_attn_proj_w"]).T)
    w1T = np.ascontiguousarray(_f32c(inp["L0_ffn_w1"]).T)
    w3T = np.ascontiguousarray(_f32c(inp["L0_ffn_w3"]).T)
    w2T = np.ascontiguousarray(_f32c(inp["L0_ffn_w2"]).T)
    w2Tp = np.zeros((8 * 683, DM), np.float32)
    w2Tp[:HID] = w2T
    wattT = np.ascontiguousarray(_f32c(inp["L1_attn_w_atten"]).T)
    wp1T = np.ascontiguousarray(_f32c(inp["L1_attn_w_proj"]).T)
    wfT = np.ascontiguousarray(_f32c(inp["L1_ffn_W"]).T)
    shared = {
        "qkvnrm": _f32c(inp["L0_attn_qkv_norm_w"]).reshape(DM, 1),
        "s_col": s_np,
        "s_row": np.ascontiguousarray(s_np.reshape(1, H)),
        "normw_in": _f32c(inp["L0_attn_norm_w"]).reshape(DM, 1),
        "pnormw_in": _f32c(inp["L0_attn_proj_norm_w"]).reshape(DM, 1),

        "bvec": _f32c(inp["L1_ffn_b"]).reshape(DM, 1),
        "AT": np.ascontiguousarray(lA.reshape(E * RLO, DM).T),
        "BT": np.ascontiguousarray(np.transpose(lB, (0, 2, 1)).reshape(E * RLO, DM)),
        "gwT": np.ascontiguousarray(_f32c(inp["L1_ffn_gate_w"]).T),
        "gb_row": _f32c(inp["L1_ffn_gate_b"]).reshape(1, E),
    }
    maps = []
    pp = np.arange(128, dtype=np.int64)
    for c in range(NCORES):
        b, j = c // 4, c % 4
        m = dict(shared)
        m["xT"] = np.ascontiguousarray(x[b, j * TOK:(j + 1) * TOK, :].T)
        m["qkv_sl"] = np.ascontiguousarray(wqkvT[c * 256:(c + 1) * 256])
        m["prj_sl"] = np.ascontiguousarray(wprojT[c * 256:(c + 1) * 256])
        m["w1_sl"] = np.ascontiguousarray(w1T[c * 256:(c + 1) * 256])
        m["w3_sl"] = np.ascontiguousarray(w3T[c * 256:(c + 1) * 256])
        m["w2_sl"] = np.ascontiguousarray(w2Tp[c * 683:(c + 1) * 683])
        m["watt_sl"] = np.ascontiguousarray(wattT[c * 256:(c + 1) * 256])
        m["wp1_sl"] = np.ascontiguousarray(wp1T[c * 256:(c + 1) * 256])
        m["wf_sl"] = np.ascontiguousarray(wfT[c * 256:(c + 1) * 256])
        m["pos_row"] = np.arange(j * TOK, (j + 1) * TOK,
                                 dtype=np.float32).reshape(1, TOK)
        wf = np.zeros((1, 3), np.float32)
        gp_ = np.zeros((1, 3), np.float32)
        for p in range(j):
            wf[0, p] = 1.0
            gp_[0, p] = 8.0 * (j - p - 1)
        m["wfac_row"] = wf
        m["gap_row"] = gp_
        idx_a = np.zeros((128, 48), np.int32)
        for bh in range(4):
            hh = 4 * j + bh
            g_ = hh // 4
            for role in range(3):
                for j2 in range(4):
                    if role == 2:
                        base = j2 * 2048 + hh * 128
                    else:
                        base = (8192 if role == 0 else 16384) + \
                            g_ * 2048 + j2 * 512 + (hh % 4) * 128
                    idx_a[:, bh * 12 + role * 4 + j2] = base + pp
        m["idx_l1a"] = idx_a
        idx_y = np.zeros((128, H), np.int32)
        for hh in range(H):
            base = (hh // 4) * 512 + j * 128
            idx_y[:, hh] = base + pp
        m["idx_y"] = idx_y
        maps.append(m)
    return maps


def _f32c(a):
    return np.ascontiguousarray(np.asarray(a, dtype=np.float32))


_BUILD_CACHE = {}


def _get(name, fn):
    if name not in _BUILD_CACHE:
        _BUILD_CACHE[name] = fn()
    return _BUILD_CACHE[name]


def kernel(**inp):
    maps = host_inputs(inp)
    nc = _get("fused", build_fused)
    res = run_bass_kernel_spmd(nc, maps, core_ids=list(range(NCORES))).results
    out = np.zeros((B, N, DM), np.float32)
    for c in range(NCORES):
        b, j = c // 4, c % 4
        out[b, j * TOK:(j + 1) * TOK, :] = np.asarray(
            res[c]["x2_out"], np.float32).reshape(DM, TOK).T
    return out
